# version 10
# baseline (speedup 1.0000x reference)
import numpy as np
import concourse.bass as bass
import concourse.mybir as mybir

F32 = mybir.dt.float32
BF16 = mybir.dt.bfloat16
AF = mybir.ActivationFunctionType
ALU = mybir.AluOpType
AX = mybir.AxisListType

L, D, DI, NH, HD, NST = 4096, 2048, 4096, 64, 64, 64
TT, S = 512, 128
NTILE, NCH = L // TT, TT // S
DIP = 8384
NK = D // 128
NKI = DI // 128
ENGS = ("sync", "vector", "scalar", "tensor", "gpsimd")


def sl(off, n):
    return slice(off, off + n)


class Sem:
    __slots__ = ("h", "val")

    def __init__(self, nc):
        self.h = nc.alloc_semaphore()
        self.val = 0


class KB:
    def __init__(self, debug=()):
        nc = self.nc = bass.Bass("TRN2", target_bir_lowering=False)
        self.streams = {e: [] for e in ENGS}
        self.debug = dict(debug)
        self.dbg_t = {}
        self.dbg_done = set()

        din = lambda n, sh: nc.dram_tensor(n, sh, F32, kind="ExternalInput").ap()
        self.u = din("u", [L, D])
        self.w_in = din("w_in", [D, DIP])
        self.convw = din("convw", [128, 128])
        self.convb = din("convb", [128, 32])
        self.A_log = din("A_log", [NH, 1])
        self.dt_bias = din("dt_bias", [NH, 1])
        self.D_param = din("D_param", [1, NH])
        self.norm_scale = din("norm_scale", [128, 32])
        self.w_out = din("w_out", [DI, D])
        self.w_base = din("w_base", [D, D])
        self.b_base = din("b_base", [1, D])
        self.w_small = din("w_small", [D, 118])
        self.b_down = din("b_down", [112, D])
        self.brow = din("brow", [1, 118])
        self.eye128 = din("eye128", [128, 128])
        self.mask01 = din("mask01", [128, 128])
        self.out = nc.dram_tensor("out", [L, D], F32, kind="ExternalOutput").ap()
        for name, (shape, dt_) in self.debug.items():
            self.dbg_t[name] = nc.dram_tensor("dbg_" + name, shape, dt_, kind="ExternalOutput").ap()

        dint = lambda n, sh, dt_: nc.dram_tensor(n, sh, dt_, kind="Internal").ap()
        self.wsc_inp = dint("wsc_inp", [17, 128, 4096], BF16)
        self.wsc_zp = dint("wsc_zp", [8, 128, 8192], BF16)
        self.wsc_outp = dint("wsc_outp", [4, 128, NKI * 512], BF16)
        self.wsc_basep = dint("wsc_basep", [4, 128, NK * 512], BF16)
        self.wsc_diag = dint("wsc_diag", [32, 128, 512], BF16)
        self.zsc = dint("zsc", [NCH, 128, DI], BF16)
        self.csflat = dint("csflat", [2, NH, S], F32)
        self.rowsc = dint("rowsc", [1, D], BF16)

        DTSZ = {F32: 4, BF16: 2}
        ARENA = 212832
        self.arena = nc.alloc_sbuf_tensor("arena", [128, ARENA], mybir.dt.uint8)
        self._abase = nc.lookup_mloc(self.arena).addr
        self._cursor = 0
        self._mstart = {}
        self._table = []

        def A(n, sh, dt_, overlay=None):
            nbytes = sh[-1] * DTSZ[dt_]
            for d in sh[1:-1]:
                nbytes *= d
            nbytes = (nbytes + 31) // 32 * 32
            if overlay is not None:
                start = overlay
            else:
                start = self._cursor
                self._cursor += nbytes
            assert start + nbytes <= ARENA, f"SBUF arena overflow at {n}: {start + nbytes}"
            self._table.append((n, start, nbytes, overlay is not None))
            h = nc.virtual_sbuf_tensor(n, sh, dt_)
            nc._tensor(h.name, h.shape, h.dtype, type="SB")
            mloc = nc.lookup_mloc(h)
            mloc.allocated = True
            mloc.addr = self._abase + start
            ap = h.ap()
            self._mstart[n] = start
            return ap

        self.eye128_b = A("eye128_b", [128, 128], BF16)
        self.eye64_f = A("eye64_f", [64, 64], F32)
        self.mask01_b = A("mask01_b", [128, 128], BF16)
        self.ns_cols = A("ns_cols", [128, NKI], F32)
        self.cb_cols = A("cb_cols", [128, 32], F32)
        self.negdtb = A("negdtb", [NH, 1], F32)
        self.expA = A("expA", [NH, 1], F32)
        self.drep = A("drep", [128, NH], BF16)
        self.wsmall_sb = A("wsmall_sb", [128, NK * 118], BF16)
        self.bdown_sb = A("bdown_sb", [113, D], BF16)
        self.brow_sb = A("brow_sb", [1, 118], BF16)
        self.ones_row = A("ones_row", [1, 128], BF16)
        self.cwT = A("cwT", [128, 128], F32)

        self.vt = A("vt", [128, NCH * DI], BF16)
        self.xslot = A("xslot", [128, NCH * D], F32, overlay=self._mstart["vt"])
        self.slotA = A("slotA", [128, 8192], BF16)
        self.slotB = A("slotB", [128, 8192], BF16)
        self.zbuf = None
        self.xrr = A("xrr", [128, 6 * 515], BF16)
        self.xcr = A("xcr", [128, 4 * 512], BF16)
        self.zbuf = A("zbuf", [128, DI], BF16, overlay=self._mstart["xrr"])
        self.darg = A("darg", [128, 2048], BF16)
        self.datt = A("datt", [128, 2 * 1024], BF16)
        self.csb = A("csb", [128, 1024], F32)
        self.ygt = A("ygt", [128, DI], BF16)
        self.ybf = A("ybf", [128, 2 * DI], BF16)
        self.zcF = A("zcF", [128, DI], F32, overlay=self._mstart["ybf"])
        self.prep_b = self.ygt[:, 0:2048]
        self.ntok = self.ygt[:, 2048:4096]
        self.rv = A("rv", [128, 512], BF16)
        self.vd = A("vd", [128, 512], BF16)
        self.y_tmp = A("y_tmp", [128, 512], F32)
        self.scrF = A("scrF", [128, DI], F32)
        self.y_sb = self.scrF[:, 0:DI]
        self.lns = self.scrF[:, 0:D]
        self.x2_sb = self.scrF[:, D:2 * D]
        self.prep_f = self.scrF[:, 0:DI]
        self.h_sb = A("h_sb", [NST, DI], F32)
        self.h_bf = A("h_bf", [NST, DI], BF16)

        self.wbuf = A("wbuf", [128, 2 * NK * 256], BF16)
        self.zwbuf = A("zwbuf", [128, 2 * 2 * 512], BF16)
        self.owbuf = A("owbuf", [128, 2 * 2 * 512], BF16)
        self.bwbuf = A("bwbuf", [128, 2 * 2 * 512], BF16)
        self.dgbuf = A("dgbuf", [128, 2 * 4 * 128], BF16)

        self.smt = A("smt", [128, 2 * S], BF16)
        self.bm_b = A("bm_b", [64, TT], BF16)
        self.cm_b = A("cm_b", [64, TT], BF16)
        self.rh = A("rh", [NH, S], BF16)
        self.rt = A("rt", [128, 2 * NH], BF16)
        self.kc_sb = A("kc_sb", [128, 2 * NST], BF16)
        self.hist = A("hist", [128, 32 * 3], BF16)
        self.stackt = A("stackt", [128, 113], BF16)
        self.stackT = A("stackT", [113, TT], BF16)
        self.ffn_small = A("ffn_small", [128, 118], BF16)
        self.csh = A("csh", [NH, 2 * S], F32)
        self.cst = A("cst", [128, 2 * NH], F32)
        self.et = A("et", [128, 2 * NH], F32)
        self.la = A("la", [NH, TT], F32)
        self.dtile = A("dtile", [NST, 2 * NH], F32)
        self.dcol = A("dcol", [NH, 2], F32)
        self.gates = A("gates", [128, 6], F32)
        self.gm = A("gm", [128, 6], F32)
        self.wpe = A("wpe", [128, 6], F32)
        cols = lambda n: A(n, [128, 1], F32)
        self.sum_col, self.ssq_col, self.ms_col = cols("sum_col"), cols("ssq_col"), cols("ms_col")
        self.rstd_col, self.nmr_col = cols("rstd_col"), cols("nmr_col")
        self.ss_col, self.rstd2 = cols("ss_col"), cols("rstd2")
        self.ssc = A("ssc", [128, 2], F32)
        m4 = A("m4cols", [128, 4], F32)
        self.m1c, self.m2c, self.sumg, self.rsg = m4[:, 0:1], m4[:, 1:2], m4[:, 2:3], m4[:, 3:4]

        Pp = lambda n, sh, dt_: nc.alloc_psum_tensor(n, sh, dt_).ap()
        self.psA = Pp("psA", [128, 2048], F32)
        self.psYa = Pp("psYa", [128, 512], F32)
        self.psYb = Pp("psYb", [128, 512], F32)
        self.psS = Pp("psS", [128, 512], F32)
        self.tr_ps = Pp("tr_ps", [128, 1024], BF16)
        self.ms_ps = self.psS
        self.psB = self.psS

        self.S_ = {n: Sem(nc) for n in ("sW", "sG", "sV", "sA", "sT")}
        self.DMA_SEMS = {"sW", "sG"}
        self.gfree = {}
        self._zbuf_free = 0
        self._csb_free = 0
        self._csflat_free = {0: 0, 1: 0}
        self._zsc_done = 0
        self._prep2_v = 0
        self._prep2_o = 0
        self._psS_free = 0
        self._tr_free = ("sV", 0)
        self._dslot_free = {}
        self._datt_ready = {}
        self._rv_free = 0
        self._ya_free = 0
        self._yb_free = 0
        self._ysb_free = 0
        self._ybf_free = {}
        self._ybf_ready = {}
        self._ygt_free = 0
        self._u1 = 0
        self._last_c2 = 0
        self._flv = 0
        self._pre = {0: {}, 1: {}}
        self._conv_evac = {}
        self._zjobs = []
        self._zji = 0
        self._zlastcast = {}
        self._zcast_v = 0

    def normed(self):
        return self.slotA

    def n2f_slice(self, k, tsub):
        return self.slotA[:, sl(k * 512 + tsub * 128, 128)]

    def ygf_slice(self, k, tsub):
        base = k * 512 + tsub * 128
        return (self.slotA if k < 16 else self.slotB)[:, sl(base - (8192 if k >= 16 else 0), 128)]

    def emit(self, eng, fn):
        self.streams[eng].append(fn)

    def sem(self, name):
        if name not in self.S_:
            self.S_[name] = Sem(self.nc)
        return self.S_[name]

    def wait(self, eng, semname, val):
        s = self.sem(semname)
        self.emit(eng, lambda E, h=s.h, v=val: E.wait_ge(h, v))

    def run(self, eng, fn, inc=None, n=1):
        if inc is not None:
            if n == 16:
                self.DMA_SEMS.add(inc)
            s = self.sem(inc)
            s.val += n
            self.emit(eng, lambda E, fn=fn, h=s.h, n=n: fn(E).then_inc(h, n))
            return s.val
        self.emit(eng, lambda E, fn=fn: fn(E))
        return None

    def dma(self, eng, out, in_, slow=False, sem=None):
        inc = sem or {"sync": "sW", "gpsimd": "sG"}[eng]
        self.DMA_SEMS.add(inc)
        return self.run(eng, lambda E, o=out, i=in_, s=slow: E.dma_start(
            out=o, in_=i, allow_slow_non_contiguous=s), inc, 16)

    def v(self, name):
        return self.sem(name).val

    def selfsync(self, eng):
        em = {"vector": "sV", "scalar": "sA", "tensor": "sT"}[eng]
        self.wait(eng, em, self.S_[em].val)

    def barrier(self, engines=("vector", "scalar", "tensor"), dma=(), full=False):
        em = {"vector": "sV", "scalar": "sA", "tensor": "sT"}
        if full:
            for e in engines:
                self.run(e, lambda E: E.drain(), em[e], 1)
        dma = tuple(self.DMA_SEMS) if full else tuple(dma)
        vals = {n: self.v(n) for n in list(em.values()) + list(dma)}
        targets = [em[e] for e in engines] + list(dma)
        waiters = engines + (("sync", "gpsimd") if full else ())
        for e in waiters:
            for sn in targets:
                self.wait(e, sn, vals[sn])

    def tap(self, name, src_ap):
        if name in self.dbg_t and name not in self.dbg_done:
            self.dbg_done.add(name)
            self.barrier(full=True)
            self.dma("gpsimd", self.dbg_t[name], src_ap, sem="sSO")
            self.barrier(full=True)

    def finalize(self):
        for sn in sorted(self.DMA_SEMS):
            self.wait("gpsimd", sn, self.v(sn))
        nc = self.nc
        with nc.Block() as block:
            for eng in ENGS:
                stream = self.streams[eng]
                dec = getattr(block, eng)

                def make(stream):
                    def body(E):
                        for fn in stream:
                            fn(E)
                    return body
                dec(make(stream))
        return nc



def emit_prep(kb: "KB"):
    p = kb
    st = p.scrF
    p.dma("gpsimd", st[:, 0:128], p.eye128)
    p.dma("gpsimd", p.eye64_f, p.eye128[0:64, 0:64])
    p.dma("gpsimd", st[:, 128:256], p.mask01)
    p.dma("gpsimd", p.ns_cols, p.norm_scale)
    p.dma("gpsimd", p.cb_cols, p.convb)
    p.dma("gpsimd", st[0:64, 2208:2209], p.dt_bias)
    p.dma("gpsimd", st[0:64, 2209:2210], p.A_log)
    p.dma("gpsimd", st[:, 256:320],
          p.D_param.rearrange("a (o h) -> a o h", o=1).broadcast_to([1, 128, NH]))
    p.dma("gpsimd", st[:, 320:320 + 16 * 118].rearrange("q (k c) -> q k c", k=16),
          p.w_small.rearrange("(k q) c -> q k c", q=128))
    p.dma("gpsimd", p.cwT, p.convw)
    gv = p.v("sG")
    p.wait("vector", "sG", gv)
    p.run("vector", lambda E: E.tensor_copy(p.eye128_b[:], st[:, 0:128]), "sV")
    p.run("vector", lambda E: E.tensor_copy(p.mask01_b[:], st[:, 128:256]), "sV")
    p.run("vector", lambda E: E.tensor_copy(p.drep[:], st[:, 256:320]), "sV")
    p.run("vector", lambda E: E.tensor_copy(p.wsmall_sb[:], st[:, 320:320 + 16 * 118]), "sV")
    p.run("vector", lambda E: E.tensor_scalar(out=p.negdtb[:], in0=st[0:64, 2208:2209],
                                              scalar1=-1.0, scalar2=None, op0=ALU.mult), "sV")
    p.wait("scalar", "sG", gv)
    expav = p.run("scalar", lambda E: E.activation(out=p.expA[:], in_=st[0:64, 2209:2210],
                                           func=AF.Exp, bias=0.0, scale=1.0), "sA")
    p.run("vector", lambda E: E.memset(p.ones_row[:], 1.0), "sV")
    p.run("vector", lambda E: E.memset(p.stackt[:, 112:113], 1.0), "sV")
    p.run("vector", lambda E: E.memset(p.h_sb[:], 0.0), "sV")
    p.run("vector", lambda E: E.memset(p.h_bf[:], 0.0), "sV")
    vfence = p.run("vector", lambda E: E.memset(p.hist[:], 0.0), "sV")

    p.wait("gpsimd", "sV", vfence)
    p.dma("gpsimd", st[0:112, 0:D], p.b_down)
    p.dma("gpsimd", p.csb[0:1, 0:1024], p.b_base[0:1, 0:1024])
    p.dma("gpsimd", p.y_tmp[0:1, 0:118], p.brow)
    p.wait("vector", "sG", p.v("sG"))
    p.run("vector", lambda E: E.tensor_copy(p.bdown_sb[0:112, :], st[0:112, 0:D]), "sV")
    bb0 = p.run("vector", lambda E: E.tensor_copy(p.zbuf[0:1, 0:1024], p.csb[0:1, 0:1024]), "sV")
    p.wait("gpsimd", "sV", bb0)
    p.dma("gpsimd", p.csb[0:1, 0:1024], p.b_base[0:1, 1024:2048])
    p.wait("vector", "sG", p.v("sG"))
    bb = p.run("vector", lambda E: E.tensor_copy(p.zbuf[0:1, 1024:2048], p.csb[0:1, 0:1024]), "sV")
    vfence2 = p.run("vector", lambda E: E.tensor_copy(p.brow_sb[:], p.y_tmp[0:1, 0:118]), "sV")
    p.wait("gpsimd", "sV", bb)
    rw = p.dma("gpsimd", p.rowsc, p.zbuf[0:1, 0:D])
    p.wait("gpsimd", "sG", rw)
    p.dma("gpsimd", p.bdown_sb[112:113, :], p.rowsc)
    p.wait("sync", "sV", vfence2)
    p.wait("gpsimd", "sV", vfence2)

    p.wait("vector", "sV", p.v("sV"))
    for fc in range(32):
        p.wait("vector", "sW", p.v("sW"))
        for j in range(4):
            p.run("vector", lambda E, fc=fc, j=j: E.tensor_tensor(
                out=p.prep_b[:, sl(j * 128, 128)], in0=p.eye128_b[:],
                in1=p.cwT[:, fc * 4 + j:fc * 4 + j + 1].broadcast_to([128, 128]),
                op=ALU.mult), "sV")
        vv = p.v("sV")
        p.wait("sync", "sV", vv)
        p.dma("sync", p.wsc_diag[fc][:], p.prep_b[:, 0:512])

    p.wait("vector", "sW", p.v("sW"))
    p.wait("sync", "sA", p.v("sA"))
    p.wait("sync", "sV", p.v("sV"))
    stF = [p.scrF[:, 0:2048], p.scrF[:, 2048:4096]]
    stB = [p.ygt[:, 0:2048], p.ygt[:, 2048:4096]]
    pI = ["pIA", "pIB"]
    pO = ["pOA", "pOB"]
    pieces = [(k, off, w) for (off, w) in [(8192, 192), (4096, 2048), (6144, 2048)]
              for k in range(NK)]
    kb._zjobs = [(k, off) for off in (0, 2048) for k in range(NK)]
    lastcast = {}
    for i, (k, off, w) in enumerate(pieces):
        pi = i % 2
        if i >= 2:
            p.wait("sync", "sV", lastcast[pi])
        iv = p.dma("sync", stF[pi][:, 0:w], p.w_in[sl(k * 128, 128), sl(off, w)], sem=pI[pi])
        p.wait("vector", pI[pi], iv)
        if i >= 2:
            p.wait("vector", pO[pi], p.v(pO[pi]))
        cv = p.run("vector", lambda E, pi=pi, w=w: E.tensor_copy(
            stB[pi][:, 0:w], stF[pi][:, 0:w]), "sV")
        lastcast[pi] = cv
        p.wait("gpsimd", "sV", cv)
        if off < 4096:
            dst = p.wsc_zp[sl(off // 512, 4), :, sl(k * 512, 512)].rearrange("z q c -> q z c")
            srcap = stB[pi][:, 0:2048].rearrange("q (z c) -> q z c", z=4)
        elif off < 8192:
            g0 = (off - 4096) // 256
            dst = p.wsc_inp[sl(g0, 8), :, sl(k * 256, 256)].rearrange("g q c -> q g c")
            srcap = stB[pi][:, 0:2048].rearrange("q (g c) -> q g c", g=8)
        else:
            dst = p.wsc_inp[16][:, sl(k * 192, 192)]
            srcap = stB[pi][:, 0:192]
        p.dma("gpsimd", dst, srcap, sem=pO[pi])
    p.wait("sync", "pOA", p.v("pOA"))
    p.wait("sync", "pOB", p.v("pOB"))

    p.barrier()
    kb.tap("wsc_probe", kb.wsc_inp[0][:, 0:512])


def emit_prep_outbase(kb: "KB"):
    p = kb
    stF = [p.csb[:, 0:512], p.csb[:, 512:1024]]
    stB = [p.datt[:, 0:512], p.datt[:, 512:1024]]
    pI = ["p2IA", "p2IB"]
    pO = ["p2OA", "p2OB"]
    jobs = []
    for k in range(NKI):
        for h in range(4):
            jobs.append(("out", k, h))
    for k in range(NK):
        for h in range(4):
            jobs.append(("base", k, h))
    lastcast = {}
    for i, (kind, k, h) in enumerate(jobs):
        pi = i % 2
        src = (p.w_out if kind == "out" else p.w_base)[sl(k * 128, 128), sl(h * 512, 512)]
        dst = (p.wsc_outp if kind == "out" else p.wsc_basep)[h][:, sl(k * 512, 512)]
        if i >= 2:
            p.wait("sync", "sV", lastcast[pi])
        iv = p.dma("sync", stF[pi][:, 0:512], src, sem=pI[pi])
        p.wait("vector", pI[pi], iv)
        if i >= 2:
            p.wait("vector", pO[pi], p.v(pO[pi]))
        if kind == "out":
            cv = p.run("vector", lambda E, pi=pi, k=k: E.tensor_scalar(
                out=stB[pi][:, 0:512], in0=stF[pi][:, 0:512],
                scalar1=p.ns_cols[:, k:k + 1], scalar2=None, op0=ALU.mult), "sV")
        else:
            cv = p.run("vector", lambda E, pi=pi: E.tensor_copy(
                stB[pi][:, 0:512], stF[pi][:, 0:512]), "sV")
        lastcast[pi] = cv
        p.wait("gpsimd", "sV", cv)
        p.dma("gpsimd", dst, stB[pi][:, 0:512], sem=pO[pi])
    kb._prep2_v = max(lastcast.values())
    kb._prep2_o = p.v("p2OA")
    kb._prep2_o2 = p.v("p2OB")
    kb._prep2_i = (p.v("p2IA"), p.v("p2IB"))
    p.wait("sync", "p2OA", kb._prep2_o)
    p.wait("sync", "p2OB", kb._prep2_o2)


def emit_ln(kb: "KB", t, dst_slot, tap_prefix):
    p = kb
    for tsub in range(NCH):
        xs = p.xslot[:, sl(tsub * D, D)]
        p.selfsync("vector")
        p.selfsync("scalar")
        vs = p.run("vector", lambda E, xs=xs: E.tensor_reduce(
            out=p.sum_col[:], in_=xs, axis=AX.X, op=ALU.add), "sV")
        va = p.run("scalar", lambda E, xs=xs: E.activation(
            out=p.lns[:, 0:D], in_=xs, func=AF.Square, bias=0.0, scale=1.0,
            accum_out=p.ssq_col[:]), "sA")
        p.wait("vector", "sV", vs)
        vms = p.run("vector", lambda E: E.tensor_tensor(
            out=p.ms_col[:], in0=p.sum_col[:], in1=p.sum_col[:], op=ALU.mult), "sV")
        p.wait("vector", "sV", vms)
        p.wait("vector", "sA", va)
        vs2 = p.run("vector", lambda E: E.scalar_tensor_tensor(
            out=p.ms_col[:], in0=p.ms_col[:], scalar=-1.0 / D, in1=p.ssq_col[:],
            op0=ALU.mult, op1=ALU.add), "sV")
        p.wait("vector", "sV", vs2)
        vs2b = p.run("vector", lambda E: E.tensor_scalar(
            out=p.ms_col[:], in0=p.ms_col[:], scalar1=1.0 / D, scalar2=1e-5,
            op0=ALU.mult, op1=ALU.add), "sV")
        p.wait("vector", "sV", vs2b)
        vs2c = p.run("vector", lambda E: E.reciprocal(p.nmr_col[:], p.ms_col[:]), "sV")
        p.wait("scalar", "sV", vs2c)
        va2 = p.run("scalar", lambda E: E.activation(
            out=p.rstd_col[:], in_=p.nmr_col[:], func=AF.Sqrt, bias=0.0, scale=1.0), "sA")
        p.wait("vector", "sA", va2)
        vs3 = p.run("vector", lambda E: E.scalar_tensor_tensor(
            out=p.nmr_col[:], in0=p.sum_col[:], scalar=-1.0 / D, in1=p.rstd_col[:],
            op0=ALU.mult, op1=ALU.mult), "sV")
        p.wait("scalar", "sV", vs3)
        va3 = p.run("scalar", lambda E, xs=xs: E.activation(
            out=p.ntok[:], in_=xs,
            func=AF.Identity, bias=p.nmr_col[:], scale=p.rstd_col[:]), "sA")
        p.wait("tensor", "sA", va3)
        for kg in range(2):
            p.wait("tensor", kb._tr_free[0], kb._tr_free[1])
            vt_ = None
            for k8 in range(8):
                k = kg * 8 + k8
                vt_ = p.run("tensor", lambda E, k=k, k8=k8: E.transpose(
                    p.tr_ps[:, sl(k8 * 128, 128)], p.ntok[:, sl(k * 128, 128)],
                    p.eye128_b[:]), "sT" if k8 == 7 else None)
            p.wait("scalar", "sT", vt_)
            va4 = p.run("scalar", lambda E, kg=kg, tsub=tsub: E.activation(
                out=dst_slot.rearrange("q (k c) -> q k c", k=16)[:, sl(kg * 8, 8), sl(tsub * 128, 128)],
                in_=p.tr_ps[:].rearrange("q (k c) -> q k c", k=8),
                func=AF.Copy, bias=0.0, scale=1.0), "sA")
            kb._tr_free = ("sA", va4)


def _emit_zjob(kb):
    p = kb
    i = kb._zji
    kb._zji += 1
    k, off = kb._zjobs[i]
    pi = i % 2
    sIn = ("zIA", "zIB")[pi]
    sOut = ("zOA", "zOBB")[pi]
    stF = p.zcF[:, sl(pi * 2048, 2048)]
    stB = p.ygt[:, sl(pi * 2048, 2048)]
    if i >= 2:
        p.wait("scalar", "sV", kb._zlastcast[pi])
    iv = p.run("scalar", lambda E, stF=stF, k=k, off=off: E.dma_start(
        out=stF, in_=p.w_in[sl(k * 128, 128), sl(off, 2048)]), sIn, 16)
    p.wait("vector", sIn, iv)
    if i >= 2:
        p.wait("vector", sOut, p.v(sOut))
    cv = p.run("vector", lambda E, stB=stB, stF=stF: E.tensor_copy(stB, stF), "sV")
    kb._zlastcast[pi] = cv
    kb._zcast_v = cv
    p.wait("gpsimd", "sV", cv)
    p.dma("gpsimd", p.wsc_zp[sl(off // 512, 4), :, sl(k * 512, 512)].rearrange("z q c -> q z c"),
          stB.rearrange("q (z c) -> q z c", z=4), sem=sOut)


def emit_inproj_f(kb: "KB", t):
    p = kb
    conv_done = {}
    vtr_done = {}
    evac_done = {}
    psum_evac = {}
    wbuf_free = p.gfree.setdefault("wbuf", {})
    dg_free = p.gfree.setdefault("dg", {})

    def stream_group(g):
        par = g % 2
        w = 256 if g < 16 else 192
        if par in wbuf_free:
            p.wait("sync", "sT", wbuf_free[par])
        sname = "wPA" if par == 0 else "wPB"
        p.dma("sync", p.wbuf[:, sl(par * 4096, 16 * w)], p.wsc_inp[g][:, 0:16 * w], sem=sname)
        return (sname, p.v(sname))

    def stream_diag(fc):
        par = fc % 2
        if par in dg_free:
            p.wait("sync", "sT", dg_free[par])
        sname = "dgA" if par == 0 else "dgB"
        p.dma("sync", p.dgbuf[:, sl(par * 512, 512)], p.wsc_diag[fc][:], sem=sname)
        return (sname, p.v(sname))

    wsem = {16: stream_group(16)}
    dgsem = {0: stream_diag(0), 1: stream_diag(1)}

    for g in [16] + list(range(16)):
        par = g % 2
        p.wait("tensor", *wsem[g])
        if g < 16:
            for fi in range(2):
                fc = 2 * g + fi
                psv = p.psA[:, sl(fi * 512, 512)]
                if fi in psum_evac:
                    p.wait("tensor", *psum_evac[fi])
                for k in range(NK):
                    base = par * 4096 + k * 256 + fi * 128
                    mmv = p.run("tensor", lambda E, base=base, psv=psv, k=k: E.matmul(
                        psv, p.wbuf[:, sl(base, 128)], p.normed()[:, sl(k * 512, 512)],
                        start=(k == 0), stop=(k == NK - 1)), "sT" if k == NK - 1 else None)
                if fi == 1:
                    wbuf_free[par] = mmv
                    if g + 2 <= 15:
                        wsem[g + 2] = stream_group(g + 2)
                if fc >= 4:
                    p.wait("scalar", "sT", conv_done[fc - 4])
                p.wait("scalar", "sT", mmv)
                ev = p.run("scalar", lambda E, fc=fc, psv=psv: E.activation(
                    out=p.xrr[:, sl((fc % 4) * 515 + 3, 512)], in_=psv,
                    func=AF.Copy, bias=0.0, scale=1.0), "sA")
                evac_done[fc] = ev
                psum_evac[fi] = ("sA", ev)
                if fc == 0:
                    p.wait("gpsimd", "sA", ev)
                    kb.tap("wbuf0", p.wbuf[:, 0:4096])
        else:
            for j in range(3):
                psv = p.psA[0:64, sl(j * 512, 512)]
                coff = j * 64
                for k in range(NK):
                    base = par * 4096 + k * 192 + coff
                    bcv = p.run("tensor", lambda E, base=base, psv=psv, k=k: E.matmul(
                        psv, p.wbuf[:, sl(base, 64)], p.normed()[:, sl(k * 512, 512)],
                        start=(k == 0), stop=(k == NK - 1)),
                        "sT" if (j == 2 and k == NK - 1) else None)
            wbuf_free[par] = bcv
            p.wait("vector", "sT", bcv)
            p.wait("scalar", "sT", bcv)
            for j, dst in [(0, p.bm_b), (1, p.cm_b)]:
                psv = p.psA[0:64, sl(j * 512, 512)]
                v1 = p.run("vector", lambda E, psv=psv: E.tensor_scalar(
                    out=p.lns[0:64, 0:512], in0=psv, scalar1=0.0, scalar2=None,
                    op0=ALU.min), "sV")
                p.wait("scalar", "sV", v1)
                a1 = p.run("scalar", lambda E: E.activation(
                    out=p.lns[0:64, 512:1024], in_=p.lns[0:64, 0:512],
                    func=AF.Exp, bias=0.0, scale=1.0), "sA")
                p.wait("vector", "sA", a1)
                sttv = p.run("vector", lambda E, psv=psv, dst=dst: E.scalar_tensor_tensor(
                    out=dst[:, 0:512], in0=psv, scalar=1.0, in1=p.lns[0:64, 512:1024],
                    op0=ALU.add, op1=ALU.max), "sV")
                psum_evac[j] = ("sV", sttv)
            psv = p.psA[0:64, sl(2 * 512, 512)]
            sgv = p.run("scalar", lambda E, psv=psv: E.activation(
                out=p.lns[0:64, 1024:1536], in_=psv,
                func=AF.Sigmoid, bias=p.negdtb[:], scale=-1.0), "sA")
            p.wait("scalar", "sA", sgv)
            a4 = p.run("scalar", lambda E: E.activation(
                out=p.lns[0:64, 1536:2048], in_=p.lns[0:64, 1024:1536],
                func=AF.Ln, bias=0.0, scale=1.0), "sA")
            p.wait("vector", "sA", a4)
            p.run("vector", lambda E: E.tensor_scalar(
                out=p.la[:, 0:512], in0=p.lns[0:64, 1536:2048],
                scalar1=p.expA[:], scalar2=None, op0=ALU.mult), "sV")
            kb._conv_evac[0] = sgv
            wsem[0] = stream_group(0)
            wsem[1] = stream_group(1)

        if g < 16:
            for fi in range(2):
                fc = 2 * g + fi
                par_d = fc % 2
                p.wait("vector", "sA", evac_done[fc])
                hv = p.run("vector", lambda E, fc=fc: E.tensor_copy(
                    p.xrr[:, sl((fc % 4) * 515, 3)], p.hist[:, sl(fc * 3, 3)]), "sV")
                p.wait("tensor", "sV", hv)
                p.wait("tensor", *dgsem[fc])
                cps = p.psA[:, sl(1024 + par_d * 512, 512)]
                p.wait("tensor", "sA", kb._conv_evac.get(par_d, 0))
                for j in range(4):
                    cv = p.run("tensor", lambda E, j=j, par_d=par_d, fc=fc, cps=cps: E.matmul(
                        cps, p.dgbuf[:, sl(par_d * 512 + j * 128, 128)],
                        p.xrr[:, (fc % 4) * 515 + j:(fc % 4) * 515 + j + 512],
                        start=(j == 0), stop=(j == 3)), "sT" if j == 3 else None)
                conv_done[fc] = cv
                dg_free[par_d] = cv
                if fc + 2 < 32:
                    dgsem[fc + 2] = stream_diag(fc + 2)
                p.wait("vector", "sV", hv)
                p.run("vector", lambda E, fc=fc: E.tensor_copy(
                    p.hist[:, sl(fc * 3, 3)], p.xrr[:, sl((fc % 4) * 515 + 3 + 509, 3)]), "sV")
                if fc >= 4:
                    p.wait("scalar", "sT", vtr_done[fc - 4])
                p.wait("scalar", "sT", cv)
                xe = p.run("scalar", lambda E, fc=fc, cps=cps: E.activation(
                    out=p.xcr[:, sl((fc % 4) * 512, 512)], in_=cps,
                    func=AF.Silu, bias=p.cb_cols[:, fc:fc + 1], scale=1.0), "sA")
                kb._conv_evac[par_d] = xe
                p.wait("tensor", kb._tr_free[0], kb._tr_free[1])
                p.wait("tensor", "sA", xe)
                for c in range(NCH):
                    tv = p.run("tensor", lambda E, fc=fc, c=c: E.transpose(
                        p.tr_ps[:, sl(c * 128, 128)],
                        p.xcr[:, sl((fc % 4) * 512 + c * 128, 128)], p.eye128_b[:]),
                        "sT" if c == NCH - 1 else None)
                vtr_done[fc] = tv
                p.wait("scalar", "sT", tv)
                ve = p.run("scalar", lambda E, fc=fc: E.activation(
                    out=p.vt[:].rearrange("q (c f) -> q c f", c=NCH)[:, :, sl(fc * 128, 128)],
                    in_=p.tr_ps[:, 0:512].rearrange("q (c f) -> q c f", c=NCH),
                    func=AF.Copy, bias=0.0, scale=1.0), "sA")
                kb._tr_free = ("sA", ve)
                if t == 0 and kb._zji < len(kb._zjobs):
                    _emit_zjob(kb)


def emit_inproj_z(kb: "KB", t):
    p = kb
    if t == 0:
        for sn in ("zOA", "zOBB"):
            p.wait("sync", sn, p.v(sn))
            p.wait("scalar", sn, p.v(sn))
    zfree = p.gfree.setdefault('zw', {})
    evs = {}

    def stream(zc, kg):
        par = (zc * 8 + kg) % 2
        if par in zfree:
            p.wait("sync", "sT", zfree[par])
        sname = "zwA" if par == 0 else "zwB"
        p.dma("sync", p.zwbuf[:, sl(par * 1024, 1024)], p.wsc_zp[zc][:, sl(kg * 1024, 1024)], sem=sname)
        return (sname, p.v(sname))

    sems = {(0, 0): stream(0, 0), (0, 1): stream(0, 1)}
    for zc in range(8):
        for kg in range(8):
            par = (zc * 8 + kg) % 2
            p.wait("tensor", *sems[(zc, kg)])
            nxt = (zc, kg + 2) if kg + 2 < 8 else (zc + 1, kg - 6)
            for tsub in range(NCH):
                psv = p.psA[:, sl(tsub * 512, 512)]
                if kg == 0 and tsub in evs:
                    p.wait("tensor", "sA", evs[tsub])
                for k4 in range(2):
                    k = kg * 2 + k4
                    mv = p.run("tensor", lambda E, k=k, k4=k4, kg=kg, par=par, tsub=tsub, psv=psv: E.matmul(
                        psv, p.normed()[:, sl(k * 512 + tsub * 128, 128)],
                        p.zwbuf[:, sl(par * 1024 + k4 * 512, 512)],
                        start=(kg == 0 and k4 == 0), stop=(kg == 7 and k4 == 1)),
                        "sT" if (tsub == NCH - 1 and k4 == 1) else None)
            zfree[par] = mv
            if nxt[0] < 8:
                sems[nxt] = stream(*nxt)
        p.wait("scalar", "sT", p.v("sT"))
        p.wait("scalar", "sSO", p.v("sSO"))
        for tsub in range(NCH):
            evs[tsub] = p.run("scalar", lambda E, tsub=tsub: E.activation(
                out=p.prep_b[:, sl(tsub * 512, 512)], in_=p.psA[:, sl(tsub * 512, 512)],
                func=AF.Silu, bias=0.0, scale=1.0), "sA")
        p.wait("gpsimd", "sA", p.v("sA"))
        p.dma("gpsimd", p.zsc[:, :, sl(zc * 512, 512)].rearrange("c q f -> q c f"),
              p.prep_b[:, 0:2048].rearrange("q (c f) -> q c f", c=4), sem="sSO")
    kb._zsc_done = p.v("sSO")


def emit_ssd_pre(kb: "KB", t, c):
    p = kb
    pc = c % 2
    if c == 0:
        p.wait("vector", "sSO", p.v("sSO"))
        if t == 0:
            p.wait("scalar", "sV", kb._prep2_v)
            p.wait("scalar", "p2OA", kb._prep2_o)
            p.wait("scalar", "p2OB", kb._prep2_o2)
            p.wait("gpsimd", "sV", kb._prep2_v)
            p.wait("gpsimd", "p2IA", kb._prep2_i[0])
            p.wait("gpsimd", "p2IB", kb._prep2_i[1])
    la_sl = p.la[:, sl(c * S, S)]
    csh_sl = p.csh[:, sl(pc * S, S)]
    cst_sl = p.cst[:, sl(pc * NH, NH)]
    et_sl = p.et[:, sl(pc * NH, NH)]
    rt_sl = p.rt[:, sl(pc * NH, NH)]
    smt_sl = p.smt[:, sl(pc * S, S)]
    kcT = p.bm_b[:, sl(c * S, S)]
    qcT = p.cm_b[:, sl(c * S, S)]

    sv = p.run("vector", lambda E: E.tensor_tensor_scan(
        out=csh_sl, data0=la_sl, data1=la_sl, initial=0.0,
        op0=ALU.add, op1=ALU.bypass), "sV")
    p.wait("gpsimd", "sV", sv)
    p.wait("gpsimd", "sCB", kb._csflat_free[pc])
    kb._flv = p.dma("gpsimd", p.csflat[pc], csh_sl, sem="sFL")
    p.wait("tensor", "sV", kb._psS_free)
    p.wait("tensor", "sV", sv)
    trv = p.run("tensor", lambda E: E.transpose(
        p.psS[:, 0:64], csh_sl, p.eye64_f[:]), "sT")
    p.wait("vector", "sT", trv)
    cstv = p.run("vector", lambda E: E.tensor_copy(cst_sl, p.psS[:, 0:64]), "sV")
    p.wait("scalar", "sV", cstv)
    etv = p.run("scalar", lambda E: E.activation(
        out=et_sl, in_=cst_sl, func=AF.Exp, bias=0.0, scale=1.0), "sA")
    totcol = p.csh[:, pc * S + S - 1:pc * S + S]
    rhv = p.run("scalar", lambda E: E.activation(
        out=p.rh[:], in_=csh_sl, func=AF.Exp, bias=totcol, scale=-1.0), "sA")
    dcv = p.run("scalar", lambda E, pc=pc: E.activation(
        out=p.dcol[:, pc:pc + 1], in_=totcol, func=AF.Exp, bias=0.0, scale=1.0), "sA")
    p.wait("tensor", "sA", rhv)
    p.wait("tensor", kb._tr_free[0], kb._tr_free[1])
    rtv = p.run("tensor", lambda E: E.transpose(
        p.tr_ps[:, 0:64], p.rh[:], p.eye128_b[0:64, 0:64]), "sT")
    p.wait("vector", "sT", rtv)
    rtev = p.run("vector", lambda E: E.tensor_copy(rt_sl, p.tr_ps[:, 0:64]), "sV")
    p.wait("tensor", "sA", dcv)
    p.wait("tensor", "sV", cstv)
    dtv = p.run("tensor", lambda E, pc=pc: E.transpose(
        p.psS[0:64, 64:128], p.dcol[:, pc:pc + 1].broadcast_to([NH, 64]), p.eye64_f[:]), "sT")
    p.wait("vector", "sT", dtv)
    dtev = p.run("vector", lambda E, pc=pc: E.tensor_copy(
        p.dtile[:, sl(pc * NH, NH)], p.psS[0:64, 64:128]), "sV")
    p.wait("tensor", "sV", rtev)
    kctv = p.run("tensor", lambda E: E.transpose(
        p.tr_ps[:, 64:128], kcT, p.eye128_b[0:64, 0:64]), "sT")
    p.wait("vector", "sT", kctv)
    kcev = p.run("vector", lambda E, pc=pc: E.tensor_copy(
        p.kc_sb[:, sl(pc * NST, NST)], p.tr_ps[:, 64:128]), "sV")
    kb._tr_free = ("sV", kcev)
    p.wait("tensor", "sV", dtev)
    scv = p.run("tensor", lambda E: E.matmul(
        p.psS[:, 128:256], kcT, qcT, start=True, stop=True), "sT")
    p.wait("vector", "sT", scv)
    smv = p.run("vector", lambda E: E.tensor_tensor(
        out=smt_sl, in0=p.psS[:, 128:256], in1=p.mask01_b[:], op=ALU.mult), "sV")
    kb._pre[pc] = dict(cstv=cstv, etv=etv, smv=smv, rtev=rtev, dtev=dtev, kcev=kcev, scv=scv)
    kb._psS_free = smv


def _emit_dbuild(kb, t, c, e):
    p = kb
    pc = c % 2
    cst_sl = p.cst[:, sl(pc * NH, NH)]
    smt_sl = p.smt[:, sl(pc * S, S)]
    dslot = p.datt[:, sl((e % 2) * 1024, 1024)]
    if kb._csb_free:
        p.wait("gpsimd", "sV", kb._csb_free)
    p.wait("gpsimd", "sFL", kb._flv)
    cbv = p.dma("gpsimd", p.csb[:],
                p.csflat[pc].rearrange("h s -> (h s)")[sl(e * 1024, 1024)]
                .rearrange("(a f) -> a f", a=1).broadcast_to([128, 1024]), sem="sCB")
    kb._csflat_free[pc] = cbv
    p.wait("vector", "sCB", cbv)
    p.selfsync("vector")
    dsv = p.run("vector", lambda E, e=e: E.tensor_tensor(
        out=p.darg[:, 0:1024].rearrange("z (h s) -> z h s", h=8),
        in0=p.csb[:].rearrange("z (h s) -> z h s", h=8),
        in1=cst_sl[:, sl(e * 8, 8)].rearrange("z (h o) -> z h o", o=1).broadcast_to([128, 8, S]),
        op=ALU.subtract), "sV")
    kb._csb_free = dsv
    if kb._dslot_free.get(e % 2):
        p.wait("scalar", "sT", kb._dslot_free[e % 2])
    p.wait("scalar", "sV", dsv)
    absv = p.run("scalar", lambda E, dslot=dslot: E.activation(
        out=dslot, in_=p.darg[:, 0:1024], func=AF.Abs, bias=0.0, scale=1.0), "sA")
    p.wait("scalar", "sA", absv)
    expv = p.run("scalar", lambda E, dslot=dslot: E.activation(
        out=p.darg[:, 0:1024], in_=dslot, func=AF.Exp, bias=0.0, scale=-1.0), "sA")
    p.wait("vector", "sA", expv)
    p.wait("vector", "sV", kb._pre[pc]["smv"])
    attv = p.run("vector", lambda E, dslot=dslot: E.tensor_tensor(
        out=dslot.rearrange("z (h s) -> z h s", h=8),
        in0=p.darg[:, 0:1024].rearrange("z (h s) -> z h s", h=8),
        in1=smt_sl.rearrange("z (o s) -> z o s", o=1).broadcast_to([128, 8, S]),
        op=ALU.mult), "sV")
    kb._datt_ready[e % 2] = attv


def _emit_eighth(kb, t, c, e):
    p = kb
    pc = c % 2
    qcT = p.cm_b[:, sl(c * S, S)]
    vtc = p.vt[:, sl(c * DI, DI)]
    et_sl = p.et[:, sl(pc * NH, NH)]
    dslot = p.datt[:, sl((e % 2) * 1024, 1024)]

    p.selfsync("vector")
    p.wait("vector", "sV", kb._pre[pc]["rtev"])
    if kb._rv_free:
        p.wait("vector", "sT", kb._rv_free)
    rvv = p.run("vector", lambda E, e=e, pc=pc: E.tensor_tensor(
        out=p.rv[:].rearrange("z (h x) -> z h x", h=8),
        in0=vtc[:, sl(e * 512, 512)].rearrange("z (h x) -> z h x", h=8),
        in1=p.rt[:, sl(pc * NH + e * 8, 8)].rearrange("z (h o) -> z h o", o=1)
        .broadcast_to([128, 8, HD]), op=ALU.mult), "sV")
    vdv = p.run("vector", lambda E, e=e: E.tensor_tensor(
        out=p.vd[:].rearrange("z (h x) -> z h x", h=8),
        in0=vtc[:, sl(e * 512, 512)].rearrange("z (h x) -> z h x", h=8),
        in1=p.drep[:, sl(e * 8, 8)].rearrange("z (h o) -> z h o", o=1)
        .broadcast_to([128, 8, HD]), op=ALU.mult), "sV")

    if kb._yb_free:
        p.wait("tensor", "sV", kb._yb_free)
    iv = p.run("tensor", lambda E, e=e: E.matmul(
        p.psYb[:], qcT, p.h_bf[:, sl(e * 512, 512)], start=True, stop=True), "sT")
    p.wait("tensor", "sV", kb._datt_ready[e % 2])
    p.wait("tensor", "sV", vdv)
    if kb._ya_free:
        p.wait("tensor", "sV", kb._ya_free)
    p.run("tensor", lambda E: E.matmul(
        p.psYa[:], p.eye128_b[:], p.vd[:], start=True, stop=False), None)
    for i in range(8):
        av2 = p.run("tensor", lambda E, i=i, e=e, dslot=dslot: E.matmul(
            p.psYa[:, sl(i * 64, 64)], dslot[:, sl(i * 128, 128)],
            vtc[:, sl(e * 512 + i * 64, 64)],
            start=False, stop=(i == 7)), "sT" if i == 7 else None)
    kb._dslot_free[e % 2] = av2
    kb._rv_free = av2
    p.wait("tensor", "sV", kb._psS_free)
    p.wait("tensor", "sV", rvv)
    p.wait("tensor", "sV", kb._pre[pc]["kcev"])
    stv = p.run("tensor", lambda E, pc=pc: E.matmul(
        p.psS[0:64, 0:512], p.kc_sb[:, sl(pc * NST, NST)], p.rv[:],
        start=True, stop=True), "sT")
    kb._rv_free = stv
    p.wait("vector", "sT", av2)
    p.wait("vector", "sA", kb._pre[pc]["etv"])
    if kb._ysb_free:
        p.wait("vector", "sA", kb._ysb_free)
    c1 = p.run("vector", lambda E, e=e, pc=pc: E.tensor_tensor(
        out=p.y_tmp[:].rearrange("z (h x) -> z h x", h=8),
        in0=p.psYb.rearrange("z (h x) -> z h x", h=8),
        in1=et_sl[:, sl(e * 8, 8)].rearrange("z (h o) -> z h o", o=1)
        .broadcast_to([128, 8, HD]), op=ALU.mult), "sV")
    kb._yb_free = c1
    p.wait("vector", "sV", c1)
    c2 = p.run("vector", lambda E, e=e: E.tensor_tensor(
        out=p.y_sb[:, sl(e * 512, 512)], in0=p.y_tmp[:], in1=p.psYa, op=ALU.add), "sV")
    kb._ya_free = c2
    p.wait("vector", "sT", stv)
    p.wait("vector", "sV", kb._u1)
    u2 = p.run("vector", lambda E, e=e: E.tensor_tensor(
        out=p.h_sb[:, sl(e * 512, 512)], in0=p.h_sb[:, sl(e * 512, 512)],
        in1=p.psS[0:64, 0:512], op=ALU.add), "sV")
    kb._psS_free = u2
    p.wait("scalar", "sV", u2)
    p.wait("scalar", "sT", iv)
    p.run("scalar", lambda E, e=e: E.activation(
        out=p.h_bf[:, sl(e * 512, 512)], in_=p.h_sb[:, sl(e * 512, 512)],
        func=AF.Copy, bias=0.0, scale=1.0), "sA")
    kb._last_c2 = c2


def emit_ssd_main(kb: "KB", t, c):
    p = kb
    pc = c % 2
    p.selfsync("vector")
    p.wait("vector", "sA", p.v("sA"))
    p.wait("vector", "sV", kb._pre[pc]["dtev"])
    kb._u1 = p.run("vector", lambda E, pc=pc: E.tensor_tensor(
        out=p.h_sb[:].rearrange("n (h x) -> n h x", h=NH),
        in0=p.h_sb[:].rearrange("n (h x) -> n h x", h=NH),
        in1=p.dtile[:, sl(pc * NH, NH)].rearrange("n (h o) -> n h o", o=1)
        .broadcast_to([NST, NH, HD]), op=ALU.mult), "sV")
    _emit_dbuild(kb, t, c, 0)
    _emit_dbuild(kb, t, c, 1)
    for e in range(8):
        _emit_eighth(kb, t, c, e)
        if e + 2 < 8:
            _emit_dbuild(kb, t, c, e + 2)
    if t == 0 and c < 2:
        p.wait("scalar", "sV", kb._zcast_v)
        for sn in ("zIA", "zIB"):
            p.wait("scalar", sn, p.v(sn))
    p.wait("scalar", "sV", kb._last_c2)
    yslot = p.ybf[:, sl((c % 2) * DI, DI)]
    if kb._ybf_free.get(c % 2):
        p.wait("scalar", "sV", kb._ybf_free[c % 2])
    sqv = p.run("scalar", lambda E, yslot=yslot, c=c: E.activation(
        out=yslot, in_=p.y_sb[:], func=AF.Square, bias=0.0, scale=1.0,
        accum_out=p.ssc[:, c % 2:c % 2 + 1]), "sA")
    p.wait("scalar", "sA", sqv)
    cpv = p.run("scalar", lambda E, yslot=yslot: E.activation(
        out=yslot, in_=p.y_sb[:], func=AF.Copy, bias=0.0, scale=1.0), "sA")
    kb._ysb_free = cpv
    kb._ybf_ready[c % 2] = cpv


def emit_ssd_gate(kb: "KB", t, c):
    p = kb
    yslot = p.ybf[:, sl((c % 2) * DI, DI)]
    p.wait("gpsimd", "sV", kb._zbuf_free)
    p.wait("gpsimd", "sSO", kb._zsc_done)
    if c == 0:
        p.wait("gpsimd", "sV", kb._prep2_v)
    zv = p.dma("gpsimd", p.zbuf[:, 0:DI], p.zsc[c], sem="sZB")
    p.selfsync("vector")
    r1 = p.run("vector", lambda E, c=c: E.tensor_scalar(
        out=p.ms_col[:], in0=p.ssc[:, c % 2:c % 2 + 1], scalar1=1.0 / DI, scalar2=1e-6,
        op0=ALU.mult, op1=ALU.add), "sV")
    p.wait("vector", "sV", r1)
    r2 = p.run("vector", lambda E: E.reciprocal(p.ss_col[:], p.ms_col[:]), "sV")
    p.wait("scalar", "sV", r2)
    rsv = p.run("scalar", lambda E: E.activation(
        out=p.rstd2[:], in_=p.ss_col[:], func=AF.Sqrt, bias=0.0, scale=1.0), "sA")
    p.wait("vector", "sA", rsv)
    p.wait("vector", "sA", kb._ybf_ready[c % 2])
    p.wait("vector", "sZB", zv)
    if t == 0 and c == 0:
        for sn in ("zOA", "zOBB"):
            p.wait("vector", sn, p.v(sn))
    if kb._ygt_free:
        p.wait("vector", "sT", kb._ygt_free)
    gv = p.run("vector", lambda E, yslot=yslot: E.scalar_tensor_tensor(
        out=p.ygt[:], in0=yslot, scalar=p.rstd2[:], in1=p.zbuf[:, 0:DI],
        op0=ALU.mult, op1=ALU.mult), "sV")
    kb._zbuf_free = gv
    kb._ybf_free[c % 2] = gv
    p.wait("tensor", "sV", gv)
    for bg in range(4):
        p.wait("tensor", kb._tr_free[0], kb._tr_free[1])
        for f8 in range(8):
            fc = bg * 8 + f8
            tv = p.run("tensor", lambda E, fc=fc, f8=f8: E.transpose(
                p.tr_ps[:, sl(f8 * 128, 128)], p.ygt[:, sl(fc * 128, 128)], p.eye128_b[:]),
                "sT" if f8 == 7 else None)
        p.wait("scalar", "sT", tv)
        dst = (p.slotA if bg < 2 else p.slotB)
        ev = p.run("scalar", lambda E, dst=dst, bg=bg, c=c: E.activation(
            out=dst.rearrange("z (f x) -> z f x", f=16)[:, sl((bg % 2) * 8, 8), sl(c * 128, 128)],
            in_=p.tr_ps[:].rearrange("z (f x) -> z f x", f=8),
            func=AF.Copy, bias=0.0, scale=1.0), "sA")
        kb._tr_free = ("sA", ev)
    kb._ygt_free = p.v("sT")


def emit_outproj(kb: "KB", t):
    p = kb
    p.wait("gpsimd", "sT", p.v("sT"))
    p.wait("gpsimd", "sV", p.v("sV"))
    uld = p.dma("gpsimd", p.xslot[:].rearrange("z (a d_) -> z a d_", a=NCH),
                p.u[sl(t * TT, TT), :].rearrange("(a z) d_ -> z a d_", z=128), sem="sU")
    ofree = p.gfree.setdefault('ow', {})
    evs = {}

    def stream(cb, kg):
        par = (cb * 16 + kg) % 2
        if par in ofree:
            p.wait("sync", "sT", ofree[par])
        sname = "owA" if par == 0 else "owB"
        p.dma("sync", p.owbuf[:, sl(par * 1024, 1024)], p.wsc_outp[cb][:, sl(kg * 1024, 1024)], sem=sname)
        return (sname, p.v(sname))

    sems = {(0, 0): stream(0, 0), (0, 1): stream(0, 1)}
    for cb in range(4):
        for kg in range(16):
            par = (cb * 16 + kg) % 2
            p.wait("tensor", *sems[(cb, kg)])
            nxt = (cb, kg + 2) if kg + 2 < 16 else (cb + 1, kg + 2 - 16)
            for tsub in range(NCH):
                psv = p.psA[:, sl(tsub * 512, 512)]
                if kg == 0 and tsub in evs:
                    p.wait("tensor", "sV", evs[tsub])
                for k4 in range(2):
                    k = kg * 2 + k4
                    mv = p.run("tensor", lambda E, k=k, k4=k4, kg=kg, par=par, tsub=tsub, psv=psv: E.matmul(
                        psv, p.ygf_slice(k, tsub), p.owbuf[:, sl(par * 1024 + k4 * 512, 512)],
                        start=(kg == 0 and k4 == 0), stop=(kg == 15 and k4 == 1)),
                        "sT" if (tsub == NCH - 1 and k4 == 1) else None)
            ofree[par] = mv
            if nxt[0] < 4:
                sems[nxt] = stream(*nxt)
        p.wait("vector", "sT", p.v("sT"))
        p.wait("vector", "sU", uld)
        for tsub in range(NCH):
            dst = p.xslot[:, sl(tsub * D + cb * 512, 512)]
            evs[tsub] = p.run("vector", lambda E, dst=dst, tsub=tsub: E.tensor_tensor(
                out=dst, in0=p.psA[:, sl(tsub * 512, 512)], in1=dst, op=ALU.add), "sV")


def emit_ffn(kb: "KB", t):
    p = kb
    n2f = p.slotA
    for tsub in range(NCH):
        p.selfsync("vector")
        p.selfsync("scalar")
        psv = p.psB[:, 0:128]
        for k in range(NK):
            p.run("tensor", lambda E, k=k, tsub=tsub, psv=psv: E.matmul(
                psv[:, 0:118], p.n2f_slice(k, tsub), p.wsmall_sb[:, sl(k * 118, 118)],
                start=(k == 0), stop=False), None)
        smv = p.run("tensor", lambda E, psv=psv: E.matmul(
            psv[:, 0:118], p.ones_row[:], p.brow_sb[:], start=False, stop=True), "sT")
        p.wait("vector", "sT", smv)
        lg = psv[:, 16:22]
        v1 = p.run("vector", lambda E, lg=lg: E.reduce_max(
            out=p.m1c[:], in_=lg, axis=AX.X), "sV")
        p.wait("vector", "sV", v1)
        v2 = p.run("vector", lambda E: E.tensor_scalar(
            out=p.ms_col[:], in0=p.m1c[:], scalar1=-1.0, scalar2=None, op0=ALU.mult), "sV")
        p.wait("scalar", "sV", v2)
        a1 = p.run("scalar", lambda E, lg=lg: E.activation(
            out=p.gates[:], in_=lg, func=AF.Exp, bias=p.ms_col[:], scale=1.0,
            accum_out=p.sumg[:]), "sA")
        p.wait("vector", "sA", a1)
        v3 = p.run("vector", lambda E: E.reciprocal(p.rsg[:], p.sumg[:]), "sV")
        p.wait("vector", "sV", v3)
        v4 = p.run("vector", lambda E: E.tensor_scalar(
            out=p.gates[:], in0=p.gates[:], scalar1=p.rsg[:], scalar2=None, op0=ALU.mult), "sV")
        p.wait("vector", "sV", v4)
        v5 = p.run("vector", lambda E: E.reduce_max(
            out=p.m1c[:], in_=p.gates[:], axis=AX.X), "sV")
        p.wait("vector", "sV", v5)
        v6 = p.run("vector", lambda E: E.tensor_scalar(
            out=p.gm[:], in0=p.gates[:], scalar1=p.m1c[:], scalar2=-1e30,
            op0=ALU.is_ge, op1=ALU.mult), "sV")
        p.wait("vector", "sV", v6)
        v7 = p.run("vector", lambda E: E.tensor_tensor(
            out=p.gm[:], in0=p.gates[:], in1=p.gm[:], op=ALU.add), "sV")
        p.wait("vector", "sV", v7)
        v8 = p.run("vector", lambda E: E.reduce_max(
            out=p.m2c[:], in_=p.gm[:], axis=AX.X), "sV")
        p.wait("vector", "sV", v8)
        v9 = p.run("vector", lambda E: E.tensor_scalar(
            out=p.wpe[:], in0=p.gates[:], scalar1=p.m2c[:], scalar2=None, op0=ALU.is_ge), "sV")
        p.wait("vector", "sV", v9)
        v10 = p.run("vector", lambda E: E.tensor_tensor(
            out=p.wpe[:], in0=p.gates[:], in1=p.wpe[:], op=ALU.mult), "sV")
        p.wait("scalar", "sT", smv)
        a2 = p.run("scalar", lambda E, psv=psv: E.activation(
            out=p.ffn_small[:], in_=psv[:, 0:118], func=AF.Copy, bias=0.0, scale=1.0), "sA")
        p.wait("vector", "sA", a2)
        p.wait("vector", "sV", v10)
        s1 = p.run("vector", lambda E: E.tensor_tensor(
            out=p.stackt[:, 16:112].rearrange("z (e r) -> z e r", e=6),
            in0=p.ffn_small[:, 22:118].rearrange("z (e r) -> z e r", e=6),
            in1=p.wpe[:].rearrange("z (e o) -> z e o", o=1).broadcast_to([128, 6, 16]),
            op=ALU.mult), "sV")
        s2 = p.run("vector", lambda E: E.tensor_copy(
            p.stackt[:, 0:16], p.ffn_small[:, 0:16]), "sV")
        p.wait("tensor", "sV", s2)
        tvv = p.run("tensor", lambda E: E.transpose(
            p.tr_ps[0:113, 0:128], p.stackt[:], p.eye128_b[:]), "sT")
        p.wait("vector", "sT", tvv)
        sev = p.run("vector", lambda E, tsub=tsub: E.tensor_copy(
            p.stackT[:, sl(tsub * 128, 128)], p.tr_ps[0:113, 0:128]), "sV")
        p.wait("tensor", "sV", sev)

    bfree = p.gfree.setdefault('bw', {})
    evs = {}
    stv = {}

    def stream(cb, kg):
        par = (cb * 8 + kg) % 2
        if par in bfree:
            p.wait("sync", "sT", bfree[par])
        sname = "bwA" if par == 0 else "bwB"
        p.dma("sync", p.bwbuf[:, sl(par * 1024, 1024)], p.wsc_basep[cb][:, sl(kg * 1024, 1024)], sem=sname)
        return (sname, p.v(sname))

    sems = {(0, 0): stream(0, 0), (0, 1): stream(0, 1)}
    for cb in range(4):
        for kg in range(8):
            par = (cb * 8 + kg) % 2
            p.wait("tensor", *sems[(cb, kg)])
            nxt = (cb, kg + 2) if kg + 2 < 8 else (cb + 1, kg + 2 - 8)
            for tsub in range(NCH):
                psv = p.psA[:, sl(tsub * 512, 512)]
                if kg == 0 and tsub in evs:
                    p.wait("tensor", "sV", evs[tsub])
                for k4 in range(2):
                    k = kg * 2 + k4
                    mv = p.run("tensor", lambda E, k=k, k4=k4, kg=kg, par=par, tsub=tsub, psv=psv: E.matmul(
                        psv, p.n2f_slice(k, tsub), p.bwbuf[:, sl(par * 1024 + k4 * 512, 512)],
                        start=(kg == 0 and k4 == 0), stop=False),
                        "sT" if (tsub == NCH - 1 and k4 == 1) else None)
            bfree[par] = mv
            if nxt[0] < 4:
                sems[nxt] = stream(*nxt)
        for tsub in range(NCH):
            psv = p.psA[:, sl(tsub * 512, 512)]
            dv = p.run("tensor", lambda E, tsub=tsub, psv=psv, cb=cb: E.matmul(
                psv, p.stackT[:, sl(tsub * 128, 128)], p.bdown_sb[:, sl(cb * 512, 512)],
                start=False, stop=True), "sT" if tsub == NCH - 1 else None)
        p.wait("vector", "sT", dv)
        if cb > 0:
            p.wait("vector", "sSO", stv[cb - 1])
        for tsub in range(NCH):
            evs[tsub] = p.run("vector", lambda E, tsub=tsub, cb=cb: E.tensor_tensor(
                out=p.x2_sb[:, sl(tsub * 512, 512)], in0=p.psA[:, sl(tsub * 512, 512)],
                in1=p.xslot[:, sl(tsub * D + cb * 512, 512)], op=ALU.add), "sV")
        p.wait("gpsimd", "sV", max(evs.values()))
        for tsub in range(NCH):
            p.dma("gpsimd",
                  p.out[sl(t * TT + tsub * 128, 128), sl(cb * 512, 512)],
                  p.x2_sb[:, sl(tsub * 512, 512)], sem="sSO")
        stv[cb] = p.v("sSO")


def build_full(ntiles=NTILE, debug=()):
    kb = KB(debug=debug)
    p = kb
    emit_prep(kb)
    for t in range(ntiles):
        p.wait("gpsimd", "sV", p.v("sV"))
        p.wait("gpsimd", "sA", p.v("sA"))
        p.wait("gpsimd", "sT", p.v("sT"))
        uld = p.dma("gpsimd", p.xslot[:].rearrange("z (a d_) -> z a d_", a=NCH),
                    p.u[sl(t * TT, TT), :].rearrange("(a z) d_ -> z a d_", z=128), sem="sU")
        p.wait("vector", "sU", uld)
        p.wait("scalar", "sU", uld)
        emit_ln(kb, t, kb.slotA, "ln1")
        p.barrier()
        kb.tap("normed_t%d" % t, kb.slotA[:])
        emit_inproj_f(kb, t)
        p.barrier()
        kb.tap("vt_t%d" % t, kb.vt[:])
        kb.tap("bm_t%d" % t, kb.bm_b[:])
        kb.tap("cm_t%d" % t, kb.cm_b[:])
        kb.tap("la_t%d" % t, kb.la[:])
        if t == 0:
            emit_prep_outbase(kb)
        emit_ssd_pre(kb, t, 0)
        emit_ssd_pre(kb, t, 1)
        emit_ssd_main(kb, t, 0)
        emit_inproj_z(kb, t)
        emit_ssd_pre(kb, t, 2)
        emit_ssd_main(kb, t, 1)
        emit_ssd_gate(kb, t, 0)
        kb.tap("ysb_t%d_c%d" % (t, 0), kb.ybf[:, 0:DI])
        emit_ssd_pre(kb, t, 3)
        emit_ssd_main(kb, t, 2)
        emit_ssd_gate(kb, t, 1)
        emit_ssd_main(kb, t, 3)
        emit_ssd_gate(kb, t, 2)
        emit_ssd_gate(kb, t, 3)
        p.barrier()
        emit_outproj(kb, t)
        p.barrier()
        kb.tap("x1_t%d" % t, kb.xslot[:])
        if t == 0:
            for sn in ("zOA", "zOBB"):
                p.wait("scalar", sn, p.v(sn))
        emit_ln(kb, t, kb.slotA, "ln2")
        p.barrier()
        emit_ffn(kb, t)
        p.barrier()
    return kb


def _make_in_map(inputs, core):
    f = np.ascontiguousarray
    u = f(np.asarray(inputs["L0_attn_u"][core], np.float32))
    w_small = np.concatenate([
        np.asarray(inputs["L0_ffn_A_s"], np.float32),
        np.asarray(inputs["L0_ffn_W_router"], np.float32),
        np.asarray(inputs["L0_ffn_A_r"], np.float32).transpose(1, 0, 2).reshape(2048, 96),
    ], axis=1)
    b_down = np.concatenate([
        np.asarray(inputs["L0_ffn_B_s"], np.float32),
        np.asarray(inputs["L0_ffn_B_r"], np.float32).reshape(96, 2048),
    ], axis=0)
    brow = np.zeros((1, 118), np.float32)
    brow[0, 16:22] = np.asarray(inputs["L0_ffn_b_router"], np.float32)
    mask01 = np.triu(np.ones((128, 128), np.float32))
    return dict(
        u=u,
        w_in=f(np.asarray(inputs["L0_attn_in_proj_w"], np.float32)),
        convw=f(np.asarray(inputs["L0_attn_conv_weight"], np.float32)
                .reshape(4, 32, 128).transpose(2, 1, 0).reshape(128, 128)),
        convb=f(np.asarray(inputs["L0_attn_conv_bias"], np.float32).reshape(32, 128).T),
        A_log=np.asarray(inputs["L0_attn_A_log"], np.float32).reshape(-1, 1),
        dt_bias=np.asarray(inputs["L0_attn_dt_bias"], np.float32).reshape(-1, 1),
        D_param=np.asarray(inputs["L0_attn_D_param"], np.float32).reshape(1, -1),
        norm_scale=f(np.asarray(inputs["L0_attn_norm_scale"], np.float32).reshape(32, 128).T),
        w_out=f(np.asarray(inputs["L0_attn_out_proj_w"], np.float32)),
        w_base=f(np.asarray(inputs["L0_ffn_W_base"], np.float32)),
        b_base=np.asarray(inputs["L0_ffn_b_base"], np.float32).reshape(1, -1),
        w_small=f(w_small),
        b_down=f(b_down),
        brow=brow,
        eye128=np.eye(128, dtype=np.float32),
        mask01=mask01,
    )


def kernel(**inputs):
    from concourse.bass_utils import run_bass_kernel_spmd
    kb = build_full(ntiles=NTILE)
    nc = kb.finalize()
    in_maps = [_make_in_map(inputs, c) for c in range(8)]
    res = run_bass_kernel_spmd(nc, in_maps, core_ids=list(range(8)))
    out = np.stack([np.asarray(r["out"], np.float32) for r in res.results], axis=0)
    return out


# version 11
# speedup vs baseline: 1.0105x; 1.0105x over previous
import numpy as np
import concourse.bass as bass
import concourse.mybir as mybir

F32 = mybir.dt.float32
BF16 = mybir.dt.bfloat16
AF = mybir.ActivationFunctionType
ALU = mybir.AluOpType
AX = mybir.AxisListType

L, D, DI, NH, HD, NST = 4096, 2048, 4096, 64, 64, 64
TT, S = 512, 128
NTILE, NCH = L // TT, TT // S
DIP = 8384
NK = D // 128
NKI = DI // 128
ENGS = ("sync", "vector", "scalar", "tensor", "gpsimd")


def sl(off, n):
    return slice(off, off + n)


class Sem:
    __slots__ = ("h", "val")

    def __init__(self, nc):
        self.h = nc.alloc_semaphore()
        self.val = 0


class KB:
    def __init__(self, debug=()):
        nc = self.nc = bass.Bass("TRN2", target_bir_lowering=False)
        self.streams = {e: [] for e in ENGS}
        self.debug = dict(debug)
        self.dbg_t = {}
        self.dbg_done = set()

        din = lambda n, sh: nc.dram_tensor(n, sh, F32, kind="ExternalInput").ap()
        self.u = din("u", [L, D])
        self.w_in = din("w_in", [D, DIP])
        self.convw = din("convw", [128, 128])
        self.convb = din("convb", [128, 32])
        self.A_log = din("A_log", [NH, 1])
        self.dt_bias = din("dt_bias", [NH, 1])
        self.D_param = din("D_param", [1, NH])
        self.norm_scale = din("norm_scale", [128, 32])
        self.w_out = din("w_out", [DI, D])
        self.w_base = din("w_base", [D, D])
        self.b_base = din("b_base", [1, D])
        self.w_small = din("w_small", [D, 118])
        self.b_down = din("b_down", [112, D])
        self.brow = din("brow", [1, 118])
        self.eye128 = din("eye128", [128, 128])
        self.mask01 = din("mask01", [128, 128])
        self.out = nc.dram_tensor("out", [L, D], F32, kind="ExternalOutput").ap()
        for name, (shape, dt_) in self.debug.items():
            self.dbg_t[name] = nc.dram_tensor("dbg_" + name, shape, dt_, kind="ExternalOutput").ap()

        dint = lambda n, sh, dt_: nc.dram_tensor(n, sh, dt_, kind="Internal").ap()
        self.wsc_inp = dint("wsc_inp", [17, 128, 4096], BF16)
        self.wsc_zp = dint("wsc_zp", [8, 128, 8192], BF16)
        self.wsc_outp = dint("wsc_outp", [4, 128, NKI * 512], BF16)
        self.wsc_basep = dint("wsc_basep", [4, 128, NK * 512], BF16)
        self.wsc_diag = dint("wsc_diag", [32, 128, 512], BF16)
        self.zsc = dint("zsc", [NCH, 128, DI], BF16)
        self.csflat = dint("csflat", [2, NH, S], F32)
        self.rowsc = dint("rowsc", [1, D], BF16)

        DTSZ = {F32: 4, BF16: 2}
        ARENA = 212832
        self.arena = nc.alloc_sbuf_tensor("arena", [128, ARENA], mybir.dt.uint8)
        self._abase = nc.lookup_mloc(self.arena).addr
        self._cursor = 0
        self._mstart = {}
        self._table = []

        def A(n, sh, dt_, overlay=None):
            nbytes = sh[-1] * DTSZ[dt_]
            for d in sh[1:-1]:
                nbytes *= d
            nbytes = (nbytes + 31) // 32 * 32
            if overlay is not None:
                start = overlay
            else:
                start = self._cursor
                self._cursor += nbytes
            assert start + nbytes <= ARENA, f"SBUF arena overflow at {n}: {start + nbytes}"
            self._table.append((n, start, nbytes, overlay is not None))
            h = nc.virtual_sbuf_tensor(n, sh, dt_)
            nc._tensor(h.name, h.shape, h.dtype, type="SB")
            mloc = nc.lookup_mloc(h)
            mloc.allocated = True
            mloc.addr = self._abase + start
            ap = h.ap()
            self._mstart[n] = start
            return ap

        self.eye128_b = A("eye128_b", [128, 128], BF16)
        self.eye64_f = A("eye64_f", [64, 64], F32)
        self.mask01_b = A("mask01_b", [128, 128], BF16)
        self.ns_cols = A("ns_cols", [128, NKI], F32)
        self.cb_cols = A("cb_cols", [128, 32], F32)
        self.negdtb = A("negdtb", [NH, 1], F32)
        self.expA = A("expA", [NH, 1], F32)
        self.drep = A("drep", [128, NH], BF16)
        self.wsmall_sb = A("wsmall_sb", [128, NK * 118], BF16)
        self.bdown_sb = A("bdown_sb", [113, D], BF16)
        self.brow_sb = A("brow_sb", [1, 118], BF16)
        self.ones_row = A("ones_row", [1, 128], BF16)
        self.cwT = A("cwT", [128, 128], F32)

        self.vt = A("vt", [128, NCH * DI], BF16)
        self.xslot = A("xslot", [128, NCH * D], F32, overlay=self._mstart["vt"])
        self.slotA = A("slotA", [128, 8192], BF16)
        self.slotB = A("slotB", [128, 8192], BF16)
        self.zbuf = None
        self.xrr = A("xrr", [128, 6 * 515], BF16)
        self.xcr = A("xcr", [128, 4 * 512], BF16)
        self.zbuf = A("zbuf", [128, DI], BF16, overlay=self._mstart["xrr"])
        self.darg = A("darg", [128, 2048], BF16)
        self.datt = A("datt", [128, 2 * 1024], BF16)
        self.csb = A("csb", [128, 1024], F32)
        self.ygt = A("ygt", [128, DI], BF16)
        self.ybf = A("ybf", [128, 2 * DI], BF16)
        self.zcF = A("zcF", [128, DI], F32, overlay=self._mstart["ybf"])
        self.prep_b = self.ygt[:, 0:2048]
        self.ntok = self.ygt[:, 2048:4096]
        self.rv = A("rv", [128, 512], BF16)
        self.vd = A("vd", [128, 512], BF16)
        self.y_tmp = A("y_tmp", [128, 512], F32)
        self.scrF = A("scrF", [128, DI], F32)
        self.y_sb = self.scrF[:, 0:DI]
        self.lns = self.scrF[:, 0:D]
        self.x2_sb = self.scrF[:, D:2 * D]
        self.prep_f = self.scrF[:, 0:DI]
        self.h_sb = A("h_sb", [NST, DI], F32)
        self.h_bf = A("h_bf", [NST, DI], BF16)

        self.wbuf = A("wbuf", [128, 2 * NK * 256], BF16)
        self.obF = A("obF", [128, 1024], F32, overlay=self._mstart["wbuf"])
        self.obB = A("obB", [128, 1024], BF16, overlay=self._mstart["wbuf"] + 4096)
        self.zwbuf = A("zwbuf", [128, 2 * 2 * 512], BF16)
        self.owbuf = A("owbuf", [128, 2 * 2 * 512], BF16)
        self.bwbuf = A("bwbuf", [128, 2 * 2 * 512], BF16)
        self.dgbuf = A("dgbuf", [128, 2 * 4 * 128], BF16)

        self.smt = A("smt", [128, 2 * S], BF16)
        self.bm_b = A("bm_b", [64, TT], BF16)
        self.cm_b = A("cm_b", [64, TT], BF16)
        self.rh = A("rh", [NH, S], BF16)
        self.rt = A("rt", [128, 2 * NH], BF16)
        self.kc_sb = A("kc_sb", [128, 2 * NST], BF16)
        self.hist = A("hist", [128, 32 * 3], BF16)
        self.stackt = A("stackt", [128, 113], BF16)
        self.stackT = A("stackT", [113, TT], BF16)
        self.ffn_small = A("ffn_small", [128, 118], BF16)
        self.csh = A("csh", [NH, 2 * S], F32)
        self.cst = A("cst", [128, 2 * NH], F32)
        self.et = A("et", [128, 2 * NH], F32)
        self.la = A("la", [NH, TT], F32)
        self.dtile = A("dtile", [NST, 2 * NH], F32)
        self.dcol = A("dcol", [NH, 2], F32)
        self.gates = A("gates", [128, 6], F32)
        self.gm = A("gm", [128, 6], F32)
        self.wpe = A("wpe", [128, 6], F32)
        cols = lambda n: A(n, [128, 1], F32)
        self.sum_col, self.ssq_col, self.ms_col = cols("sum_col"), cols("ssq_col"), cols("ms_col")
        self.rstd_col, self.nmr_col = cols("rstd_col"), cols("nmr_col")
        self.ss_col, self.rstd2 = cols("ss_col"), cols("rstd2")
        self.ssc = A("ssc", [128, 2], F32)
        m4 = A("m4cols", [128, 4], F32)
        self.m1c, self.m2c, self.sumg, self.rsg = m4[:, 0:1], m4[:, 1:2], m4[:, 2:3], m4[:, 3:4]

        Pp = lambda n, sh, dt_: nc.alloc_psum_tensor(n, sh, dt_).ap()
        self.psA = Pp("psA", [128, 2048], F32)
        self.psYa = Pp("psYa", [128, 512], F32)
        self.psYb = Pp("psYb", [128, 512], F32)
        self.psS = Pp("psS", [128, 512], F32)
        self.tr_ps = Pp("tr_ps", [128, 1024], BF16)
        self.ms_ps = self.psS
        self.psB = self.psS

        self.S_ = {n: Sem(nc) for n in ("sW", "sG", "sV", "sA", "sT")}
        self.DMA_SEMS = {"sW", "sG"}
        self.gfree = {}
        self._zbuf_free = 0
        self._csb_free = 0
        self._csflat_free = {0: 0, 1: 0}
        self._zsc_done = 0
        self._prep2_v = 0
        self._prep2_o = 0
        self._psS_free = 0
        self._tr_free = ("sV", 0)
        self._dslot_free = {}
        self._datt_ready = {}
        self._rv_free = 0
        self._ya_free = 0
        self._yb_free = 0
        self._ysb_free = 0
        self._ybf_free = {}
        self._ybf_ready = {}
        self._ygt_free = 0
        self._u1 = 0
        self._last_c2 = 0
        self._flv = 0
        self._pre = {0: {}, 1: {}}
        self._conv_evac = {}
        self._zjobs = []
        self._zji = 0
        self._zlastcast = {}
        self._zcast_v = 0
        self._objobs = []
        self._obji = 0
        self._oblastcast = {}
        self._ob_out_gate = None
        self._ob_all_gate = None

    def normed(self):
        return self.slotA

    def n2f_slice(self, k, tsub):
        return self.slotA[:, sl(k * 512 + tsub * 128, 128)]

    def ygf_slice(self, k, tsub):
        base = k * 512 + tsub * 128
        return (self.slotA if k < 16 else self.slotB)[:, sl(base - (8192 if k >= 16 else 0), 128)]

    def emit(self, eng, fn):
        self.streams[eng].append(fn)

    def sem(self, name):
        if name not in self.S_:
            self.S_[name] = Sem(self.nc)
        return self.S_[name]

    def wait(self, eng, semname, val):
        s = self.sem(semname)
        self.emit(eng, lambda E, h=s.h, v=val: E.wait_ge(h, v))

    def run(self, eng, fn, inc=None, n=1):
        if inc is not None:
            if n == 16:
                self.DMA_SEMS.add(inc)
            s = self.sem(inc)
            s.val += n
            self.emit(eng, lambda E, fn=fn, h=s.h, n=n: fn(E).then_inc(h, n))
            return s.val
        self.emit(eng, lambda E, fn=fn: fn(E))
        return None

    def dma(self, eng, out, in_, slow=False, sem=None):
        inc = sem or {"sync": "sW", "gpsimd": "sG"}[eng]
        self.DMA_SEMS.add(inc)
        return self.run(eng, lambda E, o=out, i=in_, s=slow: E.dma_start(
            out=o, in_=i, allow_slow_non_contiguous=s), inc, 16)

    def v(self, name):
        return self.sem(name).val

    def selfsync(self, eng):
        em = {"vector": "sV", "scalar": "sA", "tensor": "sT"}[eng]
        self.wait(eng, em, self.S_[em].val)

    def barrier(self, engines=("vector", "scalar", "tensor"), dma=(), full=False):
        em = {"vector": "sV", "scalar": "sA", "tensor": "sT"}
        if full:
            for e in engines:
                self.run(e, lambda E: E.drain(), em[e], 1)
        dma = tuple(self.DMA_SEMS) if full else tuple(dma)
        vals = {n: self.v(n) for n in list(em.values()) + list(dma)}
        targets = [em[e] for e in engines] + list(dma)
        waiters = engines + (("sync", "gpsimd") if full else ())
        for e in waiters:
            for sn in targets:
                self.wait(e, sn, vals[sn])

    def tap(self, name, src_ap):
        if name in self.dbg_t and name not in self.dbg_done:
            self.dbg_done.add(name)
            self.barrier(full=True)
            self.dma("gpsimd", self.dbg_t[name], src_ap, sem="sSO")
            self.barrier(full=True)

    def finalize(self):
        for sn in sorted(self.DMA_SEMS):
            self.wait("gpsimd", sn, self.v(sn))
        nc = self.nc
        with nc.Block() as block:
            for eng in ENGS:
                stream = self.streams[eng]
                dec = getattr(block, eng)

                def make(stream):
                    def body(E):
                        for fn in stream:
                            fn(E)
                    return body
                dec(make(stream))
        return nc



def emit_prep(kb: "KB"):
    p = kb
    st = p.scrF
    p.dma("gpsimd", st[:, 0:128], p.eye128)
    p.dma("gpsimd", p.eye64_f, p.eye128[0:64, 0:64])
    p.dma("gpsimd", st[:, 128:256], p.mask01)
    p.dma("gpsimd", p.ns_cols, p.norm_scale)
    p.dma("gpsimd", p.cb_cols, p.convb)
    p.dma("gpsimd", st[0:64, 2208:2209], p.dt_bias)
    p.dma("gpsimd", st[0:64, 2209:2210], p.A_log)
    p.dma("gpsimd", st[:, 256:320],
          p.D_param.rearrange("a (o h) -> a o h", o=1).broadcast_to([1, 128, NH]))
    p.dma("gpsimd", st[:, 320:320 + 16 * 118].rearrange("q (k c) -> q k c", k=16),
          p.w_small.rearrange("(k q) c -> q k c", q=128))
    p.dma("gpsimd", p.cwT, p.convw)
    gv = p.v("sG")
    p.wait("vector", "sG", gv)
    p.run("vector", lambda E: E.tensor_copy(p.eye128_b[:], st[:, 0:128]), "sV")
    p.run("vector", lambda E: E.tensor_copy(p.mask01_b[:], st[:, 128:256]), "sV")
    p.run("vector", lambda E: E.tensor_copy(p.drep[:], st[:, 256:320]), "sV")
    p.run("vector", lambda E: E.tensor_copy(p.wsmall_sb[:], st[:, 320:320 + 16 * 118]), "sV")
    p.run("vector", lambda E: E.tensor_scalar(out=p.negdtb[:], in0=st[0:64, 2208:2209],
                                              scalar1=-1.0, scalar2=None, op0=ALU.mult), "sV")
    p.wait("scalar", "sG", gv)
    expav = p.run("scalar", lambda E: E.activation(out=p.expA[:], in_=st[0:64, 2209:2210],
                                           func=AF.Exp, bias=0.0, scale=1.0), "sA")
    p.run("vector", lambda E: E.memset(p.ones_row[:], 1.0), "sV")
    p.run("vector", lambda E: E.memset(p.stackt[:, 112:113], 1.0), "sV")
    p.run("vector", lambda E: E.memset(p.h_sb[:], 0.0), "sV")
    p.run("vector", lambda E: E.memset(p.h_bf[:], 0.0), "sV")
    vfence = p.run("vector", lambda E: E.memset(p.hist[:], 0.0), "sV")

    p.wait("gpsimd", "sV", vfence)
    p.dma("gpsimd", st[0:112, 0:D], p.b_down)
    p.dma("gpsimd", p.csb[0:1, 0:1024], p.b_base[0:1, 0:1024])
    p.dma("gpsimd", p.y_tmp[0:1, 0:118], p.brow)
    p.wait("vector", "sG", p.v("sG"))
    p.run("vector", lambda E: E.tensor_copy(p.bdown_sb[0:112, :], st[0:112, 0:D]), "sV")
    bb0 = p.run("vector", lambda E: E.tensor_copy(p.zbuf[0:1, 0:1024], p.csb[0:1, 0:1024]), "sV")
    p.wait("gpsimd", "sV", bb0)
    p.dma("gpsimd", p.csb[0:1, 0:1024], p.b_base[0:1, 1024:2048])
    p.wait("vector", "sG", p.v("sG"))
    bb = p.run("vector", lambda E: E.tensor_copy(p.zbuf[0:1, 1024:2048], p.csb[0:1, 0:1024]), "sV")
    vfence2 = p.run("vector", lambda E: E.tensor_copy(p.brow_sb[:], p.y_tmp[0:1, 0:118]), "sV")
    p.wait("gpsimd", "sV", bb)
    rw = p.dma("gpsimd", p.rowsc, p.zbuf[0:1, 0:D])
    p.wait("gpsimd", "sG", rw)
    p.dma("gpsimd", p.bdown_sb[112:113, :], p.rowsc)
    p.wait("sync", "sV", vfence2)
    p.wait("gpsimd", "sV", vfence2)

    p.wait("vector", "sV", p.v("sV"))
    for fc in range(32):
        p.wait("vector", "sW", p.v("sW"))
        for j in range(4):
            p.run("vector", lambda E, fc=fc, j=j: E.tensor_tensor(
                out=p.prep_b[:, sl(j * 128, 128)], in0=p.eye128_b[:],
                in1=p.cwT[:, fc * 4 + j:fc * 4 + j + 1].broadcast_to([128, 128]),
                op=ALU.mult), "sV")
        vv = p.v("sV")
        p.wait("sync", "sV", vv)
        p.dma("sync", p.wsc_diag[fc][:], p.prep_b[:, 0:512])

    p.wait("vector", "sW", p.v("sW"))
    p.wait("sync", "sA", p.v("sA"))
    p.wait("sync", "sV", p.v("sV"))
    stF = [p.scrF[:, 0:2048], p.scrF[:, 2048:4096]]
    stB = [p.ygt[:, 0:2048], p.ygt[:, 2048:4096]]
    pI = ["pIA", "pIB"]
    pO = ["pOA", "pOB"]
    pieces = [(k, off, w) for (off, w) in [(8192, 192), (4096, 2048), (6144, 2048)]
              for k in range(NK)]
    kb._zjobs = [(k, off) for off in (0, 2048) for k in range(NK)]
    lastcast = {}
    for i, (k, off, w) in enumerate(pieces):
        pi = i % 2
        if i >= 2:
            p.wait("sync", "sV", lastcast[pi])
        iv = p.dma("sync", stF[pi][:, 0:w], p.w_in[sl(k * 128, 128), sl(off, w)], sem=pI[pi])
        p.wait("vector", pI[pi], iv)
        if i >= 2:
            p.wait("vector", pO[pi], p.v(pO[pi]))
        cv = p.run("vector", lambda E, pi=pi, w=w: E.tensor_copy(
            stB[pi][:, 0:w], stF[pi][:, 0:w]), "sV")
        lastcast[pi] = cv
        p.wait("gpsimd", "sV", cv)
        if off < 4096:
            dst = p.wsc_zp[sl(off // 512, 4), :, sl(k * 512, 512)].rearrange("z q c -> q z c")
            srcap = stB[pi][:, 0:2048].rearrange("q (z c) -> q z c", z=4)
        elif off < 8192:
            g0 = (off - 4096) // 256
            dst = p.wsc_inp[sl(g0, 8), :, sl(k * 256, 256)].rearrange("g q c -> q g c")
            srcap = stB[pi][:, 0:2048].rearrange("q (g c) -> q g c", g=8)
        else:
            dst = p.wsc_inp[16][:, sl(k * 192, 192)]
            srcap = stB[pi][:, 0:192]
        p.dma("gpsimd", dst, srcap, sem=pO[pi])
    p.wait("sync", "pOA", p.v("pOA"))
    p.wait("sync", "pOB", p.v("pOB"))

    p.barrier()
    kb.tap("wsc_probe", kb.wsc_inp[0][:, 0:512])


def ob_setup(kb: "KB"):
    kb._objobs = ([("out", k, h) for k in range(NKI) for h in range(4)] +
                  [("base", k, h) for k in range(NK) for h in range(4)])
    kb._obji = 0
    kb._oblastcast = {}


def emit_ob_jobs(kb: "KB", n):
    p = kb
    pI = ["p2IA", "p2IB"]
    pO = ["p2OA", "p2OB"]
    for _ in range(n):
        i = kb._obji
        if i >= len(kb._objobs):
            return
        kb._obji += 1
        kind, k, h = kb._objobs[i]
        pi = i % 2
        stF = p.obF[:, sl(pi * 512, 512)]
        stB = p.obB[:, sl(pi * 512, 512)]
        src = (p.w_out if kind == "out" else p.w_base)[sl(k * 128, 128), sl(h * 512, 512)]
        dst = (p.wsc_outp if kind == "out" else p.wsc_basep)[h][:, sl(k * 512, 512)]
        if i >= 2:
            p.wait("scalar", "sV", kb._oblastcast[pi])
        iv = p.run("scalar", lambda E, stF=stF, src=src: E.dma_start(
            out=stF, in_=src), pI[pi], 16)
        p.wait("vector", pI[pi], iv)
        if i >= 2:
            p.wait("vector", pO[pi], p.v(pO[pi]))
        if kind == "out":
            cv = p.run("vector", lambda E, stB=stB, stF=stF, k=k: E.tensor_scalar(
                out=stB, in0=stF, scalar1=p.ns_cols[:, k:k + 1], scalar2=None,
                op0=ALU.mult), "sV")
        else:
            cv = p.run("vector", lambda E, stB=stB, stF=stF: E.tensor_copy(stB, stF), "sV")
        kb._oblastcast[pi] = cv
        kb._prep2_v = cv
        p.wait("gpsimd", "sV", cv)
        p.dma("gpsimd", dst, stB, sem=pO[pi])
        if kb._obji == NKI * 4:
            kb._ob_out_gate = (p.v("p2OA"), p.v("p2OB"))
    if kb._obji == len(kb._objobs):
        kb._ob_all_gate = (p.v("p2IA"), p.v("p2IB"), p.v("p2OA"), p.v("p2OB"))


def emit_ln(kb: "KB", t, dst_slot, tap_prefix):
    p = kb
    for tsub in range(NCH):
        xs = p.xslot[:, sl(tsub * D, D)]
        p.selfsync("vector")
        p.selfsync("scalar")
        vs = p.run("vector", lambda E, xs=xs: E.tensor_reduce(
            out=p.sum_col[:], in_=xs, axis=AX.X, op=ALU.add), "sV")
        va = p.run("scalar", lambda E, xs=xs: E.activation(
            out=p.lns[:, 0:D], in_=xs, func=AF.Square, bias=0.0, scale=1.0,
            accum_out=p.ssq_col[:]), "sA")
        p.wait("vector", "sV", vs)
        vms = p.run("vector", lambda E: E.tensor_tensor(
            out=p.ms_col[:], in0=p.sum_col[:], in1=p.sum_col[:], op=ALU.mult), "sV")
        p.wait("vector", "sV", vms)
        p.wait("vector", "sA", va)
        vs2 = p.run("vector", lambda E: E.scalar_tensor_tensor(
            out=p.ms_col[:], in0=p.ms_col[:], scalar=-1.0 / D, in1=p.ssq_col[:],
            op0=ALU.mult, op1=ALU.add), "sV")
        p.wait("vector", "sV", vs2)
        vs2b = p.run("vector", lambda E: E.tensor_scalar(
            out=p.ms_col[:], in0=p.ms_col[:], scalar1=1.0 / D, scalar2=1e-5,
            op0=ALU.mult, op1=ALU.add), "sV")
        p.wait("vector", "sV", vs2b)
        vs2c = p.run("vector", lambda E: E.reciprocal(p.nmr_col[:], p.ms_col[:]), "sV")
        p.wait("scalar", "sV", vs2c)
        va2 = p.run("scalar", lambda E: E.activation(
            out=p.rstd_col[:], in_=p.nmr_col[:], func=AF.Sqrt, bias=0.0, scale=1.0), "sA")
        p.wait("vector", "sA", va2)
        vs3 = p.run("vector", lambda E: E.scalar_tensor_tensor(
            out=p.nmr_col[:], in0=p.sum_col[:], scalar=-1.0 / D, in1=p.rstd_col[:],
            op0=ALU.mult, op1=ALU.mult), "sV")
        p.wait("scalar", "sV", vs3)
        va3 = p.run("scalar", lambda E, xs=xs: E.activation(
            out=p.ntok[:], in_=xs,
            func=AF.Identity, bias=p.nmr_col[:], scale=p.rstd_col[:]), "sA")
        p.wait("tensor", "sA", va3)
        for kg in range(2):
            p.wait("tensor", kb._tr_free[0], kb._tr_free[1])
            vt_ = None
            for k8 in range(8):
                k = kg * 8 + k8
                vt_ = p.run("tensor", lambda E, k=k, k8=k8: E.transpose(
                    p.tr_ps[:, sl(k8 * 128, 128)], p.ntok[:, sl(k * 128, 128)],
                    p.eye128_b[:]), "sT" if k8 == 7 else None)
            p.wait("scalar", "sT", vt_)
            va4 = p.run("scalar", lambda E, kg=kg, tsub=tsub: E.activation(
                out=dst_slot.rearrange("q (k c) -> q k c", k=16)[:, sl(kg * 8, 8), sl(tsub * 128, 128)],
                in_=p.tr_ps[:].rearrange("q (k c) -> q k c", k=8),
                func=AF.Copy, bias=0.0, scale=1.0), "sA")
            kb._tr_free = ("sA", va4)


def _emit_zjob(kb):
    p = kb
    i = kb._zji
    kb._zji += 1
    k, off = kb._zjobs[i]
    pi = i % 2
    sIn = ("zIA", "zIB")[pi]
    sOut = ("zOA", "zOBB")[pi]
    stF = p.zcF[:, sl(pi * 2048, 2048)]
    stB = p.ygt[:, sl(pi * 2048, 2048)]
    if i >= 2:
        p.wait("scalar", "sV", kb._zlastcast[pi])
    iv = p.run("scalar", lambda E, stF=stF, k=k, off=off: E.dma_start(
        out=stF, in_=p.w_in[sl(k * 128, 128), sl(off, 2048)]), sIn, 16)
    p.wait("vector", sIn, iv)
    if i >= 2:
        p.wait("vector", sOut, p.v(sOut))
    cv = p.run("vector", lambda E, stB=stB, stF=stF: E.tensor_copy(stB, stF), "sV")
    kb._zlastcast[pi] = cv
    kb._zcast_v = cv
    p.wait("gpsimd", "sV", cv)
    p.dma("gpsimd", p.wsc_zp[sl(off // 512, 4), :, sl(k * 512, 512)].rearrange("z q c -> q z c"),
          stB.rearrange("q (z c) -> q z c", z=4), sem=sOut)


def emit_inproj_f(kb: "KB", t):
    p = kb
    if t == 1 and kb._ob_all_gate:
        p.wait("sync", "p2IA", kb._ob_all_gate[0])
        p.wait("sync", "p2IB", kb._ob_all_gate[1])
        p.wait("sync", "p2OA", kb._ob_all_gate[2])
        p.wait("sync", "p2OB", kb._ob_all_gate[3])
        p.wait("sync", "sV", kb._prep2_v)
    conv_done = {}
    vtr_done = {}
    evac_done = {}
    psum_evac = {}
    wbuf_free = p.gfree.setdefault("wbuf", {})
    dg_free = p.gfree.setdefault("dg", {})

    def stream_group(g):
        par = g % 2
        w = 256 if g < 16 else 192
        if par in wbuf_free:
            p.wait("sync", "sT", wbuf_free[par])
        sname = "wPA" if par == 0 else "wPB"
        p.dma("sync", p.wbuf[:, sl(par * 4096, 16 * w)], p.wsc_inp[g][:, 0:16 * w], sem=sname)
        return (sname, p.v(sname))

    def stream_diag(fc):
        par = fc % 2
        if par in dg_free:
            p.wait("sync", "sT", dg_free[par])
        sname = "dgA" if par == 0 else "dgB"
        p.dma("sync", p.dgbuf[:, sl(par * 512, 512)], p.wsc_diag[fc][:], sem=sname)
        return (sname, p.v(sname))

    wsem = {16: stream_group(16)}
    dgsem = {0: stream_diag(0), 1: stream_diag(1)}

    for g in [16] + list(range(16)):
        par = g % 2
        p.wait("tensor", *wsem[g])
        if g < 16:
            for fi in range(2):
                fc = 2 * g + fi
                psv = p.psA[:, sl(fi * 512, 512)]
                if fi in psum_evac:
                    p.wait("tensor", *psum_evac[fi])
                for k in range(NK):
                    base = par * 4096 + k * 256 + fi * 128
                    mmv = p.run("tensor", lambda E, base=base, psv=psv, k=k: E.matmul(
                        psv, p.wbuf[:, sl(base, 128)], p.normed()[:, sl(k * 512, 512)],
                        start=(k == 0), stop=(k == NK - 1)), "sT" if k == NK - 1 else None)
                if fi == 1:
                    wbuf_free[par] = mmv
                    if g + 2 <= 15:
                        wsem[g + 2] = stream_group(g + 2)
                if fc >= 4:
                    p.wait("scalar", "sT", conv_done[fc - 4])
                p.wait("scalar", "sT", mmv)
                ev = p.run("scalar", lambda E, fc=fc, psv=psv: E.activation(
                    out=p.xrr[:, sl((fc % 4) * 515 + 3, 512)], in_=psv,
                    func=AF.Copy, bias=0.0, scale=1.0), "sA")
                evac_done[fc] = ev
                psum_evac[fi] = ("sA", ev)
                if fc == 0:
                    p.wait("gpsimd", "sA", ev)
                    kb.tap("wbuf0", p.wbuf[:, 0:4096])
        else:
            for j in range(3):
                psv = p.psA[0:64, sl(j * 512, 512)]
                coff = j * 64
                for k in range(NK):
                    base = par * 4096 + k * 192 + coff
                    bcv = p.run("tensor", lambda E, base=base, psv=psv, k=k: E.matmul(
                        psv, p.wbuf[:, sl(base, 64)], p.normed()[:, sl(k * 512, 512)],
                        start=(k == 0), stop=(k == NK - 1)),
                        "sT" if (j == 2 and k == NK - 1) else None)
            wbuf_free[par] = bcv
            p.wait("vector", "sT", bcv)
            p.wait("scalar", "sT", bcv)
            for j, dst in [(0, p.bm_b), (1, p.cm_b)]:
                psv = p.psA[0:64, sl(j * 512, 512)]
                v1 = p.run("vector", lambda E, psv=psv: E.tensor_scalar(
                    out=p.lns[0:64, 0:512], in0=psv, scalar1=0.0, scalar2=None,
                    op0=ALU.min), "sV")
                p.wait("scalar", "sV", v1)
                a1 = p.run("scalar", lambda E: E.activation(
                    out=p.lns[0:64, 512:1024], in_=p.lns[0:64, 0:512],
                    func=AF.Exp, bias=0.0, scale=1.0), "sA")
                p.wait("vector", "sA", a1)
                sttv = p.run("vector", lambda E, psv=psv, dst=dst: E.scalar_tensor_tensor(
                    out=dst[:, 0:512], in0=psv, scalar=1.0, in1=p.lns[0:64, 512:1024],
                    op0=ALU.add, op1=ALU.max), "sV")
                psum_evac[j] = ("sV", sttv)
            psv = p.psA[0:64, sl(2 * 512, 512)]
            sgv = p.run("scalar", lambda E, psv=psv: E.activation(
                out=p.lns[0:64, 1024:1536], in_=psv,
                func=AF.Sigmoid, bias=p.negdtb[:], scale=-1.0), "sA")
            p.wait("scalar", "sA", sgv)
            a4 = p.run("scalar", lambda E: E.activation(
                out=p.lns[0:64, 1536:2048], in_=p.lns[0:64, 1024:1536],
                func=AF.Ln, bias=0.0, scale=1.0), "sA")
            p.wait("vector", "sA", a4)
            p.run("vector", lambda E: E.tensor_scalar(
                out=p.la[:, 0:512], in0=p.lns[0:64, 1536:2048],
                scalar1=p.expA[:], scalar2=None, op0=ALU.mult), "sV")
            kb._conv_evac[0] = sgv
            wsem[0] = stream_group(0)
            wsem[1] = stream_group(1)

        if g < 16:
            for fi in range(2):
                fc = 2 * g + fi
                par_d = fc % 2
                p.wait("vector", "sA", evac_done[fc])
                hv = p.run("vector", lambda E, fc=fc: E.tensor_copy(
                    p.xrr[:, sl((fc % 4) * 515, 3)], p.hist[:, sl(fc * 3, 3)]), "sV")
                p.wait("tensor", "sV", hv)
                p.wait("tensor", *dgsem[fc])
                cps = p.psA[:, sl(1024 + par_d * 512, 512)]
                p.wait("tensor", "sA", kb._conv_evac.get(par_d, 0))
                for j in range(4):
                    cv = p.run("tensor", lambda E, j=j, par_d=par_d, fc=fc, cps=cps: E.matmul(
                        cps, p.dgbuf[:, sl(par_d * 512 + j * 128, 128)],
                        p.xrr[:, (fc % 4) * 515 + j:(fc % 4) * 515 + j + 512],
                        start=(j == 0), stop=(j == 3)), "sT" if j == 3 else None)
                conv_done[fc] = cv
                dg_free[par_d] = cv
                if fc + 2 < 32:
                    dgsem[fc + 2] = stream_diag(fc + 2)
                p.wait("vector", "sV", hv)
                p.run("vector", lambda E, fc=fc: E.tensor_copy(
                    p.hist[:, sl(fc * 3, 3)], p.xrr[:, sl((fc % 4) * 515 + 3 + 509, 3)]), "sV")
                if fc >= 4:
                    p.wait("scalar", "sT", vtr_done[fc - 4])
                p.wait("scalar", "sT", cv)
                xe = p.run("scalar", lambda E, fc=fc, cps=cps: E.activation(
                    out=p.xcr[:, sl((fc % 4) * 512, 512)], in_=cps,
                    func=AF.Silu, bias=p.cb_cols[:, fc:fc + 1], scale=1.0), "sA")
                kb._conv_evac[par_d] = xe
                p.wait("tensor", kb._tr_free[0], kb._tr_free[1])
                p.wait("tensor", "sA", xe)
                for c in range(NCH):
                    tv = p.run("tensor", lambda E, fc=fc, c=c: E.transpose(
                        p.tr_ps[:, sl(c * 128, 128)],
                        p.xcr[:, sl((fc % 4) * 512 + c * 128, 128)], p.eye128_b[:]),
                        "sT" if c == NCH - 1 else None)
                vtr_done[fc] = tv
                p.wait("scalar", "sT", tv)
                ve = p.run("scalar", lambda E, fc=fc: E.activation(
                    out=p.vt[:].rearrange("q (c f) -> q c f", c=NCH)[:, :, sl(fc * 128, 128)],
                    in_=p.tr_ps[:, 0:512].rearrange("q (c f) -> q c f", c=NCH),
                    func=AF.Copy, bias=0.0, scale=1.0), "sA")
                kb._tr_free = ("sA", ve)
                if t == 0 and kb._zji < len(kb._zjobs):
                    _emit_zjob(kb)


def emit_inproj_z(kb: "KB", t):
    p = kb
    if t == 0:
        for sn in ("zOA", "zOBB"):
            p.wait("sync", sn, p.v(sn))
            p.wait("scalar", sn, p.v(sn))
    zfree = p.gfree.setdefault('zw', {})
    evs = {}

    def stream(zc, kg):
        par = (zc * 8 + kg) % 2
        if par in zfree:
            p.wait("sync", "sT", zfree[par])
        sname = "zwA" if par == 0 else "zwB"
        p.dma("sync", p.zwbuf[:, sl(par * 1024, 1024)], p.wsc_zp[zc][:, sl(kg * 1024, 1024)], sem=sname)
        return (sname, p.v(sname))

    sems = {(0, 0): stream(0, 0), (0, 1): stream(0, 1)}
    for zc in range(8):
        for kg in range(8):
            par = (zc * 8 + kg) % 2
            p.wait("tensor", *sems[(zc, kg)])
            nxt = (zc, kg + 2) if kg + 2 < 8 else (zc + 1, kg - 6)
            for tsub in range(NCH):
                psv = p.psA[:, sl(tsub * 512, 512)]
                if kg == 0 and tsub in evs:
                    p.wait("tensor", "sA", evs[tsub])
                for k4 in range(2):
                    k = kg * 2 + k4
                    mv = p.run("tensor", lambda E, k=k, k4=k4, kg=kg, par=par, tsub=tsub, psv=psv: E.matmul(
                        psv, p.normed()[:, sl(k * 512 + tsub * 128, 128)],
                        p.zwbuf[:, sl(par * 1024 + k4 * 512, 512)],
                        start=(kg == 0 and k4 == 0), stop=(kg == 7 and k4 == 1)),
                        "sT" if (tsub == NCH - 1 and k4 == 1) else None)
            zfree[par] = mv
            if nxt[0] < 8:
                sems[nxt] = stream(*nxt)
        p.wait("scalar", "sT", p.v("sT"))
        p.wait("scalar", "sSO", p.v("sSO"))
        for tsub in range(NCH):
            evs[tsub] = p.run("scalar", lambda E, tsub=tsub: E.activation(
                out=p.prep_b[:, sl(tsub * 512, 512)], in_=p.psA[:, sl(tsub * 512, 512)],
                func=AF.Silu, bias=0.0, scale=1.0), "sA")
        p.wait("gpsimd", "sA", p.v("sA"))
        p.dma("gpsimd", p.zsc[:, :, sl(zc * 512, 512)].rearrange("c q f -> q c f"),
              p.prep_b[:, 0:2048].rearrange("q (c f) -> q c f", c=4), sem="sSO")
    kb._zsc_done = p.v("sSO")


def emit_ssd_pre(kb: "KB", t, c):
    p = kb
    pc = c % 2
    if c == 0:
        p.wait("vector", "sSO", p.v("sSO"))
    la_sl = p.la[:, sl(c * S, S)]
    csh_sl = p.csh[:, sl(pc * S, S)]
    cst_sl = p.cst[:, sl(pc * NH, NH)]
    et_sl = p.et[:, sl(pc * NH, NH)]
    rt_sl = p.rt[:, sl(pc * NH, NH)]
    smt_sl = p.smt[:, sl(pc * S, S)]
    kcT = p.bm_b[:, sl(c * S, S)]
    qcT = p.cm_b[:, sl(c * S, S)]

    sv = p.run("vector", lambda E: E.tensor_tensor_scan(
        out=csh_sl, data0=la_sl, data1=la_sl, initial=0.0,
        op0=ALU.add, op1=ALU.bypass), "sV")
    p.wait("gpsimd", "sV", sv)
    p.wait("gpsimd", "sCB", kb._csflat_free[pc])
    kb._flv = p.dma("gpsimd", p.csflat[pc], csh_sl, sem="sFL")
    p.wait("tensor", "sV", kb._psS_free)
    p.wait("tensor", "sV", sv)
    trv = p.run("tensor", lambda E: E.transpose(
        p.psS[:, 0:64], csh_sl, p.eye64_f[:]), "sT")
    p.wait("vector", "sT", trv)
    cstv = p.run("vector", lambda E: E.tensor_copy(cst_sl, p.psS[:, 0:64]), "sV")
    p.wait("scalar", "sV", cstv)
    etv = p.run("scalar", lambda E: E.activation(
        out=et_sl, in_=cst_sl, func=AF.Exp, bias=0.0, scale=1.0), "sA")
    totcol = p.csh[:, pc * S + S - 1:pc * S + S]
    rhv = p.run("scalar", lambda E: E.activation(
        out=p.rh[:], in_=csh_sl, func=AF.Exp, bias=totcol, scale=-1.0), "sA")
    dcv = p.run("scalar", lambda E, pc=pc: E.activation(
        out=p.dcol[:, pc:pc + 1], in_=totcol, func=AF.Exp, bias=0.0, scale=1.0), "sA")
    p.wait("tensor", "sA", rhv)
    p.wait("tensor", kb._tr_free[0], kb._tr_free[1])
    rtv = p.run("tensor", lambda E: E.transpose(
        p.tr_ps[:, 0:64], p.rh[:], p.eye128_b[0:64, 0:64]), "sT")
    p.wait("vector", "sT", rtv)
    rtev = p.run("vector", lambda E: E.tensor_copy(rt_sl, p.tr_ps[:, 0:64]), "sV")
    p.wait("tensor", "sA", dcv)
    p.wait("tensor", "sV", cstv)
    dtv = p.run("tensor", lambda E, pc=pc: E.transpose(
        p.psS[0:64, 64:128], p.dcol[:, pc:pc + 1].broadcast_to([NH, 64]), p.eye64_f[:]), "sT")
    p.wait("vector", "sT", dtv)
    dtev = p.run("vector", lambda E, pc=pc: E.tensor_copy(
        p.dtile[:, sl(pc * NH, NH)], p.psS[0:64, 64:128]), "sV")
    p.wait("tensor", "sV", rtev)
    kctv = p.run("tensor", lambda E: E.transpose(
        p.tr_ps[:, 64:128], kcT, p.eye128_b[0:64, 0:64]), "sT")
    p.wait("vector", "sT", kctv)
    kcev = p.run("vector", lambda E, pc=pc: E.tensor_copy(
        p.kc_sb[:, sl(pc * NST, NST)], p.tr_ps[:, 64:128]), "sV")
    kb._tr_free = ("sV", kcev)
    p.wait("tensor", "sV", dtev)
    scv = p.run("tensor", lambda E: E.matmul(
        p.psS[:, 128:256], kcT, qcT, start=True, stop=True), "sT")
    p.wait("vector", "sT", scv)
    smv = p.run("vector", lambda E: E.tensor_tensor(
        out=smt_sl, in0=p.psS[:, 128:256], in1=p.mask01_b[:], op=ALU.mult), "sV")
    kb._pre[pc] = dict(cstv=cstv, etv=etv, smv=smv, rtev=rtev, dtev=dtev, kcev=kcev, scv=scv)
    kb._psS_free = smv


def _emit_dbuild(kb, t, c, e):
    p = kb
    pc = c % 2
    cst_sl = p.cst[:, sl(pc * NH, NH)]
    smt_sl = p.smt[:, sl(pc * S, S)]
    dslot = p.datt[:, sl((e % 2) * 1024, 1024)]
    if kb._csb_free:
        p.wait("gpsimd", "sV", kb._csb_free)
    p.wait("gpsimd", "sFL", kb._flv)
    cbv = p.dma("gpsimd", p.csb[:],
                p.csflat[pc].rearrange("h s -> (h s)")[sl(e * 1024, 1024)]
                .rearrange("(a f) -> a f", a=1).broadcast_to([128, 1024]), sem="sCB")
    kb._csflat_free[pc] = cbv
    p.wait("vector", "sCB", cbv)
    p.selfsync("vector")
    dsv = p.run("vector", lambda E, e=e: E.tensor_tensor(
        out=p.darg[:, 0:1024].rearrange("z (h s) -> z h s", h=8),
        in0=p.csb[:].rearrange("z (h s) -> z h s", h=8),
        in1=cst_sl[:, sl(e * 8, 8)].rearrange("z (h o) -> z h o", o=1).broadcast_to([128, 8, S]),
        op=ALU.subtract), "sV")
    kb._csb_free = dsv
    if kb._dslot_free.get(e % 2):
        p.wait("scalar", "sT", kb._dslot_free[e % 2])
    p.wait("scalar", "sV", dsv)
    absv = p.run("scalar", lambda E, dslot=dslot: E.activation(
        out=dslot, in_=p.darg[:, 0:1024], func=AF.Abs, bias=0.0, scale=1.0), "sA")
    p.wait("scalar", "sA", absv)
    expv = p.run("scalar", lambda E, dslot=dslot: E.activation(
        out=p.darg[:, 0:1024], in_=dslot, func=AF.Exp, bias=0.0, scale=-1.0), "sA")
    p.wait("vector", "sA", expv)
    p.wait("vector", "sV", kb._pre[pc]["smv"])
    attv = p.run("vector", lambda E, dslot=dslot: E.tensor_tensor(
        out=dslot.rearrange("z (h s) -> z h s", h=8),
        in0=p.darg[:, 0:1024].rearrange("z (h s) -> z h s", h=8),
        in1=smt_sl.rearrange("z (o s) -> z o s", o=1).broadcast_to([128, 8, S]),
        op=ALU.mult), "sV")
    kb._datt_ready[e % 2] = attv


def _emit_eighth(kb, t, c, e):
    p = kb
    pc = c % 2
    qcT = p.cm_b[:, sl(c * S, S)]
    vtc = p.vt[:, sl(c * DI, DI)]
    et_sl = p.et[:, sl(pc * NH, NH)]
    dslot = p.datt[:, sl((e % 2) * 1024, 1024)]

    p.selfsync("vector")
    p.wait("vector", "sV", kb._pre[pc]["rtev"])
    if kb._rv_free:
        p.wait("vector", "sT", kb._rv_free)
    rvv = p.run("vector", lambda E, e=e, pc=pc: E.tensor_tensor(
        out=p.rv[:].rearrange("z (h x) -> z h x", h=8),
        in0=vtc[:, sl(e * 512, 512)].rearrange("z (h x) -> z h x", h=8),
        in1=p.rt[:, sl(pc * NH + e * 8, 8)].rearrange("z (h o) -> z h o", o=1)
        .broadcast_to([128, 8, HD]), op=ALU.mult), "sV")
    vdv = p.run("vector", lambda E, e=e: E.tensor_tensor(
        out=p.vd[:].rearrange("z (h x) -> z h x", h=8),
        in0=vtc[:, sl(e * 512, 512)].rearrange("z (h x) -> z h x", h=8),
        in1=p.drep[:, sl(e * 8, 8)].rearrange("z (h o) -> z h o", o=1)
        .broadcast_to([128, 8, HD]), op=ALU.mult), "sV")

    if kb._yb_free:
        p.wait("tensor", "sV", kb._yb_free)
    iv = p.run("tensor", lambda E, e=e: E.matmul(
        p.psYb[:], qcT, p.h_bf[:, sl(e * 512, 512)], start=True, stop=True), "sT")
    p.wait("tensor", "sV", kb._datt_ready[e % 2])
    p.wait("tensor", "sV", vdv)
    if kb._ya_free:
        p.wait("tensor", "sV", kb._ya_free)
    p.run("tensor", lambda E: E.matmul(
        p.psYa[:], p.eye128_b[:], p.vd[:], start=True, stop=False), None)
    for i in range(8):
        av2 = p.run("tensor", lambda E, i=i, e=e, dslot=dslot: E.matmul(
            p.psYa[:, sl(i * 64, 64)], dslot[:, sl(i * 128, 128)],
            vtc[:, sl(e * 512 + i * 64, 64)],
            start=False, stop=(i == 7)), "sT" if i == 7 else None)
    kb._dslot_free[e % 2] = av2
    kb._rv_free = av2
    p.wait("tensor", "sV", kb._psS_free)
    p.wait("tensor", "sV", rvv)
    p.wait("tensor", "sV", kb._pre[pc]["kcev"])
    stv = p.run("tensor", lambda E, pc=pc: E.matmul(
        p.psS[0:64, 0:512], p.kc_sb[:, sl(pc * NST, NST)], p.rv[:],
        start=True, stop=True), "sT")
    kb._rv_free = stv
    p.wait("vector", "sT", av2)
    p.wait("vector", "sA", kb._pre[pc]["etv"])
    if kb._ysb_free:
        p.wait("vector", "sA", kb._ysb_free)
    c1 = p.run("vector", lambda E, e=e, pc=pc: E.tensor_tensor(
        out=p.y_tmp[:].rearrange("z (h x) -> z h x", h=8),
        in0=p.psYb.rearrange("z (h x) -> z h x", h=8),
        in1=et_sl[:, sl(e * 8, 8)].rearrange("z (h o) -> z h o", o=1)
        .broadcast_to([128, 8, HD]), op=ALU.mult), "sV")
    kb._yb_free = c1
    p.wait("vector", "sV", c1)
    c2 = p.run("vector", lambda E, e=e: E.tensor_tensor(
        out=p.y_sb[:, sl(e * 512, 512)], in0=p.y_tmp[:], in1=p.psYa, op=ALU.add), "sV")
    kb._ya_free = c2
    p.wait("vector", "sT", stv)
    p.wait("vector", "sV", kb._u1)
    u2 = p.run("vector", lambda E, e=e: E.tensor_tensor(
        out=p.h_sb[:, sl(e * 512, 512)], in0=p.h_sb[:, sl(e * 512, 512)],
        in1=p.psS[0:64, 0:512], op=ALU.add), "sV")
    kb._psS_free = u2
    p.wait("scalar", "sV", u2)
    p.wait("scalar", "sT", iv)
    p.run("scalar", lambda E, e=e: E.activation(
        out=p.h_bf[:, sl(e * 512, 512)], in_=p.h_sb[:, sl(e * 512, 512)],
        func=AF.Copy, bias=0.0, scale=1.0), "sA")
    kb._last_c2 = c2


def emit_ssd_main(kb: "KB", t, c):
    p = kb
    pc = c % 2
    p.selfsync("vector")
    p.wait("vector", "sA", p.v("sA"))
    p.wait("vector", "sV", kb._pre[pc]["dtev"])
    kb._u1 = p.run("vector", lambda E, pc=pc: E.tensor_tensor(
        out=p.h_sb[:].rearrange("n (h x) -> n h x", h=NH),
        in0=p.h_sb[:].rearrange("n (h x) -> n h x", h=NH),
        in1=p.dtile[:, sl(pc * NH, NH)].rearrange("n (h o) -> n h o", o=1)
        .broadcast_to([NST, NH, HD]), op=ALU.mult), "sV")
    _emit_dbuild(kb, t, c, 0)
    _emit_dbuild(kb, t, c, 1)
    for e in range(8):
        _emit_eighth(kb, t, c, e)
        if e + 2 < 8:
            _emit_dbuild(kb, t, c, e + 2)
    if t == 0 and c < 2:
        p.wait("scalar", "sV", kb._zcast_v)
        for sn in ("zIA", "zIB"):
            p.wait("scalar", sn, p.v(sn))
    p.wait("scalar", "sV", kb._last_c2)
    yslot = p.ybf[:, sl((c % 2) * DI, DI)]
    if kb._ybf_free.get(c % 2):
        p.wait("scalar", "sV", kb._ybf_free[c % 2])
    sqv = p.run("scalar", lambda E, yslot=yslot, c=c: E.activation(
        out=yslot, in_=p.y_sb[:], func=AF.Square, bias=0.0, scale=1.0,
        accum_out=p.ssc[:, c % 2:c % 2 + 1]), "sA")
    p.wait("scalar", "sA", sqv)
    cpv = p.run("scalar", lambda E, yslot=yslot: E.activation(
        out=yslot, in_=p.y_sb[:], func=AF.Copy, bias=0.0, scale=1.0), "sA")
    kb._ysb_free = cpv
    kb._ybf_ready[c % 2] = cpv


def emit_ssd_gate(kb: "KB", t, c):
    p = kb
    yslot = p.ybf[:, sl((c % 2) * DI, DI)]
    p.wait("gpsimd", "sV", kb._zbuf_free)
    p.wait("gpsimd", "sSO", kb._zsc_done)
    zv = p.dma("gpsimd", p.zbuf[:, 0:DI], p.zsc[c], sem="sZB")
    p.selfsync("vector")
    r1 = p.run("vector", lambda E, c=c: E.tensor_scalar(
        out=p.ms_col[:], in0=p.ssc[:, c % 2:c % 2 + 1], scalar1=1.0 / DI, scalar2=1e-6,
        op0=ALU.mult, op1=ALU.add), "sV")
    p.wait("vector", "sV", r1)
    r2 = p.run("vector", lambda E: E.reciprocal(p.ss_col[:], p.ms_col[:]), "sV")
    p.wait("scalar", "sV", r2)
    rsv = p.run("scalar", lambda E: E.activation(
        out=p.rstd2[:], in_=p.ss_col[:], func=AF.Sqrt, bias=0.0, scale=1.0), "sA")
    p.wait("vector", "sA", rsv)
    p.wait("vector", "sA", kb._ybf_ready[c % 2])
    p.wait("vector", "sZB", zv)
    if t == 0 and c == 0:
        for sn in ("zOA", "zOBB"):
            p.wait("vector", sn, p.v(sn))
    if kb._ygt_free:
        p.wait("vector", "sT", kb._ygt_free)
    gv = p.run("vector", lambda E, yslot=yslot: E.scalar_tensor_tensor(
        out=p.ygt[:], in0=yslot, scalar=p.rstd2[:], in1=p.zbuf[:, 0:DI],
        op0=ALU.mult, op1=ALU.mult), "sV")
    kb._zbuf_free = gv
    kb._ybf_free[c % 2] = gv
    p.wait("tensor", "sV", gv)
    for bg in range(4):
        p.wait("tensor", kb._tr_free[0], kb._tr_free[1])
        for f8 in range(8):
            fc = bg * 8 + f8
            tv = p.run("tensor", lambda E, fc=fc, f8=f8: E.transpose(
                p.tr_ps[:, sl(f8 * 128, 128)], p.ygt[:, sl(fc * 128, 128)], p.eye128_b[:]),
                "sT" if f8 == 7 else None)
        p.wait("scalar", "sT", tv)
        dst = (p.slotA if bg < 2 else p.slotB)
        ev = p.run("scalar", lambda E, dst=dst, bg=bg, c=c: E.activation(
            out=dst.rearrange("z (f x) -> z f x", f=16)[:, sl((bg % 2) * 8, 8), sl(c * 128, 128)],
            in_=p.tr_ps[:].rearrange("z (f x) -> z f x", f=8),
            func=AF.Copy, bias=0.0, scale=1.0), "sA")
        kb._tr_free = ("sA", ev)
    kb._ygt_free = p.v("sT")


def emit_outproj(kb: "KB", t):
    p = kb
    if t == 0 and kb._ob_out_gate:
        p.wait("sync", "p2OA", kb._ob_out_gate[0])
        p.wait("sync", "p2OB", kb._ob_out_gate[1])
    p.wait("gpsimd", "sT", p.v("sT"))
    p.wait("gpsimd", "sV", p.v("sV"))
    uld = p.dma("gpsimd", p.xslot[:].rearrange("z (a d_) -> z a d_", a=NCH),
                p.u[sl(t * TT, TT), :].rearrange("(a z) d_ -> z a d_", z=128), sem="sU")
    ofree = p.gfree.setdefault('ow', {})
    evs = {}

    def stream(cb, kg):
        par = (cb * 16 + kg) % 2
        if par in ofree:
            p.wait("sync", "sT", ofree[par])
        sname = "owA" if par == 0 else "owB"
        p.dma("sync", p.owbuf[:, sl(par * 1024, 1024)], p.wsc_outp[cb][:, sl(kg * 1024, 1024)], sem=sname)
        return (sname, p.v(sname))

    sems = {(0, 0): stream(0, 0), (0, 1): stream(0, 1)}
    for cb in range(4):
        for kg in range(16):
            par = (cb * 16 + kg) % 2
            p.wait("tensor", *sems[(cb, kg)])
            nxt = (cb, kg + 2) if kg + 2 < 16 else (cb + 1, kg + 2 - 16)
            for tsub in range(NCH):
                psv = p.psA[:, sl(tsub * 512, 512)]
                if kg == 0 and tsub in evs:
                    p.wait("tensor", "sV", evs[tsub])
                for k4 in range(2):
                    k = kg * 2 + k4
                    mv = p.run("tensor", lambda E, k=k, k4=k4, kg=kg, par=par, tsub=tsub, psv=psv: E.matmul(
                        psv, p.ygf_slice(k, tsub), p.owbuf[:, sl(par * 1024 + k4 * 512, 512)],
                        start=(kg == 0 and k4 == 0), stop=(kg == 15 and k4 == 1)),
                        "sT" if (tsub == NCH - 1 and k4 == 1) else None)
            ofree[par] = mv
            if nxt[0] < 4:
                sems[nxt] = stream(*nxt)
        p.wait("vector", "sT", p.v("sT"))
        p.wait("vector", "sU", uld)
        for tsub in range(NCH):
            dst = p.xslot[:, sl(tsub * D + cb * 512, 512)]
            evs[tsub] = p.run("vector", lambda E, dst=dst, tsub=tsub: E.tensor_tensor(
                out=dst, in0=p.psA[:, sl(tsub * 512, 512)], in1=dst, op=ALU.add), "sV")


def emit_ffn(kb: "KB", t):
    p = kb
    if t == 0 and kb._ob_all_gate:
        p.wait("sync", "p2OA", kb._ob_all_gate[2])
        p.wait("sync", "p2OB", kb._ob_all_gate[3])
    n2f = p.slotA
    for tsub in range(NCH):
        p.selfsync("vector")
        p.selfsync("scalar")
        psv = p.psB[:, 0:128]
        for k in range(NK):
            p.run("tensor", lambda E, k=k, tsub=tsub, psv=psv: E.matmul(
                psv[:, 0:118], p.n2f_slice(k, tsub), p.wsmall_sb[:, sl(k * 118, 118)],
                start=(k == 0), stop=False), None)
        smv = p.run("tensor", lambda E, psv=psv: E.matmul(
            psv[:, 0:118], p.ones_row[:], p.brow_sb[:], start=False, stop=True), "sT")
        p.wait("vector", "sT", smv)
        lg = psv[:, 16:22]
        v1 = p.run("vector", lambda E, lg=lg: E.reduce_max(
            out=p.m1c[:], in_=lg, axis=AX.X), "sV")
        p.wait("vector", "sV", v1)
        v2 = p.run("vector", lambda E: E.tensor_scalar(
            out=p.ms_col[:], in0=p.m1c[:], scalar1=-1.0, scalar2=None, op0=ALU.mult), "sV")
        p.wait("scalar", "sV", v2)
        a1 = p.run("scalar", lambda E, lg=lg: E.activation(
            out=p.gates[:], in_=lg, func=AF.Exp, bias=p.ms_col[:], scale=1.0,
            accum_out=p.sumg[:]), "sA")
        p.wait("vector", "sA", a1)
        v3 = p.run("vector", lambda E: E.reciprocal(p.rsg[:], p.sumg[:]), "sV")
        p.wait("vector", "sV", v3)
        v4 = p.run("vector", lambda E: E.tensor_scalar(
            out=p.gates[:], in0=p.gates[:], scalar1=p.rsg[:], scalar2=None, op0=ALU.mult), "sV")
        p.wait("vector", "sV", v4)
        v5 = p.run("vector", lambda E: E.reduce_max(
            out=p.m1c[:], in_=p.gates[:], axis=AX.X), "sV")
        p.wait("vector", "sV", v5)
        v6 = p.run("vector", lambda E: E.tensor_scalar(
            out=p.gm[:], in0=p.gates[:], scalar1=p.m1c[:], scalar2=-1e30,
            op0=ALU.is_ge, op1=ALU.mult), "sV")
        p.wait("vector", "sV", v6)
        v7 = p.run("vector", lambda E: E.tensor_tensor(
            out=p.gm[:], in0=p.gates[:], in1=p.gm[:], op=ALU.add), "sV")
        p.wait("vector", "sV", v7)
        v8 = p.run("vector", lambda E: E.reduce_max(
            out=p.m2c[:], in_=p.gm[:], axis=AX.X), "sV")
        p.wait("vector", "sV", v8)
        v9 = p.run("vector", lambda E: E.tensor_scalar(
            out=p.wpe[:], in0=p.gates[:], scalar1=p.m2c[:], scalar2=None, op0=ALU.is_ge), "sV")
        p.wait("vector", "sV", v9)
        v10 = p.run("vector", lambda E: E.tensor_tensor(
            out=p.wpe[:], in0=p.gates[:], in1=p.wpe[:], op=ALU.mult), "sV")
        p.wait("scalar", "sT", smv)
        a2 = p.run("scalar", lambda E, psv=psv: E.activation(
            out=p.ffn_small[:], in_=psv[:, 0:118], func=AF.Copy, bias=0.0, scale=1.0), "sA")
        p.wait("vector", "sA", a2)
        p.wait("vector", "sV", v10)
        s1 = p.run("vector", lambda E: E.tensor_tensor(
            out=p.stackt[:, 16:112].rearrange("z (e r) -> z e r", e=6),
            in0=p.ffn_small[:, 22:118].rearrange("z (e r) -> z e r", e=6),
            in1=p.wpe[:].rearrange("z (e o) -> z e o", o=1).broadcast_to([128, 6, 16]),
            op=ALU.mult), "sV")
        s2 = p.run("vector", lambda E: E.tensor_copy(
            p.stackt[:, 0:16], p.ffn_small[:, 0:16]), "sV")
        p.wait("tensor", "sV", s2)
        tvv = p.run("tensor", lambda E: E.transpose(
            p.tr_ps[0:113, 0:128], p.stackt[:], p.eye128_b[:]), "sT")
        p.wait("vector", "sT", tvv)
        sev = p.run("vector", lambda E, tsub=tsub: E.tensor_copy(
            p.stackT[:, sl(tsub * 128, 128)], p.tr_ps[0:113, 0:128]), "sV")
        p.wait("tensor", "sV", sev)

    bfree = p.gfree.setdefault('bw', {})
    evs = {}
    stv = {}

    def stream(cb, kg):
        par = (cb * 8 + kg) % 2
        if par in bfree:
            p.wait("sync", "sT", bfree[par])
        sname = "bwA" if par == 0 else "bwB"
        p.dma("sync", p.bwbuf[:, sl(par * 1024, 1024)], p.wsc_basep[cb][:, sl(kg * 1024, 1024)], sem=sname)
        return (sname, p.v(sname))

    sems = {(0, 0): stream(0, 0), (0, 1): stream(0, 1)}
    for cb in range(4):
        for kg in range(8):
            par = (cb * 8 + kg) % 2
            p.wait("tensor", *sems[(cb, kg)])
            nxt = (cb, kg + 2) if kg + 2 < 8 else (cb + 1, kg + 2 - 8)
            for tsub in range(NCH):
                psv = p.psA[:, sl(tsub * 512, 512)]
                if kg == 0 and tsub in evs:
                    p.wait("tensor", "sV", evs[tsub])
                for k4 in range(2):
                    k = kg * 2 + k4
                    mv = p.run("tensor", lambda E, k=k, k4=k4, kg=kg, par=par, tsub=tsub, psv=psv: E.matmul(
                        psv, p.n2f_slice(k, tsub), p.bwbuf[:, sl(par * 1024 + k4 * 512, 512)],
                        start=(kg == 0 and k4 == 0), stop=False),
                        "sT" if (tsub == NCH - 1 and k4 == 1) else None)
            bfree[par] = mv
            if nxt[0] < 4:
                sems[nxt] = stream(*nxt)
        for tsub in range(NCH):
            psv = p.psA[:, sl(tsub * 512, 512)]
            dv = p.run("tensor", lambda E, tsub=tsub, psv=psv, cb=cb: E.matmul(
                psv, p.stackT[:, sl(tsub * 128, 128)], p.bdown_sb[:, sl(cb * 512, 512)],
                start=False, stop=True), "sT" if tsub == NCH - 1 else None)
        p.wait("vector", "sT", dv)
        if cb > 0:
            p.wait("vector", "sSO", stv[cb - 1])
        for tsub in range(NCH):
            evs[tsub] = p.run("vector", lambda E, tsub=tsub, cb=cb: E.tensor_tensor(
                out=p.x2_sb[:, sl(tsub * 512, 512)], in0=p.psA[:, sl(tsub * 512, 512)],
                in1=p.xslot[:, sl(tsub * D + cb * 512, 512)], op=ALU.add), "sV")
        p.wait("gpsimd", "sV", max(evs.values()))
        for tsub in range(NCH):
            p.dma("gpsimd",
                  p.out[sl(t * TT + tsub * 128, 128), sl(cb * 512, 512)],
                  p.x2_sb[:, sl(tsub * 512, 512)], sem="sSO")
        stv[cb] = p.v("sSO")


def build_full(ntiles=NTILE, debug=()):
    kb = KB(debug=debug)
    p = kb
    emit_prep(kb)
    for t in range(ntiles):
        p.wait("gpsimd", "sV", p.v("sV"))
        p.wait("gpsimd", "sA", p.v("sA"))
        p.wait("gpsimd", "sT", p.v("sT"))
        uld = p.dma("gpsimd", p.xslot[:].rearrange("z (a d_) -> z a d_", a=NCH),
                    p.u[sl(t * TT, TT), :].rearrange("(a z) d_ -> z a d_", z=128), sem="sU")
        p.wait("vector", "sU", uld)
        p.wait("scalar", "sU", uld)
        emit_ln(kb, t, kb.slotA, "ln1")
        p.barrier()
        kb.tap("normed_t%d" % t, kb.slotA[:])
        emit_inproj_f(kb, t)
        p.barrier()
        kb.tap("vt_t%d" % t, kb.vt[:])
        kb.tap("bm_t%d" % t, kb.bm_b[:])
        kb.tap("cm_t%d" % t, kb.cm_b[:])
        kb.tap("la_t%d" % t, kb.la[:])
        if t == 0:
            ob_setup(kb)
        ob = (lambda n: emit_ob_jobs(kb, n)) if t == 0 else (lambda n: None)
        emit_ssd_pre(kb, t, 0)
        emit_ssd_pre(kb, t, 1)
        ob(12)
        emit_ssd_main(kb, t, 0)
        ob(24)
        emit_inproj_z(kb, t)
        ob(24)
        emit_ssd_pre(kb, t, 2)
        ob(12)
        emit_ssd_main(kb, t, 1)
        ob(24)
        emit_ssd_gate(kb, t, 0)
        ob(24)
        kb.tap("ysb_t%d_c%d" % (t, 0), kb.ybf[:, 0:DI])
        emit_ssd_pre(kb, t, 3)
        ob(12)
        emit_ssd_main(kb, t, 2)
        ob(24)
        emit_ssd_gate(kb, t, 1)
        ob(24)
        emit_ssd_main(kb, t, 3)
        ob(12)
        emit_ssd_gate(kb, t, 2)
        emit_ssd_gate(kb, t, 3)
        ob(999)
        p.barrier()
        emit_outproj(kb, t)
        p.barrier()
        kb.tap("x1_t%d" % t, kb.xslot[:])
        if t == 0:
            for sn in ("zOA", "zOBB"):
                p.wait("scalar", sn, p.v(sn))
        emit_ln(kb, t, kb.slotA, "ln2")
        p.barrier()
        emit_ffn(kb, t)
        p.barrier()
    return kb


def _make_in_map(inputs, core):
    f = np.ascontiguousarray
    u = f(np.asarray(inputs["L0_attn_u"][core], np.float32))
    w_small = np.concatenate([
        np.asarray(inputs["L0_ffn_A_s"], np.float32),
        np.asarray(inputs["L0_ffn_W_router"], np.float32),
        np.asarray(inputs["L0_ffn_A_r"], np.float32).transpose(1, 0, 2).reshape(2048, 96),
    ], axis=1)
    b_down = np.concatenate([
        np.asarray(inputs["L0_ffn_B_s"], np.float32),
        np.asarray(inputs["L0_ffn_B_r"], np.float32).reshape(96, 2048),
    ], axis=0)
    brow = np.zeros((1, 118), np.float32)
    brow[0, 16:22] = np.asarray(inputs["L0_ffn_b_router"], np.float32)
    mask01 = np.triu(np.ones((128, 128), np.float32))
    return dict(
        u=u,
        w_in=f(np.asarray(inputs["L0_attn_in_proj_w"], np.float32)),
        convw=f(np.asarray(inputs["L0_attn_conv_weight"], np.float32)
                .reshape(4, 32, 128).transpose(2, 1, 0).reshape(128, 128)),
        convb=f(np.asarray(inputs["L0_attn_conv_bias"], np.float32).reshape(32, 128).T),
        A_log=np.asarray(inputs["L0_attn_A_log"], np.float32).reshape(-1, 1),
        dt_bias=np.asarray(inputs["L0_attn_dt_bias"], np.float32).reshape(-1, 1),
        D_param=np.asarray(inputs["L0_attn_D_param"], np.float32).reshape(1, -1),
        norm_scale=f(np.asarray(inputs["L0_attn_norm_scale"], np.float32).reshape(32, 128).T),
        w_out=f(np.asarray(inputs["L0_attn_out_proj_w"], np.float32)),
        w_base=f(np.asarray(inputs["L0_ffn_W_base"], np.float32)),
        b_base=np.asarray(inputs["L0_ffn_b_base"], np.float32).reshape(1, -1),
        w_small=f(w_small),
        b_down=f(b_down),
        brow=brow,
        eye128=np.eye(128, dtype=np.float32),
        mask01=mask01,
    )


def kernel(**inputs):
    from concourse.bass_utils import run_bass_kernel_spmd
    kb = build_full(ntiles=NTILE)
    nc = kb.finalize()
    in_maps = [_make_in_map(inputs, c) for c in range(8)]
    res = run_bass_kernel_spmd(nc, in_maps, core_ids=list(range(8)))
    out = np.stack([np.asarray(r["out"], np.float32) for r in res.results], axis=0)
    return out


# version 12 (final)
# speedup vs baseline: 1.0305x; 1.0198x over previous
import numpy as np
import concourse.bass as bass
import concourse.mybir as mybir

F32 = mybir.dt.float32
BF16 = mybir.dt.bfloat16
AF = mybir.ActivationFunctionType
ALU = mybir.AluOpType
AX = mybir.AxisListType

L, D, DI, NH, HD, NST = 4096, 2048, 4096, 64, 64, 64
TT, S = 512, 128
NTILE, NCH = L // TT, TT // S
DIP = 8384
NK = D // 128
NKI = DI // 128
ENGS = ("sync", "vector", "scalar", "tensor", "gpsimd")


def sl(off, n):
    return slice(off, off + n)


class Sem:
    __slots__ = ("h", "val")

    def __init__(self, nc):
        self.h = nc.alloc_semaphore()
        self.val = 0


class KB:
    def __init__(self, debug=()):
        nc = self.nc = bass.Bass("TRN2", target_bir_lowering=False)
        self.streams = {e: [] for e in ENGS}
        self.debug = dict(debug)
        self.dbg_t = {}
        self.dbg_done = set()

        din = lambda n, sh: nc.dram_tensor(n, sh, F32, kind="ExternalInput").ap()
        self.u = din("u", [L, D])
        self.w_in = din("w_in", [D, DIP])
        self.convw = din("convw", [128, 128])
        self.convb = din("convb", [128, 32])
        self.A_log = din("A_log", [NH, 1])
        self.dt_bias = din("dt_bias", [NH, 1])
        self.D_param = din("D_param", [1, NH])
        self.norm_scale = din("norm_scale", [128, 32])
        self.w_out = din("w_out", [DI, D])
        self.w_base = din("w_base", [D, D])
        self.b_base = din("b_base", [1, D])
        self.w_small = din("w_small", [D, 118])
        self.b_down = din("b_down", [112, D])
        self.brow = din("brow", [1, 118])
        self.eye128 = din("eye128", [128, 128])
        self.mask01 = din("mask01", [128, 128])
        self.out = nc.dram_tensor("out", [L, D], F32, kind="ExternalOutput").ap()
        for name, (shape, dt_) in self.debug.items():
            self.dbg_t[name] = nc.dram_tensor("dbg_" + name, shape, dt_, kind="ExternalOutput").ap()

        dint = lambda n, sh, dt_: nc.dram_tensor(n, sh, dt_, kind="Internal").ap()
        self.wsc_inp = dint("wsc_inp", [17, 128, 4096], BF16)
        self.wsc_zp = dint("wsc_zp", [8, 128, 8192], BF16)
        self.wsc_outp = dint("wsc_outp", [4, 128, NKI * 512], BF16)
        self.wsc_basep = dint("wsc_basep", [4, 128, NK * 512], BF16)
        self.wsc_diag = dint("wsc_diag", [32, 128, 512], BF16)
        self.zsc = dint("zsc", [NCH, 128, DI], BF16)
        self.csflat = dint("csflat", [2, NH, S], F32)
        self.rowsc = dint("rowsc", [1, D], BF16)

        DTSZ = {F32: 4, BF16: 2}
        ARENA = 212832
        self.arena = nc.alloc_sbuf_tensor("arena", [128, ARENA], mybir.dt.uint8)
        self._abase = nc.lookup_mloc(self.arena).addr
        self._cursor = 0
        self._mstart = {}
        self._table = []

        def A(n, sh, dt_, overlay=None):
            nbytes = sh[-1] * DTSZ[dt_]
            for d in sh[1:-1]:
                nbytes *= d
            nbytes = (nbytes + 31) // 32 * 32
            if overlay is not None:
                start = overlay
            else:
                start = self._cursor
                self._cursor += nbytes
            assert start + nbytes <= ARENA, f"SBUF arena overflow at {n}: {start + nbytes}"
            self._table.append((n, start, nbytes, overlay is not None))
            h = nc.virtual_sbuf_tensor(n, sh, dt_)
            nc._tensor(h.name, h.shape, h.dtype, type="SB")
            mloc = nc.lookup_mloc(h)
            mloc.allocated = True
            mloc.addr = self._abase + start
            ap = h.ap()
            self._mstart[n] = start
            return ap

        self.eye128_b = A("eye128_b", [128, 128], BF16)
        self.eye64_f = A("eye64_f", [64, 64], F32)
        self.mask01_b = A("mask01_b", [128, 128], BF16)
        self.ns_cols = A("ns_cols", [128, NKI], F32)
        self.cb_cols = A("cb_cols", [128, 32], F32)
        self.negdtb = A("negdtb", [NH, 1], F32)
        self.expA = A("expA", [NH, 1], F32)
        self.drep = A("drep", [128, NH], BF16)
        self.wsmall_sb = A("wsmall_sb", [128, NK * 118], BF16)
        self.bdown_sb = A("bdown_sb", [113, D], BF16)
        self.brow_sb = A("brow_sb", [1, 118], BF16)
        self.ones_row = A("ones_row", [1, 128], BF16)
        self.cwT = A("cwT", [128, 128], F32)

        self.vt = A("vt", [128, NCH * DI], BF16)
        self.xslot = A("xslot", [128, NCH * D], F32, overlay=self._mstart["vt"])
        self.slotA = A("slotA", [128, 8192], BF16)
        self.slotB = A("slotB", [128, 8192], BF16)
        self.zbuf = None
        self.xrr = A("xrr", [128, 6 * 515], BF16)
        self.xcr = A("xcr", [128, 4 * 512], BF16)
        self.zbuf = A("zbuf", [128, DI], BF16, overlay=self._mstart["xrr"])
        self.darg = A("darg", [128, 2048], BF16)
        self.datt = A("datt", [128, 2 * 1024], BF16)
        self.csb = A("csb", [128, 1024], F32)
        self.ygt = A("ygt", [128, DI], BF16)
        self.ybf = A("ybf", [128, 2 * DI], BF16)
        self.prep_b = self.ygt[:, 0:2048]
        self.ntok = self.ygt[:, 2048:4096]
        self.rv = A("rv", [128, 512], BF16)
        self.vd = A("vd", [128, 512], BF16)
        self.y_tmp = A("y_tmp", [128, 512], F32)
        self.scrF = A("scrF", [128, DI], F32)
        self.y_sb = self.scrF[:, 0:DI]
        self.lns = self.scrF[:, 0:D]
        self.x2_sb = self.scrF[:, D:2 * D]
        self.prep_f = self.scrF[:, 0:DI]
        self.h_sb = A("h_sb", [NST, DI], F32)
        self.h_bf = A("h_bf", [NST, DI], BF16)

        self.wbuf = A("wbuf", [128, 2 * NK * 256], BF16)
        self.zwbuf = A("zwbuf", [128, 2 * 2 * 512], BF16)
        self.owbuf = A("owbuf", [128, 2 * 2 * 512], BF16)
        self.bwbuf = A("bwbuf", [128, 2 * 2 * 512], BF16)
        self.dgbuf = A("dgbuf", [128, 2 * 4 * 128], BF16)

        self.smt = A("smt", [128, 2 * S], BF16)
        self.bm_b = A("bm_b", [64, TT], BF16)
        self.cm_b = A("cm_b", [64, TT], BF16)
        self.rh = A("rh", [NH, S], BF16)
        self.rt = A("rt", [128, 2 * NH], BF16)
        self.kc_sb = A("kc_sb", [128, 2 * NST], BF16)
        self.hist = A("hist", [128, 32 * 3], BF16)
        self.stackt = A("stackt", [128, 113], BF16)
        self.stackT = A("stackT", [113, TT], BF16)
        self.ffn_small = A("ffn_small", [128, 118], BF16)
        self.csh = A("csh", [NH, 2 * S], F32)
        self.cst = A("cst", [128, 2 * NH], F32)
        self.et = A("et", [128, 2 * NH], F32)
        self.la = A("la", [NH, TT], F32)
        self.dtile = A("dtile", [NST, 2 * NH], F32)
        self.dcol = A("dcol", [NH, 2], F32)
        self.gates = A("gates", [128, 6], F32)
        self.gm = A("gm", [128, 6], F32)
        self.wpe = A("wpe", [128, 6], F32)
        cols = lambda n: A(n, [128, 1], F32)
        self.sum_col, self.ssq_col, self.ms_col = cols("sum_col"), cols("ssq_col"), cols("ms_col")
        self.rstd_col, self.nmr_col = cols("rstd_col"), cols("nmr_col")
        self.ss_col, self.rstd2 = cols("ss_col"), cols("rstd2")
        self.ssc = A("ssc", [128, 2], F32)
        m4 = A("m4cols", [128, 4], F32)
        self.m1c, self.m2c, self.sumg, self.rsg = m4[:, 0:1], m4[:, 1:2], m4[:, 2:3], m4[:, 3:4]

        Pp = lambda n, sh, dt_: nc.alloc_psum_tensor(n, sh, dt_).ap()
        self.psA = Pp("psA", [128, 2048], F32)
        self.psYa = Pp("psYa", [128, 512], F32)
        self.psYb = Pp("psYb", [128, 512], F32)
        self.psS = Pp("psS", [128, 512], F32)
        self.tr_ps = Pp("tr_ps", [128, 1024], BF16)
        self.ms_ps = self.psS
        self.psB = self.psS

        self.S_ = {n: Sem(nc) for n in ("sW", "sG", "sV", "sA", "sT")}
        self.DMA_SEMS = {"sW", "sG"}
        self.gfree = {}
        self._zbuf_free = 0
        self._csb_free = 0
        self._csflat_free = {0: 0, 1: 0}
        self._zsc_done = 0
        self._prep2_v = 0
        self._prep2_o = 0
        self._psS_free = 0
        self._tr_free = ("sV", 0)
        self._dslot_free = {}
        self._datt_ready = {}
        self._rv_free = 0
        self._ya_free = 0
        self._yb_free = 0
        self._ysb_free = 0
        self._ybf_free = {}
        self._ybf_ready = {}
        self._ygt_free = 0
        self._u1 = 0
        self._last_c2 = 0
        self._flv = 0
        self._pre = {0: {}, 1: {}}
        self._conv_evac = {}

    def normed(self):
        return self.slotA

    def n2f_slice(self, k, tsub):
        return self.slotA[:, sl(k * 512 + tsub * 128, 128)]

    def ygf_slice(self, k, tsub):
        base = k * 512 + tsub * 128
        return (self.slotA if k < 16 else self.slotB)[:, sl(base - (8192 if k >= 16 else 0), 128)]

    def emit(self, eng, fn):
        self.streams[eng].append(fn)

    def sem(self, name):
        if name not in self.S_:
            self.S_[name] = Sem(self.nc)
        return self.S_[name]

    def wait(self, eng, semname, val):
        s = self.sem(semname)
        self.emit(eng, lambda E, h=s.h, v=val: E.wait_ge(h, v))

    def run(self, eng, fn, inc=None, n=1):
        if inc is not None:
            if n == 16:
                self.DMA_SEMS.add(inc)
            s = self.sem(inc)
            s.val += n
            self.emit(eng, lambda E, fn=fn, h=s.h, n=n: fn(E).then_inc(h, n))
            return s.val
        self.emit(eng, lambda E, fn=fn: fn(E))
        return None

    def dma(self, eng, out, in_, slow=False, sem=None):
        inc = sem or {"sync": "sW", "gpsimd": "sG"}[eng]
        self.DMA_SEMS.add(inc)
        return self.run(eng, lambda E, o=out, i=in_, s=slow: E.dma_start(
            out=o, in_=i, allow_slow_non_contiguous=s), inc, 16)

    def v(self, name):
        return self.sem(name).val

    def selfsync(self, eng):
        em = {"vector": "sV", "scalar": "sA", "tensor": "sT"}[eng]
        self.wait(eng, em, self.S_[em].val)

    def barrier(self, engines=("vector", "scalar", "tensor"), dma=(), full=False):
        em = {"vector": "sV", "scalar": "sA", "tensor": "sT"}
        if full:
            for e in engines:
                self.run(e, lambda E: E.drain(), em[e], 1)
        dma = tuple(self.DMA_SEMS) if full else tuple(dma)
        vals = {n: self.v(n) for n in list(em.values()) + list(dma)}
        targets = [em[e] for e in engines] + list(dma)
        waiters = engines + (("sync", "gpsimd") if full else ())
        for e in waiters:
            for sn in targets:
                self.wait(e, sn, vals[sn])

    def tap(self, name, src_ap):
        if name in self.dbg_t and name not in self.dbg_done:
            self.dbg_done.add(name)
            self.barrier(full=True)
            self.dma("gpsimd", self.dbg_t[name], src_ap, sem="sSO")
            self.barrier(full=True)

    def finalize(self):
        for sn in sorted(self.DMA_SEMS):
            self.wait("gpsimd", sn, self.v(sn))
        nc = self.nc
        with nc.Block() as block:
            for eng in ENGS:
                stream = self.streams[eng]
                dec = getattr(block, eng)

                def make(stream):
                    def body(E):
                        for fn in stream:
                            fn(E)
                    return body
                dec(make(stream))
        return nc



def emit_prep(kb: "KB"):
    p = kb
    st = p.scrF
    p.dma("gpsimd", st[:, 0:128], p.eye128)
    p.dma("gpsimd", p.eye64_f, p.eye128[0:64, 0:64])
    p.dma("gpsimd", st[:, 128:256], p.mask01)
    p.dma("gpsimd", p.ns_cols, p.norm_scale)
    p.dma("gpsimd", p.cb_cols, p.convb)
    p.dma("gpsimd", st[0:64, 2208:2209], p.dt_bias)
    p.dma("gpsimd", st[0:64, 2209:2210], p.A_log)
    p.dma("gpsimd", st[:, 256:320],
          p.D_param.rearrange("a (o h) -> a o h", o=1).broadcast_to([1, 128, NH]))
    p.dma("gpsimd", st[:, 320:320 + 16 * 118].rearrange("q (k c) -> q k c", k=16),
          p.w_small.rearrange("(k q) c -> q k c", q=128))
    p.dma("gpsimd", p.cwT, p.convw)
    gv = p.v("sG")
    p.wait("vector", "sG", gv)
    p.run("vector", lambda E: E.tensor_copy(p.eye128_b[:], st[:, 0:128]), "sV")
    p.run("vector", lambda E: E.tensor_copy(p.mask01_b[:], st[:, 128:256]), "sV")
    p.run("vector", lambda E: E.tensor_copy(p.drep[:], st[:, 256:320]), "sV")
    p.run("vector", lambda E: E.tensor_copy(p.wsmall_sb[:], st[:, 320:320 + 16 * 118]), "sV")
    p.run("vector", lambda E: E.tensor_scalar(out=p.negdtb[:], in0=st[0:64, 2208:2209],
                                              scalar1=-1.0, scalar2=None, op0=ALU.mult), "sV")
    p.wait("scalar", "sG", gv)
    expav = p.run("scalar", lambda E: E.activation(out=p.expA[:], in_=st[0:64, 2209:2210],
                                           func=AF.Exp, bias=0.0, scale=1.0), "sA")
    p.run("vector", lambda E: E.memset(p.ones_row[:], 1.0), "sV")
    p.run("vector", lambda E: E.memset(p.stackt[:, 112:113], 1.0), "sV")
    p.run("vector", lambda E: E.memset(p.h_sb[:], 0.0), "sV")
    p.run("vector", lambda E: E.memset(p.h_bf[:], 0.0), "sV")
    vfence = p.run("vector", lambda E: E.memset(p.hist[:], 0.0), "sV")

    p.wait("gpsimd", "sV", vfence)
    p.dma("gpsimd", st[0:112, 0:D], p.b_down)
    p.dma("gpsimd", p.csb[0:1, 0:1024], p.b_base[0:1, 0:1024])
    p.dma("gpsimd", p.y_tmp[0:1, 0:118], p.brow)
    p.wait("vector", "sG", p.v("sG"))
    p.run("vector", lambda E: E.tensor_copy(p.bdown_sb[0:112, :], st[0:112, 0:D]), "sV")
    bb0 = p.run("vector", lambda E: E.tensor_copy(p.zbuf[0:1, 0:1024], p.csb[0:1, 0:1024]), "sV")
    p.wait("gpsimd", "sV", bb0)
    p.dma("gpsimd", p.csb[0:1, 0:1024], p.b_base[0:1, 1024:2048])
    p.wait("vector", "sG", p.v("sG"))
    bb = p.run("vector", lambda E: E.tensor_copy(p.zbuf[0:1, 1024:2048], p.csb[0:1, 0:1024]), "sV")
    vfence2 = p.run("vector", lambda E: E.tensor_copy(p.brow_sb[:], p.y_tmp[0:1, 0:118]), "sV")
    p.wait("gpsimd", "sV", bb)
    rw = p.dma("gpsimd", p.rowsc, p.zbuf[0:1, 0:D])
    p.wait("gpsimd", "sG", rw)
    p.dma("gpsimd", p.bdown_sb[112:113, :], p.rowsc)
    p.wait("sync", "sV", vfence2)
    p.wait("gpsimd", "sV", vfence2)

    p.wait("vector", "sV", p.v("sV"))
    for fc in range(32):
        p.wait("vector", "sW", p.v("sW"))
        for j in range(4):
            p.run("vector", lambda E, fc=fc, j=j: E.tensor_tensor(
                out=p.prep_b[:, sl(j * 128, 128)], in0=p.eye128_b[:],
                in1=p.cwT[:, fc * 4 + j:fc * 4 + j + 1].broadcast_to([128, 128]),
                op=ALU.mult), "sV")
        vv = p.v("sV")
        p.wait("sync", "sV", vv)
        p.dma("sync", p.wsc_diag[fc][:], p.prep_b[:, 0:512])

    p.wait("vector", "sW", p.v("sW"))
    p.wait("sync", "sA", p.v("sA"))
    p.wait("sync", "sV", p.v("sV"))
    stF = [p.scrF[:, 0:2048], p.scrF[:, 2048:4096]]
    stB = [p.ygt[:, 0:2048], p.ygt[:, 2048:4096]]
    pI = ["pIA", "pIB"]
    pO = ["pOA", "pOB"]
    pieces = [(k, off, w) for k in range(NK)
              for (off, w) in [(0, 2048), (2048, 2048), (4096, 2048), (6144, 2048), (8192, 192)]]
    lastcast = {}
    for i, (k, off, w) in enumerate(pieces):
        pi = i % 2
        if i >= 2:
            p.wait("sync", "sV", lastcast[pi])
        iv = p.dma("sync", stF[pi][:, 0:w], p.w_in[sl(k * 128, 128), sl(off, w)], sem=pI[pi])
        p.wait("vector", pI[pi], iv)
        if i >= 2:
            p.wait("vector", pO[pi], p.v(pO[pi]))
        cv = p.run("vector", lambda E, pi=pi, w=w: E.tensor_copy(
            stB[pi][:, 0:w], stF[pi][:, 0:w]), "sV")
        lastcast[pi] = cv
        p.wait("gpsimd", "sV", cv)
        if off < 4096:
            dst = p.wsc_zp[sl(off // 512, 4), :, sl(k * 512, 512)].rearrange("z q c -> q z c")
            srcap = stB[pi][:, 0:2048].rearrange("q (z c) -> q z c", z=4)
        elif off < 8192:
            g0 = (off - 4096) // 256
            dst = p.wsc_inp[sl(g0, 8), :, sl(k * 256, 256)].rearrange("g q c -> q g c")
            srcap = stB[pi][:, 0:2048].rearrange("q (g c) -> q g c", g=8)
        else:
            dst = p.wsc_inp[16][:, sl(k * 192, 192)]
            srcap = stB[pi][:, 0:192]
        p.dma("gpsimd", dst, srcap, sem=pO[pi])
    p.wait("sync", "pOA", p.v("pOA"))
    p.wait("sync", "pOB", p.v("pOB"))

    p.barrier()
    kb.tap("wsc_probe", kb.wsc_zp[0][:, 0:512])


def emit_prep_outbase(kb: "KB"):
    p = kb
    stF = [p.csb[:, 0:512], p.csb[:, 512:1024]]
    stB = [p.datt[:, 0:512], p.datt[:, 512:1024]]
    pI = ["p2IA", "p2IB"]
    pO = ["p2OA", "p2OB"]
    jobs = []
    for k in range(NKI):
        for h in range(4):
            jobs.append(("out", k, h))
    for k in range(NK):
        for h in range(4):
            jobs.append(("base", k, h))
    lastcast = {}
    for i, (kind, k, h) in enumerate(jobs):
        pi = i % 2
        src = (p.w_out if kind == "out" else p.w_base)[sl(k * 128, 128), sl(h * 512, 512)]
        dst = (p.wsc_outp if kind == "out" else p.wsc_basep)[h][:, sl(k * 512, 512)]
        if i >= 2:
            p.wait("sync", "sV", lastcast[pi])
        iv = p.dma("sync", stF[pi][:, 0:512], src, sem=pI[pi])
        p.wait("vector", pI[pi], iv)
        if i >= 2:
            p.wait("vector", pO[pi], p.v(pO[pi]))
        if kind == "out":
            cv = p.run("vector", lambda E, pi=pi, k=k: E.tensor_scalar(
                out=stB[pi][:, 0:512], in0=stF[pi][:, 0:512],
                scalar1=p.ns_cols[:, k:k + 1], scalar2=None, op0=ALU.mult), "sV")
        else:
            cv = p.run("vector", lambda E, pi=pi: E.tensor_copy(
                stB[pi][:, 0:512], stF[pi][:, 0:512]), "sV")
        lastcast[pi] = cv
        p.wait("gpsimd", "sV", cv)
        p.dma("gpsimd", dst, stB[pi][:, 0:512], sem=pO[pi])
    kb._prep2_v = max(lastcast.values())
    kb._prep2_o = p.v("p2OA")
    kb._prep2_o2 = p.v("p2OB")
    kb._prep2_i = (p.v("p2IA"), p.v("p2IB"))
    p.wait("sync", "p2OA", kb._prep2_o)
    p.wait("sync", "p2OB", kb._prep2_o2)


def emit_ln(kb: "KB", t, dst_slot, tap_prefix):
    p = kb
    for tsub in range(NCH):
        xs = p.xslot[:, sl(tsub * D, D)]
        p.selfsync("vector")
        p.selfsync("scalar")
        vs = p.run("vector", lambda E, xs=xs: E.tensor_reduce(
            out=p.sum_col[:], in_=xs, axis=AX.X, op=ALU.add), "sV")
        va = p.run("scalar", lambda E, xs=xs: E.activation(
            out=p.lns[:, 0:D], in_=xs, func=AF.Square, bias=0.0, scale=1.0,
            accum_out=p.ssq_col[:]), "sA")
        p.wait("vector", "sV", vs)
        vms = p.run("vector", lambda E: E.tensor_tensor(
            out=p.ms_col[:], in0=p.sum_col[:], in1=p.sum_col[:], op=ALU.mult), "sV")
        p.wait("vector", "sV", vms)
        p.wait("vector", "sA", va)
        vs2 = p.run("vector", lambda E: E.scalar_tensor_tensor(
            out=p.ms_col[:], in0=p.ms_col[:], scalar=-1.0 / D, in1=p.ssq_col[:],
            op0=ALU.mult, op1=ALU.add), "sV")
        p.wait("vector", "sV", vs2)
        vs2b = p.run("vector", lambda E: E.tensor_scalar(
            out=p.ms_col[:], in0=p.ms_col[:], scalar1=1.0 / D, scalar2=1e-5,
            op0=ALU.mult, op1=ALU.add), "sV")
        p.wait("vector", "sV", vs2b)
        vs2c = p.run("vector", lambda E: E.reciprocal(p.nmr_col[:], p.ms_col[:]), "sV")
        p.wait("scalar", "sV", vs2c)
        va2 = p.run("scalar", lambda E: E.activation(
            out=p.rstd_col[:], in_=p.nmr_col[:], func=AF.Sqrt, bias=0.0, scale=1.0), "sA")
        p.wait("vector", "sA", va2)
        vs3 = p.run("vector", lambda E: E.scalar_tensor_tensor(
            out=p.nmr_col[:], in0=p.sum_col[:], scalar=-1.0 / D, in1=p.rstd_col[:],
            op0=ALU.mult, op1=ALU.mult), "sV")
        p.wait("scalar", "sV", vs3)
        va3h = {}
        for kg in range(2):
            va3h[kg] = p.run("scalar", lambda E, xs=xs, kg=kg: E.activation(
                out=p.ntok[:, sl(kg * 1024, 1024)], in_=xs[:, sl(kg * 1024, 1024)],
                func=AF.Identity, bias=p.nmr_col[:], scale=p.rstd_col[:]), "sA")
        for kg in range(2):
            p.wait("tensor", "sA", va3h[kg])
            p.wait("tensor", kb._tr_free[0], kb._tr_free[1])
            vt_ = None
            for k8 in range(8):
                k = kg * 8 + k8
                vt_ = p.run("tensor", lambda E, k=k, k8=k8: E.transpose(
                    p.tr_ps[:, sl(k8 * 128, 128)], p.ntok[:, sl(k * 128, 128)],
                    p.eye128_b[:]), "sT" if k8 == 7 else None)
            p.wait("scalar", "sT", vt_)
            va4 = p.run("scalar", lambda E, kg=kg, tsub=tsub: E.activation(
                out=dst_slot.rearrange("q (k c) -> q k c", k=16)[:, sl(kg * 8, 8), sl(tsub * 128, 128)],
                in_=p.tr_ps[:].rearrange("q (k c) -> q k c", k=8),
                func=AF.Copy, bias=0.0, scale=1.0), "sA")
            kb._tr_free = ("sA", va4)


def emit_inproj_f(kb: "KB", t):
    p = kb
    conv_done = {}
    vtr_done = {}
    evac_done = {}
    psum_evac = {}
    wbuf_free = p.gfree.setdefault("wbuf", {})
    dg_free = p.gfree.setdefault("dg", {})

    def stream_group(g):
        par = g % 2
        w = 256 if g < 16 else 192
        if par in wbuf_free:
            p.wait("sync", "sT", wbuf_free[par])
        sname = "wPA" if par == 0 else "wPB"
        p.dma("sync", p.wbuf[:, sl(par * 4096, 16 * w)], p.wsc_inp[g][:, 0:16 * w], sem=sname)
        return (sname, p.v(sname))

    def stream_diag(fc):
        par = fc % 2
        if par in dg_free:
            p.wait("sync", "sT", dg_free[par])
        sname = "dgA" if par == 0 else "dgB"
        p.dma("sync", p.dgbuf[:, sl(par * 512, 512)], p.wsc_diag[fc][:], sem=sname)
        return (sname, p.v(sname))

    wsem = {16: stream_group(16)}
    dgsem = {0: stream_diag(0), 1: stream_diag(1)}

    for g in [16] + list(range(16)):
        par = g % 2
        p.wait("tensor", *wsem[g])
        if g < 16:
            for fi in range(2):
                fc = 2 * g + fi
                psv = p.psA[:, sl(fi * 512, 512)]
                if fi in psum_evac:
                    p.wait("tensor", *psum_evac[fi])
                for k in range(NK):
                    base = par * 4096 + k * 256 + fi * 128
                    mmv = p.run("tensor", lambda E, base=base, psv=psv, k=k: E.matmul(
                        psv, p.wbuf[:, sl(base, 128)], p.normed()[:, sl(k * 512, 512)],
                        start=(k == 0), stop=(k == NK - 1)), "sT" if k == NK - 1 else None)
                if fi == 1:
                    wbuf_free[par] = mmv
                    if g + 2 <= 15:
                        wsem[g + 2] = stream_group(g + 2)
                if fc >= 4:
                    p.wait("scalar", "sT", conv_done[fc - 4])
                p.wait("scalar", "sT", mmv)
                ev = p.run("scalar", lambda E, fc=fc, psv=psv: E.activation(
                    out=p.xrr[:, sl((fc % 4) * 515 + 3, 512)], in_=psv,
                    func=AF.Copy, bias=0.0, scale=1.0), "sA")
                evac_done[fc] = ev
                psum_evac[fi] = ("sA", ev)
                if fc == 0:
                    p.wait("gpsimd", "sA", ev)
                    kb.tap("wbuf0", p.wbuf[:, 0:4096])
        else:
            for j in range(3):
                psv = p.psA[0:64, sl(j * 512, 512)]
                coff = j * 64
                for k in range(NK):
                    base = par * 4096 + k * 192 + coff
                    bcv = p.run("tensor", lambda E, base=base, psv=psv, k=k: E.matmul(
                        psv, p.wbuf[:, sl(base, 64)], p.normed()[:, sl(k * 512, 512)],
                        start=(k == 0), stop=(k == NK - 1)),
                        "sT" if (j == 2 and k == NK - 1) else None)
            wbuf_free[par] = bcv
            p.wait("vector", "sT", bcv)
            p.wait("scalar", "sT", bcv)
            for j, dst in [(0, p.bm_b), (1, p.cm_b)]:
                psv = p.psA[0:64, sl(j * 512, 512)]
                v1 = p.run("vector", lambda E, psv=psv: E.tensor_scalar(
                    out=p.lns[0:64, 0:512], in0=psv, scalar1=0.0, scalar2=None,
                    op0=ALU.min), "sV")
                p.wait("scalar", "sV", v1)
                a1 = p.run("scalar", lambda E: E.activation(
                    out=p.lns[0:64, 512:1024], in_=p.lns[0:64, 0:512],
                    func=AF.Exp, bias=0.0, scale=1.0), "sA")
                p.wait("vector", "sA", a1)
                sttv = p.run("vector", lambda E, psv=psv, dst=dst: E.scalar_tensor_tensor(
                    out=dst[:, 0:512], in0=psv, scalar=1.0, in1=p.lns[0:64, 512:1024],
                    op0=ALU.add, op1=ALU.max), "sV")
                psum_evac[j] = ("sV", sttv)
            psv = p.psA[0:64, sl(2 * 512, 512)]
            sgv = p.run("scalar", lambda E, psv=psv: E.activation(
                out=p.lns[0:64, 1024:1536], in_=psv,
                func=AF.Sigmoid, bias=p.negdtb[:], scale=-1.0), "sA")
            p.wait("scalar", "sA", sgv)
            a4 = p.run("scalar", lambda E: E.activation(
                out=p.lns[0:64, 1536:2048], in_=p.lns[0:64, 1024:1536],
                func=AF.Ln, bias=0.0, scale=1.0), "sA")
            p.wait("vector", "sA", a4)
            p.run("vector", lambda E: E.tensor_scalar(
                out=p.la[:, 0:512], in0=p.lns[0:64, 1536:2048],
                scalar1=p.expA[:], scalar2=None, op0=ALU.mult), "sV")
            kb._conv_evac[0] = sgv
            wsem[0] = stream_group(0)
            wsem[1] = stream_group(1)

        if g < 16:
            for fi in range(2):
                fc = 2 * g + fi
                par_d = fc % 2
                p.wait("vector", "sA", evac_done[fc])
                hv = p.run("vector", lambda E, fc=fc: E.tensor_copy(
                    p.xrr[:, sl((fc % 4) * 515, 3)], p.hist[:, sl(fc * 3, 3)]), "sV")
                p.wait("tensor", "sV", hv)
                p.wait("tensor", *dgsem[fc])
                cps = p.psA[:, sl(1024 + par_d * 512, 512)]
                p.wait("tensor", "sA", kb._conv_evac.get(par_d, 0))
                for j in range(4):
                    cv = p.run("tensor", lambda E, j=j, par_d=par_d, fc=fc, cps=cps: E.matmul(
                        cps, p.dgbuf[:, sl(par_d * 512 + j * 128, 128)],
                        p.xrr[:, (fc % 4) * 515 + j:(fc % 4) * 515 + j + 512],
                        start=(j == 0), stop=(j == 3)), "sT" if j == 3 else None)
                conv_done[fc] = cv
                dg_free[par_d] = cv
                if fc + 2 < 32:
                    dgsem[fc + 2] = stream_diag(fc + 2)
                p.wait("vector", "sV", hv)
                p.run("vector", lambda E, fc=fc: E.tensor_copy(
                    p.hist[:, sl(fc * 3, 3)], p.xrr[:, sl((fc % 4) * 515 + 3 + 509, 3)]), "sV")
                if fc >= 4:
                    p.wait("scalar", "sT", vtr_done[fc - 4])
                p.wait("scalar", "sT", cv)
                xe = p.run("scalar", lambda E, fc=fc, cps=cps: E.activation(
                    out=p.xcr[:, sl((fc % 4) * 512, 512)], in_=cps,
                    func=AF.Silu, bias=p.cb_cols[:, fc:fc + 1], scale=1.0), "sA")
                kb._conv_evac[par_d] = xe
                p.wait("tensor", kb._tr_free[0], kb._tr_free[1])
                p.wait("tensor", "sA", xe)
                for c in range(NCH):
                    tv = p.run("tensor", lambda E, fc=fc, c=c: E.transpose(
                        p.tr_ps[:, sl(c * 128, 128)],
                        p.xcr[:, sl((fc % 4) * 512 + c * 128, 128)], p.eye128_b[:]),
                        "sT" if c == NCH - 1 else None)
                vtr_done[fc] = tv
                p.wait("scalar", "sT", tv)
                ve = p.run("scalar", lambda E, fc=fc: E.activation(
                    out=p.vt[:].rearrange("q (c f) -> q c f", c=NCH)[:, :, sl(fc * 128, 128)],
                    in_=p.tr_ps[:, 0:512].rearrange("q (c f) -> q c f", c=NCH),
                    func=AF.Copy, bias=0.0, scale=1.0), "sA")
                kb._tr_free = ("sA", ve)


def emit_inproj_z(kb: "KB", t):
    p = kb
    zfree = p.gfree.setdefault('zw', {})
    evs = {}

    def stream(zc, kg):
        par = (zc * 8 + kg) % 2
        if par in zfree:
            p.wait("sync", "sT", zfree[par])
        sname = "zwA" if par == 0 else "zwB"
        p.dma("sync", p.zwbuf[:, sl(par * 1024, 1024)], p.wsc_zp[zc][:, sl(kg * 1024, 1024)], sem=sname)
        return (sname, p.v(sname))

    sems = {(0, 0): stream(0, 0), (0, 1): stream(0, 1)}
    for zc in range(8):
        for kg in range(8):
            par = (zc * 8 + kg) % 2
            p.wait("tensor", *sems[(zc, kg)])
            nxt = (zc, kg + 2) if kg + 2 < 8 else (zc + 1, kg - 6)
            for tsub in range(NCH):
                psv = p.psA[:, sl(tsub * 512, 512)]
                if kg == 0 and tsub in evs:
                    p.wait("tensor", "sA", evs[tsub])
                for k4 in range(2):
                    k = kg * 2 + k4
                    mv = p.run("tensor", lambda E, k=k, k4=k4, kg=kg, par=par, tsub=tsub, psv=psv: E.matmul(
                        psv, p.normed()[:, sl(k * 512 + tsub * 128, 128)],
                        p.zwbuf[:, sl(par * 1024 + k4 * 512, 512)],
                        start=(kg == 0 and k4 == 0), stop=(kg == 7 and k4 == 1)),
                        "sT" if (tsub == NCH - 1 and k4 == 1) else None)
            zfree[par] = mv
            if nxt[0] < 8:
                sems[nxt] = stream(*nxt)
        p.wait("scalar", "sT", p.v("sT"))
        p.wait("scalar", "sSO", p.v("sSO"))
        for tsub in range(NCH):
            evs[tsub] = p.run("scalar", lambda E, tsub=tsub: E.activation(
                out=p.prep_b[:, sl(tsub * 512, 512)], in_=p.psA[:, sl(tsub * 512, 512)],
                func=AF.Silu, bias=0.0, scale=1.0), "sA")
        p.wait("gpsimd", "sA", p.v("sA"))
        p.dma("gpsimd", p.zsc[:, :, sl(zc * 512, 512)].rearrange("c q f -> q c f"),
              p.prep_b[:, 0:2048].rearrange("q (c f) -> q c f", c=4), sem="sSO")
    kb._zsc_done = p.v("sSO")


def emit_ssd_pre(kb: "KB", t, c):
    p = kb
    pc = c % 2
    if c == 0:
        p.wait("vector", "sSO", p.v("sSO"))
        if t == 0:
            p.wait("scalar", "sV", kb._prep2_v)
            p.wait("scalar", "p2OA", kb._prep2_o)
            p.wait("scalar", "p2OB", kb._prep2_o2)
            p.wait("gpsimd", "sV", kb._prep2_v)
            p.wait("gpsimd", "p2IA", kb._prep2_i[0])
            p.wait("gpsimd", "p2IB", kb._prep2_i[1])
    la_sl = p.la[:, sl(c * S, S)]
    csh_sl = p.csh[:, sl(pc * S, S)]
    cst_sl = p.cst[:, sl(pc * NH, NH)]
    et_sl = p.et[:, sl(pc * NH, NH)]
    rt_sl = p.rt[:, sl(pc * NH, NH)]
    smt_sl = p.smt[:, sl(pc * S, S)]
    kcT = p.bm_b[:, sl(c * S, S)]
    qcT = p.cm_b[:, sl(c * S, S)]

    sv = p.run("vector", lambda E: E.tensor_tensor_scan(
        out=csh_sl, data0=la_sl, data1=la_sl, initial=0.0,
        op0=ALU.add, op1=ALU.bypass), "sV")
    p.wait("gpsimd", "sV", sv)
    p.wait("gpsimd", "sCB", kb._csflat_free[pc])
    kb._flv = p.dma("gpsimd", p.csflat[pc], csh_sl, sem="sFL")
    p.wait("tensor", "sV", kb._psS_free)
    p.wait("tensor", "sV", sv)
    trv = p.run("tensor", lambda E: E.transpose(
        p.psS[:, 0:64], csh_sl, p.eye64_f[:]), "sT")
    p.wait("vector", "sT", trv)
    cstv = p.run("vector", lambda E: E.tensor_copy(cst_sl, p.psS[:, 0:64]), "sV")
    p.wait("scalar", "sV", cstv)
    etv = p.run("scalar", lambda E: E.activation(
        out=et_sl, in_=cst_sl, func=AF.Exp, bias=0.0, scale=1.0), "sA")
    totcol = p.csh[:, pc * S + S - 1:pc * S + S]
    rhv = p.run("scalar", lambda E: E.activation(
        out=p.rh[:], in_=csh_sl, func=AF.Exp, bias=totcol, scale=-1.0), "sA")
    dcv = p.run("scalar", lambda E, pc=pc: E.activation(
        out=p.dcol[:, pc:pc + 1], in_=totcol, func=AF.Exp, bias=0.0, scale=1.0), "sA")
    p.wait("tensor", "sA", rhv)
    p.wait("tensor", kb._tr_free[0], kb._tr_free[1])
    rtv = p.run("tensor", lambda E: E.transpose(
        p.tr_ps[:, 0:64], p.rh[:], p.eye128_b[0:64, 0:64]), "sT")
    p.wait("vector", "sT", rtv)
    rtev = p.run("vector", lambda E: E.tensor_copy(rt_sl, p.tr_ps[:, 0:64]), "sV")
    p.wait("tensor", "sA", dcv)
    p.wait("tensor", "sV", cstv)
    dtv = p.run("tensor", lambda E, pc=pc: E.transpose(
        p.psS[0:64, 64:128], p.dcol[:, pc:pc + 1].broadcast_to([NH, 64]), p.eye64_f[:]), "sT")
    p.wait("vector", "sT", dtv)
    dtev = p.run("vector", lambda E, pc=pc: E.tensor_copy(
        p.dtile[:, sl(pc * NH, NH)], p.psS[0:64, 64:128]), "sV")
    p.wait("tensor", "sV", rtev)
    kctv = p.run("tensor", lambda E: E.transpose(
        p.tr_ps[:, 64:128], kcT, p.eye128_b[0:64, 0:64]), "sT")
    p.wait("vector", "sT", kctv)
    kcev = p.run("vector", lambda E, pc=pc: E.tensor_copy(
        p.kc_sb[:, sl(pc * NST, NST)], p.tr_ps[:, 64:128]), "sV")
    kb._tr_free = ("sV", kcev)
    p.wait("tensor", "sV", dtev)
    scv = p.run("tensor", lambda E: E.matmul(
        p.psS[:, 128:256], kcT, qcT, start=True, stop=True), "sT")
    p.wait("vector", "sT", scv)
    smv = p.run("vector", lambda E: E.tensor_tensor(
        out=smt_sl, in0=p.psS[:, 128:256], in1=p.mask01_b[:], op=ALU.mult), "sV")
    kb._pre[pc] = dict(cstv=cstv, etv=etv, smv=smv, rtev=rtev, dtev=dtev, kcev=kcev, scv=scv)
    kb._psS_free = smv


def _emit_dbuild(kb, t, c, e):
    p = kb
    pc = c % 2
    cst_sl = p.cst[:, sl(pc * NH, NH)]
    smt_sl = p.smt[:, sl(pc * S, S)]
    dslot = p.datt[:, sl((e % 2) * 1024, 1024)]
    if kb._csb_free:
        p.wait("gpsimd", "sV", kb._csb_free)
    p.wait("gpsimd", "sFL", kb._flv)
    cbv = p.dma("gpsimd", p.csb[:],
                p.csflat[pc].rearrange("h s -> (h s)")[sl(e * 1024, 1024)]
                .rearrange("(a f) -> a f", a=1).broadcast_to([128, 1024]), sem="sCB")
    kb._csflat_free[pc] = cbv
    p.wait("vector", "sCB", cbv)
    p.selfsync("vector")
    dsv = p.run("vector", lambda E, e=e: E.tensor_tensor(
        out=p.darg[:, 0:1024].rearrange("z (h s) -> z h s", h=8),
        in0=p.csb[:].rearrange("z (h s) -> z h s", h=8),
        in1=cst_sl[:, sl(e * 8, 8)].rearrange("z (h o) -> z h o", o=1).broadcast_to([128, 8, S]),
        op=ALU.subtract), "sV")
    kb._csb_free = dsv
    if kb._dslot_free.get(e % 2):
        p.wait("scalar", "sT", kb._dslot_free[e % 2])
    p.wait("scalar", "sV", dsv)
    absv = p.run("scalar", lambda E, dslot=dslot: E.activation(
        out=dslot, in_=p.darg[:, 0:1024], func=AF.Abs, bias=0.0, scale=1.0), "sA")
    p.wait("scalar", "sA", absv)
    expv = p.run("scalar", lambda E, dslot=dslot: E.activation(
        out=p.darg[:, 0:1024], in_=dslot, func=AF.Exp, bias=0.0, scale=-1.0), "sA")
    p.wait("vector", "sA", expv)
    p.wait("vector", "sV", kb._pre[pc]["smv"])
    attv = p.run("vector", lambda E, dslot=dslot: E.tensor_tensor(
        out=dslot.rearrange("z (h s) -> z h s", h=8),
        in0=p.darg[:, 0:1024].rearrange("z (h s) -> z h s", h=8),
        in1=smt_sl.rearrange("z (o s) -> z o s", o=1).broadcast_to([128, 8, S]),
        op=ALU.mult), "sV")
    kb._datt_ready[e % 2] = attv


def _emit_eighth(kb, t, c, e):
    p = kb
    pc = c % 2
    qcT = p.cm_b[:, sl(c * S, S)]
    vtc = p.vt[:, sl(c * DI, DI)]
    et_sl = p.et[:, sl(pc * NH, NH)]
    dslot = p.datt[:, sl((e % 2) * 1024, 1024)]

    p.selfsync("vector")
    p.wait("vector", "sV", kb._pre[pc]["rtev"])
    if kb._rv_free:
        p.wait("vector", "sT", kb._rv_free)
    rvv = p.run("vector", lambda E, e=e, pc=pc: E.tensor_tensor(
        out=p.rv[:].rearrange("z (h x) -> z h x", h=8),
        in0=vtc[:, sl(e * 512, 512)].rearrange("z (h x) -> z h x", h=8),
        in1=p.rt[:, sl(pc * NH + e * 8, 8)].rearrange("z (h o) -> z h o", o=1)
        .broadcast_to([128, 8, HD]), op=ALU.mult), "sV")
    vdv = p.run("vector", lambda E, e=e: E.tensor_tensor(
        out=p.vd[:].rearrange("z (h x) -> z h x", h=8),
        in0=vtc[:, sl(e * 512, 512)].rearrange("z (h x) -> z h x", h=8),
        in1=p.drep[:, sl(e * 8, 8)].rearrange("z (h o) -> z h o", o=1)
        .broadcast_to([128, 8, HD]), op=ALU.mult), "sV")

    if kb._yb_free:
        p.wait("tensor", "sV", kb._yb_free)
    iv = p.run("tensor", lambda E, e=e: E.matmul(
        p.psYb[:], qcT, p.h_bf[:, sl(e * 512, 512)], start=True, stop=True), "sT")
    p.wait("tensor", "sV", kb._datt_ready[e % 2])
    p.wait("tensor", "sV", vdv)
    if kb._ya_free:
        p.wait("tensor", "sV", kb._ya_free)
    p.run("tensor", lambda E: E.matmul(
        p.psYa[:], p.eye128_b[:], p.vd[:], start=True, stop=False), None)
    for i in range(8):
        av2 = p.run("tensor", lambda E, i=i, e=e, dslot=dslot: E.matmul(
            p.psYa[:, sl(i * 64, 64)], dslot[:, sl(i * 128, 128)],
            vtc[:, sl(e * 512 + i * 64, 64)],
            start=False, stop=(i == 7)), "sT" if i == 7 else None)
    kb._dslot_free[e % 2] = av2
    kb._rv_free = av2
    p.wait("tensor", "sV", kb._psS_free)
    p.wait("tensor", "sV", rvv)
    p.wait("tensor", "sV", kb._pre[pc]["kcev"])
    stv = p.run("tensor", lambda E, pc=pc: E.matmul(
        p.psS[0:64, 0:512], p.kc_sb[:, sl(pc * NST, NST)], p.rv[:],
        start=True, stop=True), "sT")
    kb._rv_free = stv
    p.wait("vector", "sT", av2)
    p.wait("vector", "sA", kb._pre[pc]["etv"])
    if kb._ysb_free:
        p.wait("vector", "sA", kb._ysb_free)
    c1 = p.run("vector", lambda E, e=e, pc=pc: E.tensor_tensor(
        out=p.y_tmp[:].rearrange("z (h x) -> z h x", h=8),
        in0=p.psYb.rearrange("z (h x) -> z h x", h=8),
        in1=et_sl[:, sl(e * 8, 8)].rearrange("z (h o) -> z h o", o=1)
        .broadcast_to([128, 8, HD]), op=ALU.mult), "sV")
    kb._yb_free = c1
    p.wait("vector", "sV", c1)
    c2 = p.run("vector", lambda E, e=e: E.tensor_tensor(
        out=p.y_sb[:, sl(e * 512, 512)], in0=p.y_tmp[:], in1=p.psYa, op=ALU.add), "sV")
    kb._ya_free = c2
    p.wait("vector", "sT", stv)
    p.wait("vector", "sV", kb._u1)
    u2 = p.run("vector", lambda E, e=e: E.tensor_tensor(
        out=p.h_sb[:, sl(e * 512, 512)], in0=p.h_sb[:, sl(e * 512, 512)],
        in1=p.psS[0:64, 0:512], op=ALU.add), "sV")
    kb._psS_free = u2
    p.wait("scalar", "sV", u2)
    p.wait("scalar", "sT", iv)
    p.run("scalar", lambda E, e=e: E.activation(
        out=p.h_bf[:, sl(e * 512, 512)], in_=p.h_sb[:, sl(e * 512, 512)],
        func=AF.Copy, bias=0.0, scale=1.0), "sA")
    kb._last_c2 = c2


def emit_ssd_main(kb: "KB", t, c):
    p = kb
    pc = c % 2
    p.selfsync("vector")
    p.wait("vector", "sA", p.v("sA"))
    p.wait("vector", "sV", kb._pre[pc]["dtev"])
    kb._u1 = p.run("vector", lambda E, pc=pc: E.tensor_tensor(
        out=p.h_sb[:].rearrange("n (h x) -> n h x", h=NH),
        in0=p.h_sb[:].rearrange("n (h x) -> n h x", h=NH),
        in1=p.dtile[:, sl(pc * NH, NH)].rearrange("n (h o) -> n h o", o=1)
        .broadcast_to([NST, NH, HD]), op=ALU.mult), "sV")
    _emit_dbuild(kb, t, c, 0)
    _emit_dbuild(kb, t, c, 1)
    for e in range(8):
        _emit_eighth(kb, t, c, e)
        if e + 2 < 8:
            _emit_dbuild(kb, t, c, e + 2)
    p.wait("scalar", "sV", kb._last_c2)
    yslot = p.ybf[:, sl((c % 2) * DI, DI)]
    if kb._ybf_free.get(c % 2):
        p.wait("scalar", "sV", kb._ybf_free[c % 2])
    sqv = p.run("scalar", lambda E, yslot=yslot, c=c: E.activation(
        out=yslot, in_=p.y_sb[:], func=AF.Square, bias=0.0, scale=1.0,
        accum_out=p.ssc[:, c % 2:c % 2 + 1]), "sA")
    p.wait("scalar", "sA", sqv)
    cpv = p.run("scalar", lambda E, yslot=yslot: E.activation(
        out=yslot, in_=p.y_sb[:], func=AF.Copy, bias=0.0, scale=1.0), "sA")
    kb._ysb_free = cpv
    kb._ybf_ready[c % 2] = cpv


def emit_ssd_gate(kb: "KB", t, c):
    p = kb
    yslot = p.ybf[:, sl((c % 2) * DI, DI)]
    p.wait("gpsimd", "sV", kb._zbuf_free)
    p.wait("gpsimd", "sSO", kb._zsc_done)
    if c == 0:
        p.wait("gpsimd", "sV", kb._prep2_v)
    zv = p.dma("gpsimd", p.zbuf[:, 0:DI], p.zsc[c], sem="sZB")
    p.selfsync("vector")
    r1 = p.run("vector", lambda E, c=c: E.tensor_scalar(
        out=p.ms_col[:], in0=p.ssc[:, c % 2:c % 2 + 1], scalar1=1.0 / DI, scalar2=1e-6,
        op0=ALU.mult, op1=ALU.add), "sV")
    p.wait("vector", "sV", r1)
    r2 = p.run("vector", lambda E: E.reciprocal(p.ss_col[:], p.ms_col[:]), "sV")
    p.wait("scalar", "sV", r2)
    rsv = p.run("scalar", lambda E: E.activation(
        out=p.rstd2[:], in_=p.ss_col[:], func=AF.Sqrt, bias=0.0, scale=1.0), "sA")
    p.wait("vector", "sA", rsv)
    p.wait("vector", "sA", kb._ybf_ready[c % 2])
    p.wait("vector", "sZB", zv)
    if kb._ygt_free:
        p.wait("vector", "sT", kb._ygt_free)
    gvh = {}
    for bg in range(4):
        gvh[bg] = p.run("vector", lambda E, yslot=yslot, bg=bg: E.scalar_tensor_tensor(
            out=p.ygt[:, sl(bg * 1024, 1024)], in0=yslot[:, sl(bg * 1024, 1024)],
            scalar=p.rstd2[:], in1=p.zbuf[:, sl(bg * 1024, 1024)],
            op0=ALU.mult, op1=ALU.mult), "sV")
    gv = gvh[3]
    kb._zbuf_free = gv
    kb._ybf_free[c % 2] = gv
    for bg in range(4):
        p.wait("tensor", "sV", gvh[bg])
        p.wait("tensor", kb._tr_free[0], kb._tr_free[1])
        for f8 in range(8):
            fc = bg * 8 + f8
            tv = p.run("tensor", lambda E, fc=fc, f8=f8: E.transpose(
                p.tr_ps[:, sl(f8 * 128, 128)], p.ygt[:, sl(fc * 128, 128)], p.eye128_b[:]),
                "sT" if f8 == 7 else None)
        p.wait("scalar", "sT", tv)
        dst = (p.slotA if bg < 2 else p.slotB)
        ev = p.run("scalar", lambda E, dst=dst, bg=bg, c=c: E.activation(
            out=dst.rearrange("z (f x) -> z f x", f=16)[:, sl((bg % 2) * 8, 8), sl(c * 128, 128)],
            in_=p.tr_ps[:].rearrange("z (f x) -> z f x", f=8),
            func=AF.Copy, bias=0.0, scale=1.0), "sA")
        kb._tr_free = ("sA", ev)
    kb._ygt_free = p.v("sT")


def emit_outproj(kb: "KB", t):
    p = kb
    p.wait("gpsimd", "sT", p.v("sT"))
    p.wait("gpsimd", "sV", p.v("sV"))
    uld = p.dma("gpsimd", p.xslot[:].rearrange("z (a d_) -> z a d_", a=NCH),
                p.u[sl(t * TT, TT), :].rearrange("(a z) d_ -> z a d_", z=128), sem="sU")
    ofree = p.gfree.setdefault('ow', {})
    evs = {}

    def stream(cb, kg):
        par = (cb * 16 + kg) % 2
        if par in ofree:
            p.wait("sync", "sT", ofree[par])
        sname = "owA" if par == 0 else "owB"
        p.dma("sync", p.owbuf[:, sl(par * 1024, 1024)], p.wsc_outp[cb][:, sl(kg * 1024, 1024)], sem=sname)
        return (sname, p.v(sname))

    sems = {(0, 0): stream(0, 0), (0, 1): stream(0, 1)}
    for cb in range(4):
        for kg in range(16):
            par = (cb * 16 + kg) % 2
            p.wait("tensor", *sems[(cb, kg)])
            nxt = (cb, kg + 2) if kg + 2 < 16 else (cb + 1, kg + 2 - 16)
            for tsub in range(NCH):
                psv = p.psA[:, sl(tsub * 512, 512)]
                if kg == 0 and tsub in evs:
                    p.wait("tensor", "sV", evs[tsub])
                for k4 in range(2):
                    k = kg * 2 + k4
                    mv = p.run("tensor", lambda E, k=k, k4=k4, kg=kg, par=par, tsub=tsub, psv=psv: E.matmul(
                        psv, p.ygf_slice(k, tsub), p.owbuf[:, sl(par * 1024 + k4 * 512, 512)],
                        start=(kg == 0 and k4 == 0), stop=(kg == 15 and k4 == 1)),
                        "sT" if (tsub == NCH - 1 and k4 == 1) else None)
            ofree[par] = mv
            if nxt[0] < 4:
                sems[nxt] = stream(*nxt)
        p.wait("vector", "sT", p.v("sT"))
        p.wait("vector", "sU", uld)
        for tsub in range(NCH):
            dst = p.xslot[:, sl(tsub * D + cb * 512, 512)]
            evs[tsub] = p.run("vector", lambda E, dst=dst, tsub=tsub: E.tensor_tensor(
                out=dst, in0=p.psA[:, sl(tsub * 512, 512)], in1=dst, op=ALU.add), "sV")


def emit_ffn(kb: "KB", t):
    p = kb
    n2f = p.slotA
    for tsub in range(NCH):
        p.selfsync("vector")
        p.selfsync("scalar")
        psv = p.psB[:, 0:128]
        for k in range(NK):
            p.run("tensor", lambda E, k=k, tsub=tsub, psv=psv: E.matmul(
                psv[:, 0:118], p.n2f_slice(k, tsub), p.wsmall_sb[:, sl(k * 118, 118)],
                start=(k == 0), stop=False), None)
        smv = p.run("tensor", lambda E, psv=psv: E.matmul(
            psv[:, 0:118], p.ones_row[:], p.brow_sb[:], start=False, stop=True), "sT")
        p.wait("vector", "sT", smv)
        lg = psv[:, 16:22]
        v1 = p.run("vector", lambda E, lg=lg: E.reduce_max(
            out=p.m1c[:], in_=lg, axis=AX.X), "sV")
        p.wait("vector", "sV", v1)
        v2 = p.run("vector", lambda E: E.tensor_scalar(
            out=p.ms_col[:], in0=p.m1c[:], scalar1=-1.0, scalar2=None, op0=ALU.mult), "sV")
        p.wait("scalar", "sV", v2)
        a1 = p.run("scalar", lambda E, lg=lg: E.activation(
            out=p.gates[:], in_=lg, func=AF.Exp, bias=p.ms_col[:], scale=1.0,
            accum_out=p.sumg[:]), "sA")
        p.wait("vector", "sA", a1)
        v3 = p.run("vector", lambda E: E.reciprocal(p.rsg[:], p.sumg[:]), "sV")
        p.wait("vector", "sV", v3)
        v4 = p.run("vector", lambda E: E.tensor_scalar(
            out=p.gates[:], in0=p.gates[:], scalar1=p.rsg[:], scalar2=None, op0=ALU.mult), "sV")
        p.wait("vector", "sV", v4)
        v5 = p.run("vector", lambda E: E.reduce_max(
            out=p.m1c[:], in_=p.gates[:], axis=AX.X), "sV")
        p.wait("vector", "sV", v5)
        v6 = p.run("vector", lambda E: E.tensor_scalar(
            out=p.gm[:], in0=p.gates[:], scalar1=p.m1c[:], scalar2=-1e30,
            op0=ALU.is_ge, op1=ALU.mult), "sV")
        p.wait("vector", "sV", v6)
        v7 = p.run("vector", lambda E: E.tensor_tensor(
            out=p.gm[:], in0=p.gates[:], in1=p.gm[:], op=ALU.add), "sV")
        p.wait("vector", "sV", v7)
        v8 = p.run("vector", lambda E: E.reduce_max(
            out=p.m2c[:], in_=p.gm[:], axis=AX.X), "sV")
        p.wait("vector", "sV", v8)
        v9 = p.run("vector", lambda E: E.tensor_scalar(
            out=p.wpe[:], in0=p.gates[:], scalar1=p.m2c[:], scalar2=None, op0=ALU.is_ge), "sV")
        p.wait("vector", "sV", v9)
        v10 = p.run("vector", lambda E: E.tensor_tensor(
            out=p.wpe[:], in0=p.gates[:], in1=p.wpe[:], op=ALU.mult), "sV")
        p.wait("scalar", "sT", smv)
        a2 = p.run("scalar", lambda E, psv=psv: E.activation(
            out=p.ffn_small[:], in_=psv[:, 0:118], func=AF.Copy, bias=0.0, scale=1.0), "sA")
        p.wait("vector", "sA", a2)
        p.wait("vector", "sV", v10)
        s1 = p.run("vector", lambda E: E.tensor_tensor(
            out=p.stackt[:, 16:112].rearrange("z (e r) -> z e r", e=6),
            in0=p.ffn_small[:, 22:118].rearrange("z (e r) -> z e r", e=6),
            in1=p.wpe[:].rearrange("z (e o) -> z e o", o=1).broadcast_to([128, 6, 16]),
            op=ALU.mult), "sV")
        s2 = p.run("vector", lambda E: E.tensor_copy(
            p.stackt[:, 0:16], p.ffn_small[:, 0:16]), "sV")
        p.wait("tensor", "sV", s2)
        tvv = p.run("tensor", lambda E: E.transpose(
            p.tr_ps[0:113, 0:128], p.stackt[:], p.eye128_b[:]), "sT")
        p.wait("vector", "sT", tvv)
        sev = p.run("vector", lambda E, tsub=tsub: E.tensor_copy(
            p.stackT[:, sl(tsub * 128, 128)], p.tr_ps[0:113, 0:128]), "sV")
        p.wait("tensor", "sV", sev)

    bfree = p.gfree.setdefault('bw', {})
    evs = {}
    stv = {}

    def stream(cb, kg):
        par = (cb * 8 + kg) % 2
        if par in bfree:
            p.wait("sync", "sT", bfree[par])
        sname = "bwA" if par == 0 else "bwB"
        p.dma("sync", p.bwbuf[:, sl(par * 1024, 1024)], p.wsc_basep[cb][:, sl(kg * 1024, 1024)], sem=sname)
        return (sname, p.v(sname))

    sems = {(0, 0): stream(0, 0), (0, 1): stream(0, 1)}
    for cb in range(4):
        for kg in range(8):
            par = (cb * 8 + kg) % 2
            p.wait("tensor", *sems[(cb, kg)])
            nxt = (cb, kg + 2) if kg + 2 < 8 else (cb + 1, kg + 2 - 8)
            for tsub in range(NCH):
                psv = p.psA[:, sl(tsub * 512, 512)]
                if kg == 0 and tsub in evs:
                    p.wait("tensor", "sV", evs[tsub])
                for k4 in range(2):
                    k = kg * 2 + k4
                    mv = p.run("tensor", lambda E, k=k, k4=k4, kg=kg, par=par, tsub=tsub, psv=psv: E.matmul(
                        psv, p.n2f_slice(k, tsub), p.bwbuf[:, sl(par * 1024 + k4 * 512, 512)],
                        start=(kg == 0 and k4 == 0), stop=False),
                        "sT" if (tsub == NCH - 1 and k4 == 1) else None)
            bfree[par] = mv
            if nxt[0] < 4:
                sems[nxt] = stream(*nxt)
        for tsub in range(NCH):
            psv = p.psA[:, sl(tsub * 512, 512)]
            dv = p.run("tensor", lambda E, tsub=tsub, psv=psv, cb=cb: E.matmul(
                psv, p.stackT[:, sl(tsub * 128, 128)], p.bdown_sb[:, sl(cb * 512, 512)],
                start=False, stop=True), "sT" if tsub == NCH - 1 else None)
        p.wait("vector", "sT", dv)
        if cb > 0:
            p.wait("vector", "sSO", stv[cb - 1])
        for tsub in range(NCH):
            evs[tsub] = p.run("vector", lambda E, tsub=tsub, cb=cb: E.tensor_tensor(
                out=p.x2_sb[:, sl(tsub * 512, 512)], in0=p.psA[:, sl(tsub * 512, 512)],
                in1=p.xslot[:, sl(tsub * D + cb * 512, 512)], op=ALU.add), "sV")
        p.wait("gpsimd", "sV", max(evs.values()))
        for tsub in range(NCH):
            p.dma("gpsimd",
                  p.out[sl(t * TT + tsub * 128, 128), sl(cb * 512, 512)],
                  p.x2_sb[:, sl(tsub * 512, 512)], sem="sSO")
        stv[cb] = p.v("sSO")


def build_full(ntiles=NTILE, debug=()):
    kb = KB(debug=debug)
    p = kb
    emit_prep(kb)
    for t in range(ntiles):
        p.wait("gpsimd", "sV", p.v("sV"))
        p.wait("gpsimd", "sA", p.v("sA"))
        p.wait("gpsimd", "sT", p.v("sT"))
        uld = p.dma("gpsimd", p.xslot[:].rearrange("z (a d_) -> z a d_", a=NCH),
                    p.u[sl(t * TT, TT), :].rearrange("(a z) d_ -> z a d_", z=128), sem="sU")
        p.wait("vector", "sU", uld)
        p.wait("scalar", "sU", uld)
        emit_ln(kb, t, kb.slotA, "ln1")
        p.barrier()
        kb.tap("normed_t%d" % t, kb.slotA[:])
        emit_inproj_f(kb, t)
        p.barrier()
        kb.tap("vt_t%d" % t, kb.vt[:])
        kb.tap("bm_t%d" % t, kb.bm_b[:])
        kb.tap("cm_t%d" % t, kb.cm_b[:])
        kb.tap("la_t%d" % t, kb.la[:])
        if t == 0:
            emit_prep_outbase(kb)
        emit_ssd_pre(kb, t, 0)
        emit_ssd_pre(kb, t, 1)
        emit_ssd_main(kb, t, 0)
        emit_inproj_z(kb, t)
        emit_ssd_pre(kb, t, 2)
        emit_ssd_main(kb, t, 1)
        emit_ssd_gate(kb, t, 0)
        kb.tap("ysb_t%d_c%d" % (t, 0), kb.ybf[:, 0:DI])
        emit_ssd_pre(kb, t, 3)
        emit_ssd_main(kb, t, 2)
        emit_ssd_gate(kb, t, 1)
        emit_ssd_main(kb, t, 3)
        emit_ssd_gate(kb, t, 2)
        emit_ssd_gate(kb, t, 3)
        p.barrier()
        emit_outproj(kb, t)
        p.barrier()
        kb.tap("x1_t%d" % t, kb.xslot[:])
        emit_ln(kb, t, kb.slotA, "ln2")
        p.barrier()
        emit_ffn(kb, t)
        p.barrier()
    return kb


def _make_in_map(inputs, core):
    f = np.ascontiguousarray
    u = f(np.asarray(inputs["L0_attn_u"][core], np.float32))
    w_small = np.concatenate([
        np.asarray(inputs["L0_ffn_A_s"], np.float32),
        np.asarray(inputs["L0_ffn_W_router"], np.float32),
        np.asarray(inputs["L0_ffn_A_r"], np.float32).transpose(1, 0, 2).reshape(2048, 96),
    ], axis=1)
    b_down = np.concatenate([
        np.asarray(inputs["L0_ffn_B_s"], np.float32),
        np.asarray(inputs["L0_ffn_B_r"], np.float32).reshape(96, 2048),
    ], axis=0)
    brow = np.zeros((1, 118), np.float32)
    brow[0, 16:22] = np.asarray(inputs["L0_ffn_b_router"], np.float32)
    mask01 = np.triu(np.ones((128, 128), np.float32))
    return dict(
        u=u,
        w_in=f(np.asarray(inputs["L0_attn_in_proj_w"], np.float32)),
        convw=f(np.asarray(inputs["L0_attn_conv_weight"], np.float32)
                .reshape(4, 32, 128).transpose(2, 1, 0).reshape(128, 128)),
        convb=f(np.asarray(inputs["L0_attn_conv_bias"], np.float32).reshape(32, 128).T),
        A_log=np.asarray(inputs["L0_attn_A_log"], np.float32).reshape(-1, 1),
        dt_bias=np.asarray(inputs["L0_attn_dt_bias"], np.float32).reshape(-1, 1),
        D_param=np.asarray(inputs["L0_attn_D_param"], np.float32).reshape(1, -1),
        norm_scale=f(np.asarray(inputs["L0_attn_norm_scale"], np.float32).reshape(32, 128).T),
        w_out=f(np.asarray(inputs["L0_attn_out_proj_w"], np.float32)),
        w_base=f(np.asarray(inputs["L0_ffn_W_base"], np.float32)),
        b_base=np.asarray(inputs["L0_ffn_b_base"], np.float32).reshape(1, -1),
        w_small=f(w_small),
        b_down=f(b_down),
        brow=brow,
        eye128=np.eye(128, dtype=np.float32),
        mask01=mask01,
    )


def kernel(**inputs):
    from concourse.bass_utils import run_bass_kernel_spmd
    kb = build_full(ntiles=NTILE)
    nc = kb.finalize()
    in_maps = [_make_in_map(inputs, c) for c in range(8)]
    res = run_bass_kernel_spmd(nc, in_maps, core_ids=list(range(8)))
    out = np.stack([np.asarray(r["out"], np.float32) for r in res.results], axis=0)
    return out
